# version 01
# baseline (speedup 1.0000x reference)
import numpy as np
import ml_dtypes
import contextlib
import concourse.bass as bass
import concourse.mybir as mybir
from concourse.bass_utils import run_bass_kernel_spmd

F32 = mybir.dt.float32
BF16 = mybir.dt.bfloat16
I32 = mybir.dt.int32
U8 = mybir.dt.uint8
AF = mybir.ActivationFunctionType
ALU = mybir.AluOpType
AX = mybir.AxisListType
AP = bass.AP
NB = ml_dtypes.bfloat16

RC = 12582912.0
D = 1024
DV = 2048
H = 4
T = 256
C = 128
SERIAL = True


class EM:
    ENGS = ("t", "v", "s", "p", "d")
    SEMK = ("t", "v", "s", "p", "d", "pd")

    def __init__(self, nc, stack):
        self.nc = nc
        self.q = {e: [] for e in self.ENGS}
        self.cnt = {e: 0 for e in self.SEMK}
        self.sems = {e: stack.enter_context(nc.semaphore("sem_" + e)) for e in self.SEMK}
        self.waited = {e: {e2: 0 for e2 in self.SEMK} for e in self.ENGS}

    def op(self, eng, fn, waits=(), inc=0):
        qk = "p" if eng == "pd" else eng
        wl = []
        if eng in ("v", "s", "p", "d", "pd") and self.cnt[eng] > 0:
            wl.append((self.sems[eng], self.cnt[eng]))
        for w in waits:
            if w is None:
                continue
            se, sv = w
            if se == eng:
                continue
            if self.waited[qk][se] < sv:
                self.waited[qk][se] = sv
                wl.append((self.sems[se], sv))
        sem = self.sems[eng]

        def run(e, wl=wl, fn=fn, inc=inc, sem=sem):
            for (s, v) in wl:
                e.wait_ge(s, v)
            ins = fn(e)
            if inc:
                ins.then_inc(sem, inc)
            return ins
        self.q[qk].append(run)
        if inc:
            self.cnt[eng] += inc
        return (eng, self.cnt[eng])

    def cop(self, eng, fn, waits=()):
        return self.op(eng, fn, waits=waits, inc=1)

    def dma(self, eng, fn, waits=()):
        return self.op(eng, fn, waits=waits, inc=16)

    def tok(self, eng):
        return (eng, self.cnt[eng])

    def barrier(self, engs=None):
        engs = engs or self.ENGS
        toks = [(e, self.cnt[e]) for e in (self.SEMK if engs is self.ENGS or engs == self.ENGS else engs)]
        for e in engs:
            for (se, sv) in toks:
                if se != e and self.waited[e][se] < sv:
                    self.waited[e][se] = sv
                    sem = self.sems[se]
                    self.q[e].append(lambda en, s=sem, v=sv: en.wait_ge(s, v))

    def mb(self):
        if SERIAL:
            self.barrier()

    def flush(self):
        with self.nc.Block() as blk:
            for ename, attr in [("t", "tensor"), ("v", "vector"), ("s", "scalar"),
                                ("p", "gpsimd"), ("d", "sync")]:
                ops = self.q[ename]
                if not ops:
                    continue
                getattr(blk, attr)(lambda e, ops=tuple(ops): [f(e) for f in ops])
        self.q = {e: [] for e in self.ENGS}


def build_nc(nchunks=16, dbg=False, stage=3, p1cut=99, lncut=99, fcut=99, g2cut=99):
    ntok = nchunks * T
    nic = ntok // C
    icpc = T // C
    import os as _os
    nc = bass.Bass("TRN2", target_bir_lowering=False,
                   detect_race_conditions=_os.environ.get("KRACE", "1") == "1")
    din = {}

    def inp(name, shape, dt=F32):
        din[name] = nc.dram_tensor(name, shape, dt, kind="ExternalInput").ap()
        return din[name]

    xT = inp("xT", [D, ntok])
    for nm, shp in [("wqT", [D, D]), ("wkT", [D, D]), ("wvT", [D, DV]),
                    ("wgT", [D, DV]), ("wgk1T", [D, 16]), ("wgk2T", [16, D]),
                    ("woT", [DV, D]), ("nq_a", [128, 8]), ("nk_a", [128, 8]),
                    ("nv_a", [128, 8]), ("ng_a", [128, 8]), ("ngk1_a", [128, 8]),
                    ("ngk2_a", [16, 1]), ("bgk2_a", [128, 8]), ("no_a", [128, 16]),
                    ("Win", [D, 64]), ("WoutT", [64, D]), ("repw1", [64, 8, 128]),
                    ("w1_a", [128, 8]), ("b1_a", [128, 8]), ("binmi_a", [128, 8]),
                    ("sumw2", [128, 8, 64]), ("b2_a", [64, 1]),
                    ("identf", [128, 128]), ("ones_cf", [128, 1]), ("ones_rf", [1, 128]),
                    ("smask", [128, 1])]:
        inp(nm, shp)
    inp("triu", [128, 128])
    inp("ident", [128, 128], BF16)
    inp("ones_c", [128, 1], BF16)
    inp("ones_r2", [1, 256], BF16)
    inp("sel", [32, 32, 128], BF16)
    inp("bout_r", [1, 8, 128], BF16)

    yT = nc.dram_tensor("yT", [D, ntok], F32, kind="ExternalOutput").ap()
    dbg_out = {}
    if dbg:
        for nm, shp, dt in [("d_rep", [128, 8 * 128], BF16), ("d_pa", [128, 1024], F32),
                            ("d_ub", [128, 8 * T], BF16), ("d_fs", [128, T], BF16),
                            ("d_fh", [128, 8 * T], BF16), ("d_fg", [128, T], BF16),
                            ("d_sc", [128, 2 * 32], F32), ("d_mbc", [128, 2 * T], F32),
                            ("d_xp", [128, 8 * T], F32), ("d_q", [128, 8 * T], BF16),
                            ("d_k", [128, 8 * T], BF16), ("d_gl", [128, 8 * T], BF16),
                            ("d_og", [128, 16 * T], BF16), ("d_of", [128, 16 * T], F32),
                            ("d_s32", [128, 8 * 512], F32), ("d_v", [128, 2 * 2048], BF16)]:
            dbg_out[nm] = nc.dram_tensor(nm, shp, dt, kind="ExternalOutput").ap()

    og_st = nc.dram_tensor("og_st", [nchunks, 128, 16 * T], BF16).ap()
    sg_st = nc.dram_tensor("sg_st", [nchunks, 128, 16 * T], BF16).ap()
    qt_st = nc.dram_tensor("qt_st", [nic, 128, 8 * C], BF16).ap()
    xp_st = nc.dram_tensor("xp_st", [nchunks, 128, 8 * T], F32).ap()
    wvq_st = nc.dram_tensor("wvq_st", [128, 8, DV], BF16).ap()
    wgq_st = nc.dram_tensor("wgq_st", [128, 8, DV], BF16).ap()
    woq_st = nc.dram_tensor("woq_st", [128, 16, D], BF16).ap()
    cc_in = nc.dram_tensor("cc_in", [1024, 512], F32)
    cc_out = nc.dram_tensor("cc_out", [2048, 512], F32)

    perm = contextlib.ExitStack()
    em = EM(nc, perm)
    sbp = lambda n, s, d=F32: perm.enter_context(nc.sbuf_tensor(n, s, d))

    WGK1 = sbp("WGK1", [128, 8, 16], BF16)
    WGK2F = sbp("WGK2F", [128, D], BF16)
    WGK2 = WGK2F[0:16, :]
    WIN = sbp("WIN", [128, 8, 64], BF16)
    WOUTF = sbp("WOUTF", [128, D], BF16)
    WOUT = WOUTF[0:64, :]
    REPF = sbp("REPF", [128, 8, 128], BF16)
    REP = REPF[0:64, :, :]
    SUM2 = sbp("SUM2", [128, 8, 64], BF16)
    B1E = sbp("B1E", [128, 8])
    NQ = sbp("NQ", [128, 8]); NK = sbp("NK", [128, 8]); NV = sbp("NV", [128, 8])
    NG = sbp("NG", [128, 8]); NGK1 = sbp("NGK1", [128, 8]); NGK2F = sbp("NGK2F", [128, 1]); NGK2 = NGK2F[0:16, :]
    BGK2 = sbp("BGK2", [128, 8]); NO = sbp("NO", [128, 16])
    B2F = sbp("B2F", [128, 1]); B2 = B2F[0:64, :]
    BOUTRF = sbp("BOUTRF", [128, 8, 128], BF16); BOUTR = BOUTRF[0:1, :, :]
    TRIU = sbp("TRIU", [128, 128])
    IDENT = sbp("IDENT", [128, 128], BF16)
    IDENTF = sbp("IDENTF", [128, 128], F32)
    ONESC = sbp("ONESC", [128, 1], BF16)
    ONESR2F = sbp("ONESR2F", [128, 256], BF16); ONESR2 = ONESR2F[0:1, :]
    ONESCF = sbp("ONESCF", [128, 1], F32)
    ONESRFF = sbp("ONESRFF", [128, 128], F32); ONESRF = ONESRFF[0:1, :]
    SELTF = sbp("SELTF", [128, 32, 128], BF16); SELT = SELTF[0:32, :, :]
    SMASK = sbp("SMASK", [128, 1])
    ZEROT = sbp("ZEROT", [128, 128])
    CEPS5 = sbp("CEPS5", [128, 1]); CEPS8 = sbp("CEPS8", [128, 1])
    MW = sbp("MW", [128, 8]); WSC = sbp("WSC", [128, 8])
    S32 = sbp("S32", [128, 8, 512])
    LTAB = sbp("LTAB", [128, 8, 32])
    COLS = sbp("COLS", [128, 48])
    SC = sbp("SC", [128, 2, 32])
    SCRF = sbp("SCRF", [128, 2, 128], BF16); SCR = SCRF[0:32, :, :]
    SROWF = sbp("SROWF", [128, 512]); SROW = SROWF[0:1, :]
    MBC = sbp("MBC", [128, 2, T])
    FSF = sbp("FSF", [128, T], BF16); FS = FSF[0:64, :]
    FH = sbp("FH", [128, 8, T], BF16)
    FGF = sbp("FGF", [128, T], BF16); FG = FGF[0:64, :]
    I1F = sbp("I1F", [128, T]); I1 = I1F[0:16, :]
    I1BF = sbp("I1BF", [128, T], BF16); I1B = I1BF[0:16, :]
    SQ16F = sbp("SQ16F", [128, T], BF16); SQ16 = SQ16F[0:16, :]
    TMRB = sbp("TMRB", [128, 2, 128], BF16)
    ELT = sbp("ELT", [128, 8])

    PA = perm.enter_context(nc.psum_tensor("PA", [128, 1024], F32))
    PB = perm.enter_context(nc.psum_tensor("PB", [128, 1024], F32))
    PO1 = perm.enter_context(nc.psum_tensor("PO1", [128, 512], F32))
    PO2 = perm.enter_context(nc.psum_tensor("PO2", [128, 512], F32))
    PBC = perm.enter_context(nc.psum_tensor("PBC", [128, 512], F32))
    PTR = perm.enter_context(nc.psum_tensor("PTR", [128, 256], BF16))

    ph1 = contextlib.ExitStack()
    sb1 = lambda n, s, d=F32: ph1.enter_context(nc.sbuf_tensor(n, s, d))
    WQ = sb1("WQ", [128, 8, D], BF16)
    WK = sb1("WK", [128, 8, D], BF16)
    WST = sb1("WST", [128, 2, 8, 256], BF16)

    stup = contextlib.ExitStack()
    sbs = lambda n, s, d=F32: stup.enter_context(nc.sbuf_tensor(n, s, d))
    STG = sbs("STG", [128, 2, 4096])
    QSC = sbs("QSC", [128, 4096])
    QSB = sbs("QSB", [128, 4096], BF16)
    FSTG = sbs("FSTG", [128, 8, 64])
    WOSF = sbs("WOSF", [128, D]); WOS = WOSF[0:64, :]
    RPSF = sbs("RPSF", [128, 8, 128]); RPS = RPSF[0:64, :, :]
    SMS = sbs("SMS", [128, 8, 64])

    for name, dst in [("nq_a", NQ), ("nk_a", NK), ("nv_a", NV), ("ng_a", NG),
                      ("ngk1_a", NGK1), ("ngk2_a", NGK2), ("bgk2_a", BGK2),
                      ("no_a", NO), ("b2_a", B2), ("bout_r", BOUTR),
                      ("triu", TRIU), ("ident", IDENT), ("identf", IDENTF),
                      ("ones_c", ONESC), ("ones_r2", ONESR2), ("ones_cf", ONESCF),
                      ("ones_rf", ONESRF), ("sel", SELT), ("smask", SMASK)]:
        em.dma("d", lambda e, s=din[name], d=dst: e.dma_start(out=d[:], in_=s))
    em.dma("d", lambda e: e.dma_start(out=FSTG[:], in_=din["Win"].rearrange("(k p) m -> p k m", p=128)))
    em.dma("d", lambda e: e.dma_start(out=WOS[:], in_=din["WoutT"]))
    em.dma("d", lambda e: e.dma_start(out=RPS[:], in_=din["repw1"]))
    em.dma("d", lambda e: e.dma_start(out=SMS[:], in_=din["sumw2"]))
    em.dma("d", lambda e: e.dma_start(out=COLS[:, 0:8], in_=din["w1_a"]))
    em.dma("d", lambda e: e.dma_start(out=COLS[:, 8:16], in_=din["b1_a"]))
    em.dma("d", lambda e: e.dma_start(out=COLS[:, 16:24], in_=din["binmi_a"]))
    tld = em.tok("d")
    em.cop("v", lambda e: e.memset(ZEROT[:], 0.0))
    em.cop("v", lambda e: e.memset(CEPS5[:], 1e-5))
    em.cop("v", lambda e: e.memset(CEPS8[:], 1e-8))
    em.cop("v", lambda e: e.memset(S32[:].rearrange("p a b -> p (a b)"), 0.0))
    em.cop("v", lambda e: e.memset(LTAB[:].rearrange("p a b -> p (a b)"), 0.0))
    em.cop("v", lambda e: e.memset(SC[:].rearrange("p a b -> p (a b)"), 0.0))
    em.cop("v", lambda e: e.memset(COLS[:, 24:48], 0.0), waits=[tld])
    em.cop("v", lambda e: e.memset(SROW[:], 0.0))
    em.cop("v", lambda e: e.tensor_copy(WIN[:].rearrange("p a b -> p (a b)"),
                                        FSTG[:].rearrange("p a b -> p (a b)")), waits=[tld])
    em.cop("v", lambda e: e.tensor_copy(WOUT[:], WOS[:]))
    em.cop("v", lambda e: e.tensor_copy(REP[:].rearrange("p a b -> p (a b)"),
                                        RPS[:].rearrange("p a b -> p (a b)")))
    em.cop("v", lambda e: e.tensor_copy(SUM2[:].rearrange("p a b -> p (a b)"),
                                        SMS[:].rearrange("p a b -> p (a b)")))
    em.cop("v", lambda e: e.tensor_mul(B1E[:], COLS[:, 0:8], COLS[:, 16:24]), waits=[tld])
    em.cop("v", lambda e: e.tensor_add(B1E[:], B1E[:], COLS[:, 8:16]))
    em.barrier()

    skipwq = _os.environ.get("SKIPWQ", "0") == "1"
    if skipwq:
        wjobs = []
    else:
        pass
    wjobs = [] if skipwq else [
        (din["wqT"], ("sbuf", WQ), 8, D, 0),
        (din["wkT"], ("sbuf", WK), 8, D, 1),
        (din["wvT"], ("hbm", wvq_st), 8, DV, 2),
        (din["wgT"], ("hbm", wgq_st), 8, DV, 3),
        (din["wgk1T"], ("sbuf", WGK1), 8, 16, 4),
        (din["woT"], ("hbm", woq_st), 16, D, 6),
    ]
    for (hbm, dest, nkt, dout, widx) in wjobs:
        per_unit = max(1, 4096 // dout)
        units = [(k0, min(k0 + per_unit, nkt)) for k0 in range(0, nkt, per_unit)]
        src3 = hbm.rearrange("(k p) o -> p k o", p=128)
        prevred = None
        for ui, (k0, k1) in enumerate(units):
            fl = (k1 - k0) * dout
            slot = ui % 2
            tin = em.dma("d", lambda e, k0=k0, k1=k1, slot=slot, src3=src3, dout=dout:
                         e.dma_start(out=STG[:, slot, 0:(k1 - k0) * dout]
                                     .rearrange("p (a b) -> p a b", a=(k1 - k0)),
                                     in_=src3[:, k0:k1, :]),
                         waits=[prevred] if prevred else ())
            em.cop("v", lambda e, slot=slot, fl=fl: e.reduce_sum(
                COLS[:, 40:41], STG[:, slot, 0:fl], axis=AX.X,
                apply_absolute_value=True), waits=[tin])
            if ui == 0:
                em.cop("v", lambda e: e.tensor_copy(COLS[:, 41:42], COLS[:, 40:41]))
            else:
                em.cop("v", lambda e: e.tensor_add(COLS[:, 41:42], COLS[:, 41:42],
                                                   COLS[:, 40:41]))
            prevred = em.tok("v")
        em.cop("t", lambda e: e.matmul(PA[0:1, 0:1], ONESCF[:], COLS[:, 41:42],
                                       start=True, stop=True), waits=[prevred])
        tmm = em.tok("t")
        numel = nkt * 128 * dout
        em.cop("v", lambda e, numel=numel: e.tensor_scalar(
            SROW[0:1, 0:1], PA[0:1, 0:1], 1.0 / numel, 1e-5, ALU.mult, ALU.max),
            waits=[tmm])
        tms = em.tok("v")
        em.cop("t", lambda e: e.matmul(PB[:, 0:1], ONESRF[:], SROW[0:1, 0:1],
                                       start=True, stop=True), waits=[tms])
        tmb = em.tok("t")
        em.cop("v", lambda e, widx=widx: e.tensor_copy(MW[:, widx:widx + 1], PB[:, 0:1]),
               waits=[tmb])
        em.cop("v", lambda e, widx=widx: e.reciprocal(WSC[:, widx:widx + 1],
                                                      MW[:, widx:widx + 1]))
        em.barrier()
        for ui, (k0, k1) in enumerate(units):
            fl = (k1 - k0) * dout
            slot = ui % 2
            tin = em.dma("d", lambda e, k0=k0, k1=k1, slot=slot, src3=src3, dout=dout:
                         e.dma_start(out=STG[:, slot, 0:(k1 - k0) * dout]
                                     .rearrange("p (a b) -> p a b", a=(k1 - k0)),
                                     in_=src3[:, k0:k1, :]),
                         waits=[em.tok("v"), em.tok("s"), em.tok("d")])
            em.cop("s", lambda e, slot=slot, fl=fl, widx=widx: e.activation(
                QSC[:, 0:fl], STG[:, slot, 0:fl], AF.Identity, scale=WSC[:, widx:widx + 1]),
                waits=[tin])
            tsc = em.tok("s")
            em.cop("v", lambda e, fl=fl: e.tensor_scalar(
                QSC[:, 0:fl], QSC[:, 0:fl], RC, RC, ALU.add, ALU.subtract), waits=[tsc])
            if dest[0] == "sbuf":
                em.cop("v", lambda e, fl=fl, dest=dest, k0=k0, k1=k1: e.tensor_scalar(
                    dest[1][:, k0:k1, :].rearrange("p a b -> p (a b)"),
                    QSC[:, 0:fl], 1.0, -1.0, ALU.min, ALU.max))
            else:
                em.cop("v", lambda e, fl=fl: e.tensor_scalar(
                    QSB[:, 0:fl], QSC[:, 0:fl], 1.0, -1.0, ALU.min, ALU.max))
                em.dma("d", lambda e, fl=fl, dest=dest, k0=k0, k1=k1: e.dma_start(
                    out=dest[1][:, k0:k1, :].rearrange("p a b -> p (a b)"),
                    in_=QSB[:, 0:fl]), waits=[em.tok("v")])
            em.barrier(["v", "s", "d"])
    if skipwq:
        tin = None
    tin = em.dma("d", lambda e: e.dma_start(out=STG[0:16, 0, 0:1024], in_=din["wgk2T"])) if not skipwq else None
    if not skipwq:
      em.cop("v", lambda e: e.reduce_sum(COLS[0:16, 40:41], STG[0:16, 0, 0:1024],
                                       axis=AX.X, apply_absolute_value=True), waits=[tin])
    if not skipwq:
      em.cop("t", lambda e: e.matmul(PA[0:1, 0:1], ONESCF[0:16, :], COLS[0:16, 40:41],
                                   start=True, stop=True), waits=[em.tok("v")])
      em.cop("v", lambda e: e.tensor_scalar(SROW[0:1, 0:1], PA[0:1, 0:1],
                                          1.0 / (16 * 1024), 1e-5, ALU.mult, ALU.max),
           waits=[em.tok("t")])
      em.cop("t", lambda e: e.matmul(PB[:, 0:1], ONESRF[:], SROW[0:1, 0:1],
                                   start=True, stop=True), waits=[em.tok("v")])
      em.cop("v", lambda e: e.tensor_copy(MW[:, 7:8], PB[:, 0:1]), waits=[em.tok("t")])
      em.cop("v", lambda e: e.reciprocal(WSC[:, 7:8], MW[:, 7:8]))
      em.cop("s", lambda e: e.activation(QSC[0:16, 0:1024], STG[0:16, 0, 0:1024],
                                       AF.Identity, scale=WSC[0:16, 7:8]), waits=[em.tok("v")])
      em.cop("v", lambda e: e.tensor_scalar(QSC[0:16, 0:1024], QSC[0:16, 0:1024],
                                          RC, RC, ALU.add, ALU.subtract),
           waits=[em.tok("s")])
      em.cop("v", lambda e: e.tensor_scalar(WGK2[:], QSC[0:16, 0:1024], 1.0, -1.0,
                                          ALU.min, ALU.max))
    em.barrier()
    em.flush()
    stup.close()
    if stage == 0:
        ph1.close()
        return nc

    XA = sb1("XA", [128, 8, T])
    XP = sb1("XP", [128, 8, T])
    U1 = sb1("U1", [128, 8, T])
    QSCR = sb1("QSCR", [128, 8, T])
    XI = sb1("XI", [128, 4, T], I32)
    XQB = sb1("XQB", [128, 8, T], BF16)
    QF = sb1("QF", [128, 8, T], BF16)
    KF = sb1("KF", [128, 8, T], BF16)
    VT = sb1("VT", [128, 2, DV], BF16)
    GFB = sb1("GFB", [128, 16, T], BF16)
    THB = sb1("THB", [128, 16, T], BF16)
    SGB = sb1("SGB", [128, 16, T], BF16)
    OGB = sb1("OGB", [128, 16, T], BF16)
    GL = sb1("GL", [128, 8, T], BF16)
    E1 = sb1("E1", [128, 8, C], BF16)
    E2 = sb1("E2", [128, 8, C], BF16)
    QS = sb1("QS", [128, 8, C], BF16)
    KS = sb1("KS", [128, 8, C], BF16)
    QI = sb1("QI", [128, 8, C], BF16)
    KX = sb1("KX", [128, 8, C], BF16)
    KXT = sb1("KXT", [128, 2, C], BF16)
    ABF = sb1("ABF", [128, 128])
    AB = sb1("AB", [128, 128], BF16)
    S16 = sb1("S16", [128, 2, 512], BF16)
    WSCR = XA
    BB3 = AP(QSCR, 0, [[8 * T, 128], [T, 8], [1, C]])
    BS3 = AP(QSCR, C, [[8 * T, 128], [T, 8], [1, C]])

    def BBs(k):
        return AP(QSCR, k * T, [[8 * T, 128], [1, C]])

    bcslot = [0]

    def bcast(j, waits=()):
        s = bcslot[0]
        bcslot[0] = 1 - s
        tk = em.cop("t", lambda e, s=s, j=j: e.matmul(
            PBC[:, s * 256:(s + 1) * 256], SELT[:, j, :],
            SCR[:].rearrange("a b c -> a (b c)"), start=True, stop=True), waits=waits)
        return PBC[:, s * 256:(s + 1) * 256], tk

    def flush_sc(c0, c1, waits):
        tb = em.cop("v", lambda e: e.tensor_copy(
            TMRB[:, :, 0:32].rearrange("p a b -> p (a b)") if False else
            AP(TMRB, 0, [[256, 128], [128, 2], [1, 32]]), SC[:, :, 0:32]), waits=waits)
        tks = []
        for tt in range(2):
            tks.append(em.cop("t", lambda e, tt=tt: e.transpose(
                PTR[0:32, tt * 128:tt * 128 + 128],
                AP(TMRB, tt * 128, [[256, 128], [1, 32]]), IDENT[:]), waits=[tb]))
        out_tks = [em.cop("v", lambda e: e.tensor_copy(
            SCR[:].rearrange("a b c -> a (b c)"), PTR[0:32, 0:256]), waits=tks)]
        return out_tks

    def stats_cols(src_bf, src_sq, mcol, sqcol, waits):
        t0 = []
        for (src, r0) in [(src_bf, 0), (src_sq, 1)]:
            for k in range(8):
                t0.append(em.cop("t", lambda e, src=src, k=k, r0=r0: e.matmul(
                    PA[0:1, r0 * 256:(r0 + 1) * 256], ONESC[:], src[:, k, :],
                    start=(k == 0), stop=(k == 7)), waits=waits))
        tr = em.cop("v", lambda e: e.tensor_scalar(SROW[0:1, 0:512], PA[0:1, 0:512],
                                                   1.0 / 1024.0, None, ALU.mult), waits=t0)
        tks = []
        for (r0, col) in [(0, mcol), (1, sqcol)]:
            for tt in range(2):
                tk = em.cop("t", lambda e, r0=r0, tt=tt: e.transpose(
                    PO2[0:128, 448 + r0 * 2 + tt:449 + r0 * 2 + tt],
                    SROW[0:1, r0 * 256 + tt * 128:r0 * 256 + (tt + 1) * 128],
                    IDENTF[0:1, 0:1]), waits=[tr])
                tks.append(em.cop("v", lambda e, col=col, tt=tt, r0=r0: e.tensor_copy(
                    SC[:, tt, col:col + 1],
                    PO2[0:128, 448 + r0 * 2 + tt:449 + r0 * 2 + tt]), waits=[tk]))
        return tks

    def rsqrt_cols(rstdcol, vcols, epsval):
        em.cop("v", lambda e: e.tensor_scalar(COLS[:, vcols:vcols + 2],
                                              SC[:, :, rstdcol], epsval, None, ALU.add))
        em.cop("s", lambda e: e.activation(SC[:, :, rstdcol], COLS[:, vcols:vcols + 2],
                                           AF.Ln), waits=[em.tok("v")])
        em.cop("s", lambda e: e.activation(SC[:, :, rstdcol], SC[:, :, rstdcol],
                                           AF.Exp, scale=-0.5))
        em.cop("v", lambda e: e.tensor_tensor(COLS[:, vcols + 2:vcols + 4],
                                              SC[:, :, rstdcol], SC[:, :, rstdcol],
                                              ALU.mult), waits=[em.tok("s")])
        em.cop("v", lambda e: e.tensor_tensor(COLS[:, vcols + 2:vcols + 4],
                                              COLS[:, vcols + 2:vcols + 4],
                                              COLS[:, vcols:vcols + 2], ALU.mult))
        em.cop("v", lambda e: e.tensor_scalar(COLS[:, vcols + 2:vcols + 4],
                                              COLS[:, vcols + 2:vcols + 4],
                                              -0.5, 1.5, ALU.mult, ALU.add))
        em.cop("v", lambda e: e.tensor_tensor(SC[:, :, rstdcol], SC[:, :, rstdcol],
                                              COLS[:, vcols + 2:vcols + 4], ALU.mult))
        return em.tok("v")

    def rstd_col(mcol, sqcol, rstdcol, epscol, waits):
        for tt in range(2):
            em.cop("v", lambda e, tt=tt: e.tensor_mul(COLS[:, 44:45],
                                                      SC[:, tt, mcol:mcol + 1],
                                                      SC[:, tt, mcol:mcol + 1]), waits=waits)
            em.cop("v", lambda e, tt=tt: e.tensor_sub(SC[:, tt, rstdcol:rstdcol + 1],
                                                      SC[:, tt, sqcol:sqcol + 1],
                                                      COLS[:, 44:45]))
        return rsqrt_cols(rstdcol, 40, 1e-5)

    def rowmax_col(w_ap, ndt, col, waits):
        t1 = em.cop("v", lambda e, w=w_ap: e.reduce_max(
            TMRB[:].rearrange("p a b -> p (a b)"), w.rearrange("p a b -> p b a"),
            axis=AX.X, apply_absolute_value=True), waits=waits)
        tks = []
        for tt in range(2):
            tk = em.cop("t", lambda e, tt=tt: e.transpose(
                PTR[:, tt * 128:(tt + 1) * 128], TMRB[:, tt, :], IDENT[:]), waits=[t1])
            tks.append(em.cop("v", lambda e, tt=tt, col=col: e.reduce_max(
                SC[:, tt, col:col + 1], PTR[:, tt * 128:(tt + 1) * 128], axis=AX.X),
                waits=[tk]))
        return tks

    def emit_ffn(rstd_sb, x_sb, out_t, ubf, waits):
        t0 = []
        for k in range(8):
            t0.append(em.cop("t", lambda e, k=k, ubf=ubf: e.matmul(
                PO1[0:64, 0:256], WIN[:, k, :], ubf[:, k, :], start=(k == 0),
                stop=(k == 7)), waits=waits))
        ts_ = em.cop("v", lambda e, r=rstd_sb: e.tensor_tensor(
            FS[:], PO1[0:64, 0:256], r, ALU.mult), waits=t0)
        em.mb()
        trep = []
        for kk in range(8):
            pd = PA if kk < 4 else PB
            off = (kk % 4) * 256
            trep.append(em.cop("t", lambda e, kk=kk, pd=pd, off=off: e.matmul(
                pd[:, off:off + 256], REP[:, kk, :], FS[:], start=True, stop=True),
                waits=[ts_]))
        em.mb()
        if dbg:
            if not hasattr(em, "_dumped_pa"):
                em._dumped_pa = True
                em.cop("v", lambda e: e.tensor_copy(QSCR[:, 0:4, :].rearrange("p a b -> p (a b)"), PA[:, 0:1024]))
                em.barrier()
                em.dma("d", lambda e: e.dma_start(out=dbg_out["d_pa"], in_=QSCR[:, 0:4, :].rearrange("p a b -> p (a b)")))
                em.dma("d", lambda e: e.dma_start(out=dbg_out["d_rep"],
                                                  in_=REPF[:].rearrange("p a b -> p (a b)")))
                em.barrier()
        trl = []
        for kk in range(8):
            pd = PA if kk < 4 else PB
            off = (kk % 4) * 256
            trl.append(em.cop("s", lambda e, kk=kk, pd=pd, off=off: e.activation(
                FH[:, kk, :], pd[:, off:off + 256], AF.Relu, bias=B1E[:, kk:kk + 1]),
                waits=trep))
        em.mb()
        tsum = []
        for kk in range(8):
            tsum.append(em.cop("t", lambda e, kk=kk: e.matmul(
                PO1[0:64, 256:512], SUM2[:, kk, :], FH[:, kk, :],
                start=(kk == 0), stop=(kk == 7)), waits=trl))
        em.mb()
        tg = em.cop("s", lambda e: e.activation(FG[:], PO1[0:64, 256:512], AF.Identity,
                                                bias=B2[:]), waits=tsum)
        em.mb()
        tw = []
        for m in range(8):
            pd = PA if m < 4 else PB
            off = (m % 4) * 256
            tw.append(em.cop("t", lambda e, m=m, pd=pd, off=off: e.matmul(
                pd[:, off:off + 256], WOUT[:, m * 128:(m + 1) * 128], FG[:],
                start=True, stop=False), waits=[tg]))
            tw.append(em.cop("t", lambda e, m=m, pd=pd, off=off: e.matmul(
                pd[:, off:off + 256], BOUTR[:, m, :], ONESR2[:], start=False, stop=True)))
        em.mb()
        outs = []
        for half, pd in [(0, PA), (1, PB)]:
            outs.append(em.cop("v", lambda e, half=half, pd=pd, x_sb=x_sb, on=out_t:
                               e.scalar_tensor_tensor(
                on[:, half * 4:(half + 1) * 4, :].rearrange("p a b -> p (a b)"),
                pd[:, 0:1024], 0.5,
                x_sb[:, half * 4:(half + 1) * 4, :].rearrange("p a b -> p (a b)"),
                ALU.mult, ALU.add), waits=tw))
        return outs

    def ln_to_ubf(x_sb, mcol, sqcol, rstdcol, bfb, sqb, ubf):
        em.cop("v", lambda e, x=x_sb, bfb=bfb: e.tensor_copy(
            bfb[:].rearrange("p a b -> p (a b)"), x[:].rearrange("p a b -> p (a b)")))
        em.cop("s", lambda e, bfb=bfb, sqb=sqb: e.activation(
            sqb[:].rearrange("p a b -> p (a b)"), bfb[:].rearrange("p a b -> p (a b)"),
            AF.Square), waits=[em.tok("v")])
        em.mb()
        if lncut < 2:
            return
        st_t = stats_cols(bfb, sqb, mcol, sqcol, [em.tok("s")])
        if lncut < 3:
            return
        rs_t = rstd_col(mcol, sqcol, rstdcol, CEPS5, st_t)
        if lncut < 4:
            return
        fl = flush_sc(mcol, mcol + 1, st_t) + flush_sc(rstdcol, rstdcol + 1, [rs_t])
        em.mb()
        if lncut < 5:
            return
        mb_, tkm = bcast(mcol, fl)
        if lncut < 6:
            return
        em.cop("v", lambda e, x=x_sb, mb_=mb_, ubf=ubf: e.tensor_tensor(
            ubf[:].rearrange("p a b -> p (a b)"), x[:].rearrange("p a b -> p (a b)"),
            AP(mb_.tensor, mb_.offset, [mb_.ap[0], [0, 8], mb_.ap[1]]), ALU.subtract),
            waits=[tkm])
        rb, tkr = bcast(rstdcol, fl)
        em.cop("v", lambda e, rb=rb: e.tensor_copy(MBC[:, 0, :], rb), waits=[tkr])
        em.mb()

    NW = {0: NQ, 1: NK, 2: NV, 3: NG, 4: NGK1}
    ICTOT = [0]
    for c in range(nchunks):
        em.dma("d", lambda e, c=c: e.dma_start(
            out=XA[:],
            in_=xT.rearrange("(k p) t -> p k t", p=128)[:, :, c * T:(c + 1) * T]))
        em.mb()
        if p1cut < -2:
            em.barrier()
            continue
        ln_to_ubf(XA, 0, 1, 2, QF, XQB, KF)
        if p1cut < -1:
            em.barrier()
            continue
        emit_ffn(MBC[0:64, 0, :], XA, XP, KF, [em.tok("v")])
        em.mb()
        if dbg and c == 0:
            em.dma("d", lambda e: e.dma_start(out=dbg_out["d_ub"], in_=KF[:].rearrange("p a b -> p (a b)")))
            em.dma("d", lambda e: e.dma_start(out=dbg_out["d_fs"], in_=FSF[:]))
            em.dma("d", lambda e: e.dma_start(out=dbg_out["d_fh"], in_=FH[:].rearrange("p a b -> p (a b)")))
            em.dma("d", lambda e: e.dma_start(out=dbg_out["d_fg"], in_=FGF[:]))
            em.dma("d", lambda e: e.dma_start(out=dbg_out["d_sc"], in_=SC[:].rearrange("p a b -> p (a b)")))
            em.dma("d", lambda e: e.dma_start(out=dbg_out["d_mbc"], in_=MBC[:].rearrange("p a b -> p (a b)")))
            em.barrier()
        if p1cut < 0:
            em.barrier()
            continue
        em.dma("d", lambda e, c=c: e.dma_start(
            out=xp_st[c], in_=XP[:].rearrange("p a b -> p (a b)")), waits=[em.tok("v")])
        if p1cut < 1:
            em.barrier()
            continue
        em.cop("v", lambda e: e.tensor_copy(QF[:].rearrange("p a b -> p (a b)"),
                                            XP[:].rearrange("p a b -> p (a b)")))
        em.cop("s", lambda e: e.activation(XQB[:].rearrange("p a b -> p (a b)"),
                                           QF[:].rearrange("p a b -> p (a b)"), AF.Square),
               waits=[em.tok("v")])
        em.mb()
        st_t = stats_cols(QF, XQB, 4, 5, [em.tok("s")])
        rs_t = rstd_col(4, 5, 6, CEPS5, st_t)
        for tt in range(2):
            em.cop("v", lambda e, tt=tt: e.tensor_mul(COLS[:, 44:45], SC[:, tt, 4:5],
                                                      SC[:, tt, 4:5]), waits=[rs_t])
            em.cop("v", lambda e, tt=tt: e.tensor_sub(COLS[:, 45:46], SC[:, tt, 5:6],
                                                      COLS[:, 44:45]))
            em.cop("v", lambda e, tt=tt: e.tensor_scalar(COLS[:, 46:47], COLS[:, 45:46],
                                                         1e-5, None, ALU.add))
            em.cop("v", lambda e, tt=tt: e.reciprocal(COLS[:, 47:48], COLS[:, 46:47]))
            em.cop("v", lambda e, tt=tt: e.tensor_mul(SC[:, tt, 28:29], COLS[:, 47:48],
                                                      COLS[:, 45:46]))
        rsqrt_cols(28, 40, 1e-8)
        for tt in range(2):
            em.cop("v", lambda e, tt=tt: e.tensor_mul(SC[:, tt, 7:8], SC[:, tt, 6:7],
                                                      SC[:, tt, 28:29]))
        fl2 = flush_sc(4, 5, st_t)
        em.mb()
        m2b, tk2 = bcast(4, fl2)
        em.cop("v", lambda e, m2b=m2b: e.tensor_tensor(
            U1[:].rearrange("p a b -> p (a b)"), XP[:].rearrange("p a b -> p (a b)"),
            AP(m2b.tensor, m2b.offset, [m2b.ap[0], [0, 8], m2b.ap[1]]), ALU.subtract),
            waits=[tk2])
        em.mb()

        if p1cut < 2:
            em.barrier()
            continue
        for si in range(5):
            nten = NW[si]
            tw = em.cop("v", lambda e, nten=nten: e.tensor_tensor(
                WSCR[:].rearrange("p a b -> p (a b)"),
                U1[:].rearrange("p a b -> p (a b)"),
                AP(nten, 0, [[8, 128], [1, 8], [0, T]]), ALU.mult))
            rowmax_col(WSCR[:], 8, 8 + si, [tw])
            em.mb()
        for tt in range(2):
            em.cop("v", lambda e, tt=tt: e.scalar_tensor_tensor(
                SC[:, tt, 13:18], SC[:, tt, 8:13], 1.0,
                AP(SC, tt * 32 + 7, [[64, 128], [0, 5]]), ALU.mult, ALU.mult))
            em.cop("v", lambda e, tt=tt: e.tensor_scalar(SC[:, tt, 13:18],
                                                         SC[:, tt, 13:18],
                                                         1e-5, None, ALU.max))
            em.cop("v", lambda e, tt=tt: e.reciprocal(SC[:, tt, 18:23], SC[:, tt, 13:18]))
            em.cop("v", lambda e, tt=tt: e.scalar_tensor_tensor(
                SC[:, tt, 8:13], SC[:, tt, 18:23], 127.0,
                AP(SC, tt * 32 + 7, [[64, 128], [0, 5]]), ALU.mult, ALU.mult))
            em.cop("v", lambda e, tt=tt: e.scalar_tensor_tensor(
                SC[:, tt, 23:26], SC[:, tt, 14:17], 1.0 / 127.0, MW[:, 1:4],
                ALU.mult, ALU.mult))
        fl3 = flush_sc(8, 13, [em.tok("v")]) + flush_sc(23, 26, [em.tok("v")])
        em.mb()

        def quantize(si):
            nten = NW[si]
            phib, tkb = bcast(8 + si, fl3)
            em.cop("v", lambda e, nten=nten: e.tensor_tensor(
                QSCR[:].rearrange("p a b -> p (a b)"),
                U1[:].rearrange("p a b -> p (a b)"),
                AP(nten, 0, [[8, 128], [1, 8], [0, T]]), ALU.mult))
            em.cop("v", lambda e, phib=phib: e.tensor_tensor(
                QSCR[:].rearrange("p a b -> p (a b)"),
                QSCR[:].rearrange("p a b -> p (a b)"),
                AP(phib.tensor, phib.offset, [phib.ap[0], [0, 8], phib.ap[1]]),
                ALU.mult), waits=[tkb])
            for hf in range(2):
                em.cop("v", lambda e, hf=hf: e.tensor_scalar(
                    XI[:].rearrange("p a b -> p (a b)"),
                    QSCR[:, hf * 4:(hf + 1) * 4, :].rearrange("p a b -> p (a b)"),
                    RC, RC, ALU.add, ALU.subtract), waits=[em.tok("s")])
                em.cop("s", lambda e, hf=hf: e.activation(
                    XQB[:, hf * 4:(hf + 1) * 4, :].rearrange("p a b -> p (a b)"),
                    XI[:].rearrange("p a b -> p (a b)"), AF.Copy),
                    waits=[em.tok("v")])
            return em.tok("s")

        if p1cut < 3:
            em.barrier()
            continue
        tq = quantize(0)
        tqm = []
        for m in range(8):
            pd = PA if m < 4 else PB
            off = (m % 4) * 256
            for k in range(8):
                tqm.append(em.cop("t", lambda e, m=m, k=k, pd=pd, off=off: e.matmul(
                    pd[:, off:off + 256], WQ[:, k, m * 128:(m + 1) * 128], XQB[:, k, :],
                    start=(k == 0), stop=(k == 7)), waits=[tq]))
        for half, pd in [(0, PA), (1, PB)]:
            em.cop("s", lambda e, half=half, pd=pd: e.activation(
                QF[:, half * 4:(half + 1) * 4, :].rearrange("p a b -> p (a b)"),
                pd[:, 0:1024], AF.Copy), waits=tqm)
        em.mb()
        if p1cut < 4:
            em.barrier()
            continue
        tq = quantize(1)
        tkm = []
        for m in range(8):
            pd = PA if m < 4 else PB
            off = (m % 4) * 256
            for k in range(8):
                tkm.append(em.cop("t", lambda e, m=m, k=k, pd=pd, off=off: e.matmul(
                    pd[:, off:off + 256], WK[:, k, m * 128:(m + 1) * 128], XQB[:, k, :],
                    start=(k == 0), stop=(k == 7)), waits=[tq]))
        for half, pd in [(0, PA), (1, PB)]:
            em.cop("s", lambda e, half=half, pd=pd: e.activation(
                KF[:, half * 4:(half + 1) * 4, :].rearrange("p a b -> p (a b)"),
                pd[:, 0:1024], AF.Copy), waits=tkm)
        em.mb()
        if p1cut < 5:
            em.barrier()
            continue
        tq = quantize(2)
        for u in range(8):
            wslot = u % 2
            twl = em.dma("pd", lambda e, u=u, wslot=wslot: e.dma_start(
                out=WST[:, wslot, :, :],
                in_=wvq_st[:, :, u * 256:(u + 1) * 256]),
                waits=[em.tok("t")])
            for tt in range(2):
                pd = [PA, PB][tt]
                tvm = []
                for k in range(8):
                    tvm.append(em.cop("t", lambda e, tt=tt, k=k, pd=pd, wslot=wslot:
                                      e.matmul(pd[:, 0:256],
                                               XQB[:, k, tt * 128:(tt + 1) * 128],
                                               WST[:, wslot, k, :],
                                               start=(k == 0), stop=(k == 7)),
                                      waits=[tq, twl]))
                em.cop("s", lambda e, tt=tt, u=u, pd=pd: e.activation(
                    VT[:, tt, u * 256:(u + 1) * 256], pd[:, 0:256], AF.Identity,
                    scale=SC[:, tt, 24:25]), waits=tvm)
            em.mb()
        if p1cut < 6:
            em.barrier()
            continue
        tq = quantize(4)
        ti1 = []
        for k in range(8):
            ti1.append(em.cop("t", lambda e, k=k: e.matmul(
                PO1[0:16, 0:256], WGK1[:, k, :], XQB[:, k, :],
                start=(k == 0), stop=(k == 7)), waits=[tq]))
        em.cop("v", lambda e: e.tensor_copy(I1[:], PO1[0:16, 0:256]), waits=ti1)
        em.cop("v", lambda e: e.tensor_copy(I1B[:], I1[:]))
        em.mb()
        if p1cut < 7:
            em.barrier()
            continue
        tq = quantize(3)
        psigb, tkg = bcast(25, fl3)
        em.cop("v", lambda e, psigb=psigb: e.tensor_copy(MBC[:, 1, :], psigb),
               waits=[tkg])
        for u in range(8):
            wslot = u % 2
            twl = em.dma("pd", lambda e, u=u, wslot=wslot: e.dma_start(
                out=WST[:, wslot, :, :],
                in_=wgq_st[:, :, u * 256:(u + 1) * 256]),
                waits=[em.tok("t")])
            for mm in range(2):
                m = u * 2 + mm
                pd = [PA, PB][mm]
                tgm = []
                for k in range(8):
                    tgm.append(em.cop("t", lambda e, mm=mm, k=k, pd=pd, wslot=wslot:
                                      e.matmul(pd[:, 0:256],
                                               WST[:, wslot, k, mm * 128:(mm + 1) * 128],
                                               XQB[:, k, :],
                                               start=(k == 0), stop=(k == 7)),
                                      waits=[tq, twl]))
                tgm.append(em.cop("v", lambda e, m=m, pd=pd: e.tensor_tensor(
                    GFB[:, m, :], pd[:, 0:256], MBC[:, 1, :], ALU.mult), waits=tgm))
            em.mb()
        if p1cut < 8:
            em.barrier()
            continue
        em.cop("s", lambda e: e.activation(SQ16[:], I1B[:], AF.Square))
        t0 = [em.cop("t", lambda e: e.matmul(PA[0:1, 0:256], ONESC[0:16, :], SQ16[:],
                                             start=True, stop=True),
                     waits=[em.tok("s")])]
        tr = em.cop("v", lambda e: e.tensor_scalar(SROW[0:1, 0:256], PA[0:1, 0:256],
                                                   1.0 / 16.0, None, ALU.mult), waits=t0)
        if g2cut < 2:
            em.barrier()
            continue
        for tt in range(2):
            tk = em.cop("t", lambda e, tt=tt: e.transpose(
                PO2[0:128, 448 + tt:449 + tt], SROW[0:1, tt * 128:(tt + 1) * 128],
                IDENTF[0:1, 0:1]), waits=[tr])
            em.cop("v", lambda e, tt=tt: e.tensor_copy(
                SC[:, tt, 26:27], PO2[0:128, 448 + tt:449 + tt]), waits=[tk])
        em.cop("s", lambda e: e.activation(SC[:, :, 26], SC[:, :, 26], AF.Ln,
                                           bias=CEPS8[:]), waits=[em.tok("v")])
        em.cop("s", lambda e: e.activation(SC[:, :, 26], SC[:, :, 26], AF.Exp, scale=-0.5))
        if g2cut < 3:
            em.barrier()
            continue
        em.cop("v", lambda e: e.tensor_scalar(I1[:], I1[:], NGK2[:], None, ALU.mult),
               waits=[em.tok("s")])
        em.cop("v", lambda e: e.tensor_copy(AP(TMRB, 0, [[256, 16], [1, 256]]), I1[:]))
        for tt in range(2):
            tk = em.cop("t", lambda e, tt=tt: e.transpose(
                PTR[:, tt * 128:tt * 128 + 16],
                AP(TMRB, tt * 128, [[256, 16], [1, 128]]), IDENT[0:16, 0:16]),
                waits=[em.tok("v")])
            em.cop("v", lambda e, tt=tt: e.reduce_max(
                SC[:, tt, 27:28], PTR[:, tt * 128:tt * 128 + 16], axis=AX.X,
                apply_absolute_value=True), waits=[tk])
        if g2cut < 4:
            em.barrier()
            continue
        for tt in range(2):
            em.cop("v", lambda e, tt=tt: e.tensor_mul(SC[:, tt, 27:28], SC[:, tt, 27:28],
                                                      SC[:, tt, 26:27]))
            em.cop("v", lambda e, tt=tt: e.tensor_scalar(SC[:, tt, 27:28],
                                                         SC[:, tt, 27:28],
                                                         1e-5, None, ALU.max))
            em.cop("v", lambda e, tt=tt: e.reciprocal(COLS[:, 44:45], SC[:, tt, 27:28]))
            em.cop("v", lambda e, tt=tt: e.scalar_tensor_tensor(
                SC[:, tt, 19:20], COLS[:, 44:45], 127.0, SC[:, tt, 26:27],
                ALU.mult, ALU.mult))
            em.cop("v", lambda e, tt=tt: e.scalar_tensor_tensor(
                SC[:, tt, 20:21], SC[:, tt, 27:28], 1.0 / 127.0, MW[:, 7:8],
                ALU.mult, ALU.mult))
        fl4 = flush_sc(19, 21, [em.tok("v")])
        em.mb()
        if g2cut < 5:
            em.barrier()
            continue
        phi2b, tka = bcast(19, fl4)
        em.cop("v", lambda e, phi2b=phi2b: e.tensor_tensor(I1[:], I1[:], phi2b[0:16, :],
                                                           ALU.mult), waits=[tka])
        em.cop("v", lambda e: e.tensor_scalar(XI[0:16, 0, 0:256], I1[:], RC, RC,
                                              ALU.add, ALU.subtract))
        em.cop("s", lambda e: e.activation(I1B[:], XI[0:16, 0, 0:256], AF.Copy),
               waits=[em.tok("v")])
        if g2cut < 6:
            em.barrier()
            continue
        psi2b, tkb2 = bcast(20, fl4)
        em.cop("v", lambda e, psi2b=psi2b: e.tensor_copy(MBC[:, 1, :], psi2b),
               waits=[tkb2])
        tz = []
        for m in range(8):
            pd = PA if m < 4 else PB
            off = (m % 4) * 256
            if g2cut < 7:
                continue
            tz.append(em.cop("t", lambda e, m=m, pd=pd, off=off: e.matmul(
                pd[:, off:off + 256], WGK2[:, m * 128:(m + 1) * 128], I1B[:],
                start=True, stop=True), waits=[em.tok("s")]))
        for m in range(8):
            pd = PA if m < 4 else PB
            off = (m % 4) * 256
            if g2cut < 8:
                continue
            tz.append(em.cop("v", lambda e, m=m, pd=pd, off=off: e.tensor_tensor(
                U1[:, m, :], pd[:, off:off + 256], MBC[:, 1, :], ALU.mult), waits=tz))
            if g2cut < 9:
                continue
            tz.append(em.cop("p", lambda e, m=m: e.tensor_scalar(
                U1[:, m, :], U1[:, m, :], BGK2[:, m:m + 1], None, ALU.add), waits=tz))
        em.mb()
        if p1cut < 9:
            em.barrier()
            continue
        em.cop("s", lambda e: e.activation(WSCR[:].rearrange("p a b -> p (a b)"),
                                           U1[:].rearrange("p a b -> p (a b)"),
                                           AF.Abs), waits=[em.tok("p"), em.tok("v")])
        em.cop("s", lambda e: e.activation(WSCR[:].rearrange("p a b -> p (a b)"),
                                           WSCR[:].rearrange("p a b -> p (a b)"),
                                           AF.Exp, scale=-1.0))
        em.cop("s", lambda e: e.activation(WSCR[:].rearrange("p a b -> p (a b)"),
                                           WSCR[:].rearrange("p a b -> p (a b)"),
                                           AF.Ln, bias=1.0))
        em.cop("v", lambda e: e.tensor_scalar(QSCR[:].rearrange("p a b -> p (a b)"),
                                              U1[:].rearrange("p a b -> p (a b)"),
                                              0.0, 1.0 / 16.0, ALU.min, ALU.mult))
        em.cop("v", lambda e: e.scalar_tensor_tensor(
            GL[:].rearrange("p a b -> p (a b)"),
            WSCR[:].rearrange("p a b -> p (a b)"), -1.0 / 16.0,
            QSCR[:].rearrange("p a b -> p (a b)"), ALU.mult, ALU.add),
            waits=[em.tok("s")])
        em.mb()
        if p1cut < 10:
            em.barrier()
            continue
        em.cop("s", lambda e: e.activation(THB[:].rearrange("p a b -> p (a b)"),
                                           GFB[:].rearrange("p a b -> p (a b)"),
                                           AF.Tanh, scale=0.5))
        em.cop("p", lambda e: e.tensor_scalar(THB[:].rearrange("p a b -> p (a b)"),
                                              THB[:].rearrange("p a b -> p (a b)"),
                                              0.5, 0.5, ALU.mult, ALU.add),
               waits=[em.tok("s")])
        em.cop("v", lambda e: e.tensor_tensor(SGB[:].rearrange("p a b -> p (a b)"),
                                              THB[:].rearrange("p a b -> p (a b)"),
                                              GFB[:].rearrange("p a b -> p (a b)"),
                                              ALU.mult), waits=[em.tok("p")])
        em.mb()

        if p1cut < 11:
            em.barrier()
            continue
        for ic in range(icpc):
            gic = ICTOT[0]
            ICTOT[0] += 1
            tsl = slice(ic * C, (ic + 1) * C)
            tsc = []
            for k in range(8):
                tsc.append(em.cop("v", lambda e, k=k, tsl=tsl: e.tensor_tensor_scan(
                    BBs(k), GL[:, k, tsl], ZEROT[:], 0.0, ALU.add, ALU.bypass)))
            em.cop("v", lambda e: e.tensor_copy(
                COLS[:, 0:8], AP(QSCR, C - 1, [[8 * T, 128], [T, 8]])), waits=tsc)
            em.cop("v", lambda e: e.tensor_copy(
                COLS[:, 8:16], AP(QSCR, C // 2 - 1, [[8 * T, 128], [T, 8]])))
            em.cop("v", lambda e: e.tensor_scalar(COLS[:, 16:24], COLS[:, 8:16],
                                                  -1.0, None, ALU.mult))
            em.cop("v", lambda e, gic=gic: e.tensor_copy(
                AP(LTAB, gic % 32, [[8 * 32, 128], [32, 8]]), COLS[:, 24:32]))
            em.cop("v", lambda e: e.tensor_add(COLS[:, 24:32], COLS[:, 24:32],
                                               COLS[:, 0:8]))
            em.cop("v", lambda e: e.tensor_sub(COLS[:, 32:40], COLS[:, 0:8],
                                               COLS[:, 8:16]))
            em.cop("s", lambda e: e.activation(ELT[:], COLS[:, 0:8], AF.Exp),
                   waits=[em.tok("v")])
            em.cop("s", lambda e: e.activation(COLS[:, 8:16], COLS[:, 8:16], AF.Exp))
            em.cop("s", lambda e: e.activation(COLS[:, 32:40], COLS[:, 32:40], AF.Exp))
            em.cop("v", lambda e: e.tensor_tensor(
                BS3, BB3, AP(COLS, 16, [[48, 128], [1, 8], [0, C]]), ALU.add))
            em.cop("s", lambda e: e.activation(E1[:], BS3, AF.Exp),
                   waits=[em.tok("v")])
            em.cop("s", lambda e: e.activation(E2[:], BS3, AF.Exp, scale=-1.0))
            em.mb()
            em.cop("v", lambda e, tsl=tsl: e.tensor_tensor(
                QS[:], QF[:, :, tsl], E1[:], ALU.mult), waits=[em.tok("s")])
            em.cop("v", lambda e, tsl=tsl: e.tensor_tensor(
                KS[:], KF[:, :, tsl], E2[:], ALU.mult))
            em.cop("v", lambda e: e.tensor_tensor(
                QI[:].rearrange("p a b -> p (a b)"),
                QS[:].rearrange("p a b -> p (a b)"),
                AP(COLS, 8, [[48, 128], [1, 8], [0, C]]), ALU.mult))
            em.cop("v", lambda e: e.tensor_tensor(
                KX[:].rearrange("p a b -> p (a b)"),
                KS[:].rearrange("p a b -> p (a b)"),
                AP(COLS, 32, [[48, 128], [1, 8], [0, C]]), ALU.mult))
            em.mb()
            em.dma("d", lambda e, gic=gic: e.dma_start(
                out=qt_st[gic], in_=QI[:].rearrange("p a b -> p (a b)")),
                waits=[em.tok("v")])
            for h in range(H):
                po = [PO1, PO2][h % 2]
                ts16 = em.cop("p", lambda e, h=h: e.tensor_copy(
                    S16[:].rearrange("p a b -> p (a b)"),
                    S32[:, 2 * h:2 * h + 2, :].rearrange("p a b -> p (a b)")))
                ta = []
                for kk in range(2):
                    ta.append(em.cop("t", lambda e, h=h, kk=kk: e.matmul(
                        PA[:, 0:128], KS[:, 2 * h + kk, :], QS[:, 2 * h + kk, :],
                        start=(kk == 0), stop=(kk == 1))))
                em.cop("v", lambda e: e.tensor_tensor(ABF[:], PA[:, 0:128],
                                                      TRIU[:], ALU.mult),
                       waits=ta)
                em.cop("v", lambda e, ic=ic: e.tensor_scalar(
                    AB[:], ABF[:], SC[:, ic, 23:24], None, ALU.mult))
                tab = em.tok("v")
                to = []
                for j in range(4):
                    to.append(em.cop("t", lambda e, h=h, j=j, ic=ic, po=po: e.matmul(
                        po[:, j * 128:(j + 1) * 128],
                        VT[:, ic, h * 512 + j * 128:h * 512 + (j + 1) * 128],
                        AB[:], start=True, stop=False), waits=[tab, ts16]))
                    for kk in range(2):
                        to.append(em.cop("t", lambda e, h=h, j=j, kk=kk, po=po: e.matmul(
                            po[:, j * 128:(j + 1) * 128],
                            S16[:, kk, j * 128:(j + 1) * 128],
                            QI[:, 2 * h + kk, :], start=False, stop=(kk == 1))))
                em.cop("v", lambda e, h=h, po=po, tsl=tsl: e.tensor_tensor(
                    OGB[:, 4 * h:4 * h + 4, tsl],
                    po[:, 0:512].rearrange("p (a b) -> p a b", a=4),
                    SGB[:, 4 * h:4 * h + 4, tsl], ALU.mult), waits=to)
                tkt = []
                for kk in range(2):
                    tkt.append(em.cop("t", lambda e, h=h, kk=kk: e.transpose(
                        PTR[:, kk * 128:(kk + 1) * 128], KX[:, 2 * h + kk, :],
                        IDENT[:])))
                for kk in range(2):
                    tkt.append(em.cop("v", lambda e, kk=kk, ic=ic: e.tensor_scalar(
                        KXT[:, kk, :], PTR[:, kk * 128:(kk + 1) * 128],
                        SC[:, ic, 23:24], None, ALU.mult), waits=tkt))
                tu = []
                for kk in range(2):
                    tu.append(em.cop("t", lambda e, h=h, kk=kk, ic=ic: e.matmul(
                        PB[:, kk * 512:(kk + 1) * 512], KXT[:, kk, :],
                        VT[:, ic, h * 512:(h + 1) * 512], start=True, stop=True),
                        waits=tkt))
                for kk in range(2):
                    em.cop("v", lambda e, h=h, kk=kk: e.scalar_tensor_tensor(
                        S32[:, 2 * h + kk, :], S32[:, 2 * h + kk, :],
                        ELT[:, 2 * h + kk:2 * h + kk + 1],
                        PB[:, kk * 512:(kk + 1) * 512], ALU.mult, ALU.add), waits=tu)
                em.mb()
        em.dma("d", lambda e, c=c: e.dma_start(
            out=og_st[c], in_=OGB[:].rearrange("p a b -> p (a b)")),
            waits=[em.tok("v")])
        em.dma("d", lambda e, c=c: e.dma_start(
            out=sg_st[c], in_=SGB[:].rearrange("p a b -> p (a b)")))
        if dbg and c == 0:
            for nm, src in [("d_xp", XP), ("d_q", QF), ("d_k", KF), ("d_gl", GL),
                            ("d_og", OGB), ("d_v", VT)]:
                em.dma("d", lambda e, nm=nm, src=src: e.dma_start(
                    out=dbg_out[nm], in_=src[:].rearrange("p a b -> p (a b)")))
        em.barrier()

    if stage == 1:
        em.flush()
        ph1.close()
        return nc

    em.dma("d", lambda e: e.dma_start(
        out=AP(cc_in, 0, [[512, 128], [512 * 128, 8], [1, 512]]), in_=S32[:]))
    em.barrier()
    em.op("p", lambda e: e.collective_compute(
        "AllGather", ALU.bypass, replica_groups=[[0, 1], [2, 3], [4, 5], [6, 7]],
        ins=[cc_in.ap().opt()], outs=[cc_out.ap().opt()]), inc=1)
    em.barrier()
    em.dma("d", lambda e: e.dma_start(
        out=S32[:], in_=AP(cc_out, 0, [[512, 128], [512 * 128, 8], [1, 512]])))
    em.barrier()
    em.cop("v", lambda e: e.tensor_scalar(S32[:].rearrange("p a b -> p (a b)"),
                                          S32[:].rearrange("p a b -> p (a b)"),
                                          SMASK[:], None, ALU.mult))
    if dbg:
        em.dma("d", lambda e: e.dma_start(out=dbg_out["d_s32"],
                                          in_=S32[:].rearrange("p a b -> p (a b)")),
               waits=[em.tok("v")])
    em.barrier()
    em.flush()
    ph1.close()
    if stage == 2:
        return nc

    ph3 = contextlib.ExitStack()
    sb3 = lambda n, s, d=F32: ph3.enter_context(nc.sbuf_tensor(n, s, d))
    WO = sb3("WO", [128, 16, D], BF16)
    OF = sb3("OF", [128, 16, T])
    TMQ = sb3("TMQ", [128, 16, T])
    OQ = sb3("OQ", [128, 16, T], BF16)
    XI3 = sb3("XI3", [128, 8, T], I32)
    T16 = sb3("T16", [128, 2, 512], BF16)
    QTL = sb3("QTL", [128, 8, C], BF16)
    OGL = sb3("OGL", [128, 16, T], BF16)
    SGL = sb3("SGL", [128, 16, T], BF16)
    XPL = sb3("XPL", [128, 8, T])
    XP2 = sb3("XP2", [128, 8, T])
    OUTB = sb3("OUTB", [128, 8, T])
    TP3 = sb3("TP3", [128, 8, T])
    UB3 = sb3("UB3", [128, 8, T], BF16)

    em.dma("d", lambda e: e.dma_start(out=WO[:].rearrange("p a b -> p (a b)"),
                                      in_=woq_st.rearrange("p a b -> p (a b)")))
    em.barrier()

    for c in range(nchunks):
        tl = []
        tl.append(em.dma("d", lambda e, c=c: e.dma_start(
            out=OGL[:].rearrange("p a b -> p (a b)"), in_=og_st[c])))
        tl.append(em.dma("d", lambda e, c=c: e.dma_start(
            out=SGL[:].rearrange("p a b -> p (a b)"), in_=sg_st[c])))
        tl.append(em.dma("d", lambda e, c=c: e.dma_start(
            out=XPL[:].rearrange("p a b -> p (a b)"), in_=xp_st[c])))
        em.mb()
        for ic in range(icpc):
            gic = c * icpc + ic
            tsl = slice(ic * C, (ic + 1) * C)
            tq3 = em.dma("pd", lambda e, gic=gic: e.dma_start(
                out=QTL[:].rearrange("p a b -> p (a b)"), in_=qt_st[gic]))
            tse = em.cop("s", lambda e, gic=gic: e.activation(
                ELT[:], AP(LTAB, gic % 32, [[8 * 32, 128], [32, 8]]), AF.Exp))
            for h in range(H):
                po = [PO1, PO2][h % 2]
                for kk in range(2):
                    em.cop("v", lambda e, h=h, kk=kk: e.tensor_scalar(
                        T16[:, kk, :], S32[:, 2 * h + kk, :],
                        ELT[:, 2 * h + kk:2 * h + kk + 1], None, ALU.mult),
                        waits=[tse])
                tt16 = em.tok("v")
                tc = []
                for j in range(4):
                    for kk in range(2):
                        tc.append(em.cop("t", lambda e, h=h, j=j, kk=kk, po=po:
                                         e.matmul(
                            po[:, j * 128:(j + 1) * 128],
                            T16[:, kk, j * 128:(j + 1) * 128],
                            QTL[:, 2 * h + kk, :], start=(kk == 0), stop=(kk == 1)),
                            waits=[tt16, tq3]))
                em.cop("v", lambda e, h=h, po=po, tsl=tsl: e.tensor_tensor(
                    OF[:, 4 * h:4 * h + 4, tsl],
                    po[:, 0:512].rearrange("p (a b) -> p a b", a=4),
                    SGL[:, 4 * h:4 * h + 4, tsl], ALU.mult), waits=tc + tl)
                em.mb()
        em.cop("v", lambda e: e.tensor_tensor(OF[:].rearrange("p a b -> p (a b)"),
                                              OF[:].rearrange("p a b -> p (a b)"),
                                              OGL[:].rearrange("p a b -> p (a b)"),
                                              ALU.add))
        em.mb()
        em.cop("s", lambda e: e.activation(OGL[:].rearrange("p a b -> p (a b)"),
                                           OF[:].rearrange("p a b -> p (a b)"),
                                           AF.Square))
        t0 = []
        for k in range(16):
            t0.append(em.cop("t", lambda e, k=k: e.matmul(
                PA[0:1, 0:256], ONESC[:], OGL[:, k, :], start=(k == 0),
                stop=(k == 15)), waits=[em.tok("s")]))
        tr = em.cop("v", lambda e: e.tensor_scalar(SROW[0:1, 0:256], PA[0:1, 0:256],
                                                   1.0 / DV, None, ALU.mult), waits=t0)
        for tt in range(2):
            tk = em.cop("t", lambda e, tt=tt: e.transpose(
                PO2[0:128, 448 + tt:449 + tt], SROW[0:1, tt * 128:(tt + 1) * 128],
                IDENTF[0:1, 0:1]), waits=[tr])
            em.cop("v", lambda e, tt=tt: e.tensor_copy(
                SC[:, tt, 0:1], PO2[0:128, 448 + tt:449 + tt]), waits=[tk])
        rsqrt_cols(0, 40, 1e-8)
        tw3 = em.cop("v", lambda e: e.tensor_tensor(
            TMQ[:].rearrange("p a b -> p (a b)"), OF[:].rearrange("p a b -> p (a b)"),
            AP(NO, 0, [[16, 128], [1, 16], [0, T]]), ALU.mult), waits=[em.tok("s")])
        rowmax_col(TMQ[:], 16, 1, [tw3])
        for tt in range(2):
            em.cop("v", lambda e, tt=tt: e.tensor_mul(SC[:, tt, 1:2], SC[:, tt, 1:2],
                                                      SC[:, tt, 0:1]))
            em.cop("v", lambda e, tt=tt: e.tensor_scalar(SC[:, tt, 1:2], SC[:, tt, 1:2],
                                                         1e-5, None, ALU.max))
            em.cop("v", lambda e, tt=tt: e.reciprocal(COLS[:, 44:45], SC[:, tt, 1:2]))
            em.cop("v", lambda e, tt=tt: e.scalar_tensor_tensor(
                SC[:, tt, 3:4], COLS[:, 44:45], 127.0, SC[:, tt, 0:1],
                ALU.mult, ALU.mult))
            em.cop("v", lambda e, tt=tt: e.scalar_tensor_tensor(
                SC[:, tt, 4:5], SC[:, tt, 1:2], 1.0 / 127.0, MW[:, 6:7],
                ALU.mult, ALU.mult))
        em.cop("v", lambda e: e.tensor_copy(AP(TMRB, 0, [[256, 128], [1, 2]]),
                                            SC[:, :, 4]))
        em.cop("v", lambda e: e.tensor_copy(SC[:, :, 5], AP(TMRB, 0, [[256, 128], [1, 2]])))
        em.cop("v", lambda e: e.tensor_sub(SC[:, :, 6], SC[:, :, 4], SC[:, :, 5]))
        fl5 = flush_sc(3, 7, [em.tok("v")])
        em.mb()
        phiob, tko = bcast(3, fl5)
        em.cop("v", lambda e, phiob=phiob: e.tensor_tensor(
            TMQ[:].rearrange("p a b -> p (a b)"), TMQ[:].rearrange("p a b -> p (a b)"),
            AP(phiob.tensor, phiob.offset, [phiob.ap[0], [0, 16], phiob.ap[1]]),
            ALU.mult), waits=[tko])
        for half in range(2):
            em.cop("v", lambda e, half=half: e.tensor_scalar(
                XI3[:].rearrange("p a b -> p (a b)"),
                TMQ[:, half * 8:(half + 1) * 8, :].rearrange("p a b -> p (a b)"),
                RC, RC, ALU.add, ALU.subtract))
            em.cop("s", lambda e, half=half: e.activation(
                OQ[:, half * 8:(half + 1) * 8, :].rearrange("p a b -> p (a b)"),
                XI3[:].rearrange("p a b -> p (a b)"), AF.Copy), waits=[em.tok("v")])
            em.mb()
        s_ = bcslot[0]
        bcslot[0] = 1 - s_
        tp1 = em.cop("t", lambda e, s_=s_: e.matmul(
            PBC[:, s_ * 256:(s_ + 1) * 256], SELT[:, 5, :],
            SCR[:].rearrange("a b c -> a (b c)"), start=True, stop=False), waits=fl5)
        tp2 = em.cop("t", lambda e, s_=s_: e.matmul(
            PBC[:, s_ * 256:(s_ + 1) * 256], SELT[:, 6, :],
            SCR[:].rearrange("a b c -> a (b c)"), start=False, stop=True))
        em.cop("v", lambda e, s_=s_: e.tensor_copy(
            MBC[:, 0, :], PBC[:, s_ * 256:(s_ + 1) * 256]), waits=[tp1, tp2])
        two = []
        for m in range(8):
            pd = PA if m < 4 else PB
            off = (m % 4) * 256
            for k in range(16):
                two.append(em.cop("t", lambda e, m=m, k=k, pd=pd, off=off: e.matmul(
                    pd[:, off:off + 256], WO[:, k, m * 128:(m + 1) * 128], OQ[:, k, :],
                    start=(k == 0), stop=(k == 15)), waits=[em.tok("s")]))
        for half, pd in [(0, PA), (1, PB)]:
            em.cop("v", lambda e, half=half, pd=pd: e.tensor_tensor(
                TP3[:, half * 4:(half + 1) * 4, :].rearrange("p a b -> p (a b)"),
                pd[:, 0:1024],
                AP(MBC, 0, [[2 * T, 128], [0, 4], [1, T]]), ALU.mult), waits=two)
        em.cop("v", lambda e: e.tensor_add(XP2[:].rearrange("p a b -> p (a b)"),
                                           TP3[:].rearrange("p a b -> p (a b)"),
                                           XPL[:].rearrange("p a b -> p (a b)")))
        em.mb()
        ln_to_ubf(XP2, 8, 9, 10, OQ[:, 0:8, :], OQ[:, 8:16, :], UB3)
        emit_ffn(MBC[0:64, 0, :], XP2, OUTB, UB3, [em.tok("v")])
        if dbg and c == 0:
            em.dma("d", lambda e: e.dma_start(out=dbg_out["d_of"],
                                              in_=OF[:].rearrange("p a b -> p (a b)")),
                   waits=[em.tok("v")])
        em.mb()
        em.dma("d", lambda e, c=c: e.dma_start(
            out=yT.rearrange("(k p) t -> p k t", p=128)[:, :, c * T:(c + 1) * T],
            in_=OUTB[:]), waits=[em.tok("v")])
        em.barrier()

    em.flush()
    ph3.close()
    return nc


_CACHE = {}


def _host_inputs(inputs):
    f32 = np.float32
    wq = inputs["L0_attn_wq"]; wk = inputs["L0_attn_wk"]
    wv = inputs["L0_attn_wv"]; wg = inputs["L0_attn_wg"]
    wgk1 = inputs["L0_attn_wgk1"]; wgk2 = inputs["L0_attn_wgk2"]
    wo = inputs["L0_attn_wo"]
    W1 = np.asarray(inputs["L0_ffn_W1"], f32); b1 = np.asarray(inputs["L0_ffn_b1"], f32)
    W2 = np.asarray(inputs["L0_ffn_W2"], f32)
    bin_ = np.asarray(inputs["L0_ffn_bin_"], f32)
    com = {}
    com["wqT"] = np.ascontiguousarray(np.asarray(wq, f32).T)
    com["wkT"] = np.ascontiguousarray(np.asarray(wk, f32).T)
    com["wvT"] = np.ascontiguousarray(np.asarray(wv, f32).T)
    com["wgT"] = np.ascontiguousarray(np.asarray(wg, f32).T)
    com["wgk1T"] = np.ascontiguousarray(np.asarray(wgk1, f32).T)
    com["wgk2T"] = np.ascontiguousarray(np.asarray(wgk2, f32).T)
    com["woT"] = np.ascontiguousarray(np.asarray(wo, f32).T)
    arr = lambda v, nt: np.ascontiguousarray(np.asarray(v, f32).reshape(nt, 128).T)
    com["nq_a"] = arr(inputs["L0_attn_nq"], 8)
    com["nk_a"] = arr(inputs["L0_attn_nk"], 8)
    com["nv_a"] = arr(inputs["L0_attn_nv"], 8)
    com["ng_a"] = arr(inputs["L0_attn_ng"], 8)
    com["ngk1_a"] = arr(inputs["L0_attn_ngk1"], 8)
    com["ngk2_a"] = np.ascontiguousarray(np.asarray(inputs["L0_attn_ngk2"], f32).reshape(16, 1))
    com["bgk2_a"] = arr(inputs["L0_attn_bgk2"], 8)
    com["no_a"] = arr(inputs["L0_attn_no"], 16)
    com["Win"] = np.ascontiguousarray(np.asarray(inputs["L0_ffn_Win"], f32))
    com["WoutT"] = np.ascontiguousarray(np.asarray(inputs["L0_ffn_Wout"], f32))
    repw1 = np.zeros((64, 8, 128), f32)
    sumw2 = np.zeros((128, 8, 64), f32)
    w1a = np.zeros((128, 8), f32); b1a = np.zeros((128, 8), f32)
    bina = np.zeros((128, 8), f32)
    for k in range(8):
        for mloc in range(8):
            m = 8 * k + mloc
            for i in range(16):
                p = mloc * 16 + i
                repw1[m, k, p] = W1[m, i]
                sumw2[p, k, m] = W2[m, i]
                w1a[p, k] = W1[m, i]
                b1a[p, k] = b1[m, i]
                bina[p, k] = bin_[m]
    com["repw1"] = repw1; com["sumw2"] = sumw2
    com["w1_a"] = w1a; com["b1_a"] = b1a; com["binmi_a"] = bina
    com["b2_a"] = np.ascontiguousarray(np.asarray(inputs["L0_ffn_b2"], f32).reshape(64, 1))
    com["bout_r"] = np.ascontiguousarray(
        np.asarray(inputs["L0_ffn_bout"], f32).reshape(1, 8, 128).astype(NB))
    com["triu"] = np.ascontiguousarray(np.triu(np.ones((128, 128), np.float32)))
    com["ident"] = np.ascontiguousarray(np.eye(128, dtype=f32).astype(NB))
    com["identf"] = np.ascontiguousarray(np.eye(128, dtype=f32))
    com["ones_c"] = np.ones((128, 1), f32).astype(NB)
    com["ones_r2"] = np.ones((1, 256), f32).astype(NB)
    com["ones_cf"] = np.ones((128, 1), f32)
    com["ones_rf"] = np.ones((1, 128), f32)
    sel = np.zeros((32, 32, 128), f32)
    for j in range(32):
        sel[j, j, :] = 1.0
    com["sel"] = np.ascontiguousarray(sel.astype(NB))
    return com


def kernel(**inputs):
    x = np.asarray(inputs["L0_attn_hidden_states"], np.float32)
    B, L, Dm = x.shape
    half = L // 2
    com = _host_inputs(inputs)
    if "nc" not in _CACHE:
        _CACHE["nc"] = build_nc(16)
    nc = _CACHE["nc"]
    in_maps = []
    for core in range(8):
        b, s = core // 2, core % 2
        m = dict(com)
        m["xT"] = np.ascontiguousarray(x[b, s * half:(s + 1) * half, :].T)
        m["smask"] = np.full((128, 1), float(s), np.float32)
        in_maps.append(m)
    res = run_bass_kernel_spmd(nc, in_maps, core_ids=list(range(8)))
    out = np.empty_like(x)
    for core in range(8):
        b, s = core // 2, core % 2
        out[b, s * half:(s + 1) * half, :] = np.asarray(res.results[core]["yT"]).T
    return out


# version 02
# speedup vs baseline: 1.0419x; 1.0419x over previous
import numpy as np
import ml_dtypes
import contextlib
import concourse.bass as bass
import concourse.mybir as mybir
from concourse.bass_utils import run_bass_kernel_spmd

F32 = mybir.dt.float32
BF16 = mybir.dt.bfloat16
I32 = mybir.dt.int32
U8 = mybir.dt.uint8
AF = mybir.ActivationFunctionType
ALU = mybir.AluOpType
AX = mybir.AxisListType
AP = bass.AP
NB = ml_dtypes.bfloat16

RC = 12582912.0
D = 1024
DV = 2048
H = 4
T = 256
C = 128
SERIAL = False


class EM:
    ENGS = ("t", "v", "s", "p", "d")
    SEMK = ("t", "v", "s", "p", "d", "pd")

    def __init__(self, nc, stack):
        self.nc = nc
        self.q = {e: [] for e in self.ENGS}
        self.cnt = {e: 0 for e in self.SEMK}
        self.sems = {e: stack.enter_context(nc.semaphore("sem_" + e)) for e in self.SEMK}
        self.waited = {e: {e2: 0 for e2 in self.SEMK} for e in self.ENGS}

    def op(self, eng, fn, waits=(), inc=0):
        qk = "p" if eng == "pd" else eng
        wl = []
        if eng in ("v", "s", "p", "d", "pd") and self.cnt[eng] > 0:
            wl.append((self.sems[eng], self.cnt[eng]))
        for w in waits:
            if w is None:
                continue
            se, sv = w
            if se == eng:
                continue
            if self.waited[qk][se] < sv:
                self.waited[qk][se] = sv
                wl.append((self.sems[se], sv))
        sem = self.sems[eng]

        def run(e, wl=wl, fn=fn, inc=inc, sem=sem):
            for (s, v) in wl:
                e.wait_ge(s, v)
            ins = fn(e)
            if inc:
                ins.then_inc(sem, inc)
            return ins
        self.q[qk].append(run)
        if inc:
            self.cnt[eng] += inc
        return (eng, self.cnt[eng])

    def cop(self, eng, fn, waits=()):
        return self.op(eng, fn, waits=waits, inc=1)

    def dma(self, eng, fn, waits=()):
        return self.op(eng, fn, waits=waits, inc=16)

    def tok(self, eng):
        return (eng, self.cnt[eng])

    def barrier(self, engs=None):
        engs = engs or self.ENGS
        toks = [(e, self.cnt[e]) for e in (self.SEMK if engs is self.ENGS or engs == self.ENGS else engs)]
        for e in engs:
            for (se, sv) in toks:
                if se != e and self.waited[e][se] < sv:
                    self.waited[e][se] = sv
                    sem = self.sems[se]
                    self.q[e].append(lambda en, s=sem, v=sv: en.wait_ge(s, v))

    def mb(self):
        if SERIAL or not getattr(self, "pipe", False):
            self.barrier()

    def flush(self):
        with self.nc.Block() as blk:
            for ename, attr in [("t", "tensor"), ("v", "vector"), ("s", "scalar"),
                                ("p", "gpsimd"), ("d", "sync")]:
                ops = self.q[ename]
                if not ops:
                    continue
                getattr(blk, attr)(lambda e, ops=tuple(ops): [f(e) for f in ops])
        self.q = {e: [] for e in self.ENGS}


def build_nc(nchunks=16, dbg=False, stage=3, p1cut=99, lncut=99, fcut=99, g2cut=99):
    ntok = nchunks * T
    nic = ntok // C
    icpc = T // C
    import os as _os
    nc = bass.Bass("TRN2", target_bir_lowering=False,
                   detect_race_conditions=_os.environ.get("KRACE", "1") == "1")
    din = {}

    def inp(name, shape, dt=F32):
        din[name] = nc.dram_tensor(name, shape, dt, kind="ExternalInput").ap()
        return din[name]

    xT = inp("xT", [D, ntok])
    for nm, shp in [("wqT", [D, D]), ("wkT", [D, D]), ("wvT", [D, DV]),
                    ("wgT", [D, DV]), ("wgk1T", [D, 16]), ("wgk2T", [16, D]),
                    ("woT", [DV, D]), ("nq_a", [128, 8]), ("nk_a", [128, 8]),
                    ("nv_a", [128, 8]), ("ng_a", [128, 8]), ("ngk1_a", [128, 8]),
                    ("ngk2_a", [16, 1]), ("bgk2_a", [128, 8]), ("no_a", [128, 16]),
                    ("Win", [D, 64]), ("WoutT", [64, D]), ("repw1", [64, 8, 128]),
                    ("w1_a", [128, 8]), ("b1_a", [128, 8]), ("binmi_a", [128, 8]),
                    ("sumw2", [128, 8, 64]), ("b2_a", [64, 1]),
                    ("identf", [128, 128]), ("ones_cf", [128, 1]), ("ones_rf", [1, 128]),
                    ("smask", [128, 1])]:
        inp(nm, shp)
    inp("triu", [128, 128])
    inp("ident", [128, 128], BF16)
    inp("ones_c", [128, 1], BF16)
    inp("ones_r2", [1, 256], BF16)
    inp("sel", [32, 32, 128], BF16)
    inp("bout_r", [1, 8, 128], BF16)

    yT = nc.dram_tensor("yT", [D, ntok], F32, kind="ExternalOutput").ap()
    dbg_out = {}
    if dbg:
        for nm, shp, dt in [("d_rep", [128, 8 * 128], BF16), ("d_pa", [128, 1024], F32),
                            ("d_ub", [128, 8 * T], BF16), ("d_fs", [128, T], BF16),
                            ("d_fh", [128, 8 * T], BF16), ("d_fg", [128, T], BF16),
                            ("d_sc", [128, 2 * 32], F32), ("d_mbc", [128, 2 * T], F32),
                            ("d_xp", [128, 8 * T], F32), ("d_q", [128, 8 * T], BF16),
                            ("d_k", [128, 8 * T], BF16), ("d_gl", [128, 8 * T], BF16),
                            ("d_og", [128, 16 * T], BF16), ("d_of", [128, 16 * T], F32),
                            ("d_s32", [128, 8 * 512], F32), ("d_v", [128, 2 * 2048], BF16)]:
            dbg_out[nm] = nc.dram_tensor(nm, shp, dt, kind="ExternalOutput").ap()

    og_st = nc.dram_tensor("og_st", [nchunks, 128, 16 * T], BF16).ap()
    sg_st = nc.dram_tensor("sg_st", [nchunks, 128, 16 * T], BF16).ap()
    qt_st = nc.dram_tensor("qt_st", [nic, 128, 8 * C], BF16).ap()
    xp_st = nc.dram_tensor("xp_st", [nchunks, 128, 8 * T], F32).ap()
    wvq_st = nc.dram_tensor("wvq_st", [128, 8, DV], BF16).ap()
    wgq_st = nc.dram_tensor("wgq_st", [128, 8, DV], BF16).ap()
    woq_st = nc.dram_tensor("woq_st", [128, 16, D], BF16).ap()
    cc_in = nc.dram_tensor("cc_in", [1024, 512], F32)
    cc_out = nc.dram_tensor("cc_out", [2048, 512], F32)

    perm = contextlib.ExitStack()
    em = EM(nc, perm)
    sbp = lambda n, s, d=F32: perm.enter_context(nc.sbuf_tensor(n, s, d))

    WGK1 = sbp("WGK1", [128, 8, 16], BF16)
    WGK2F = sbp("WGK2F", [128, D], BF16)
    WGK2 = WGK2F[0:16, :]
    WIN = sbp("WIN", [128, 8, 64], BF16)
    WOUTF = sbp("WOUTF", [128, D], BF16)
    WOUT = WOUTF[0:64, :]
    REPF = sbp("REPF", [128, 8, 128], BF16)
    REP = REPF[0:64, :, :]
    SUM2 = sbp("SUM2", [128, 8, 64], BF16)
    B1E = sbp("B1E", [128, 8])
    NQ = sbp("NQ", [128, 8]); NK = sbp("NK", [128, 8]); NV = sbp("NV", [128, 8])
    NG = sbp("NG", [128, 8]); NGK1 = sbp("NGK1", [128, 8]); NGK2F = sbp("NGK2F", [128, 1]); NGK2 = NGK2F[0:16, :]
    BGK2 = sbp("BGK2", [128, 8]); NO = sbp("NO", [128, 16])
    B2F = sbp("B2F", [128, 1]); B2 = B2F[0:64, :]
    BOUTRF = sbp("BOUTRF", [128, 8, 128], BF16); BOUTR = BOUTRF[0:1, :, :]
    TRIU = sbp("TRIU", [128, 128])
    IDENT = sbp("IDENT", [128, 128], BF16)
    IDENTF = sbp("IDENTF", [128, 128], F32)
    ONESC = sbp("ONESC", [128, 1], BF16)
    ONESR2F = sbp("ONESR2F", [128, 256], BF16); ONESR2 = ONESR2F[0:1, :]
    ONESCF = sbp("ONESCF", [128, 1], F32)
    ONESRFF = sbp("ONESRFF", [128, 128], F32); ONESRF = ONESRFF[0:1, :]
    SELTF = sbp("SELTF", [128, 32, 128], BF16); SELT = SELTF[0:32, :, :]
    SMASK = sbp("SMASK", [128, 1])
    ZEROT = sbp("ZEROT", [128, 128])
    CEPS5 = sbp("CEPS5", [128, 1]); CEPS8 = sbp("CEPS8", [128, 1])
    MW = sbp("MW", [128, 8]); WSC = sbp("WSC", [128, 8])
    S32 = sbp("S32", [128, 8, 512])
    LTAB = sbp("LTAB", [128, 8, 32])
    COLS = sbp("COLS", [128, 48])
    SC = sbp("SC", [128, 2, 32])
    SCRF = sbp("SCRF", [128, 2, 128], BF16); SCR = SCRF[0:32, :, :]
    SROWF = sbp("SROWF", [128, 512]); SROW = SROWF[0:1, :]
    MBC = sbp("MBC", [128, 2, T])
    FSF = sbp("FSF", [128, T], BF16); FS = FSF[0:64, :]
    FH = sbp("FH", [128, 8, T], BF16)
    FGF = sbp("FGF", [128, T], BF16); FG = FGF[0:64, :]
    I1F = sbp("I1F", [128, T]); I1 = I1F[0:16, :]
    I1BF = sbp("I1BF", [128, T], BF16); I1B = I1BF[0:16, :]
    SQ16F = sbp("SQ16F", [128, T], BF16); SQ16 = SQ16F[0:16, :]
    TMRB = sbp("TMRB", [128, 2, 128], BF16)
    ELT = sbp("ELT", [128, 8])

    PA = perm.enter_context(nc.psum_tensor("PA", [128, 1024], F32))
    PB = perm.enter_context(nc.psum_tensor("PB", [128, 1024], F32))
    PO1 = perm.enter_context(nc.psum_tensor("PO1", [128, 512], F32))
    PO2 = perm.enter_context(nc.psum_tensor("PO2", [128, 512], F32))
    PBC = perm.enter_context(nc.psum_tensor("PBC", [128, 512], F32))
    PTR = perm.enter_context(nc.psum_tensor("PTR", [128, 256], BF16))

    ph1 = contextlib.ExitStack()
    sb1 = lambda n, s, d=F32: ph1.enter_context(nc.sbuf_tensor(n, s, d))
    WQ = sb1("WQ", [128, 8, D], BF16)
    WK = sb1("WK", [128, 8, D], BF16)
    WST = sb1("WST", [128, 2, 8, 256], BF16)

    stup = contextlib.ExitStack()
    sbs = lambda n, s, d=F32: stup.enter_context(nc.sbuf_tensor(n, s, d))
    STG = sbs("STG", [128, 2, 4096])
    QSC = sbs("QSC", [128, 4096])
    QSB = sbs("QSB", [128, 4096], BF16)
    FSTG = sbs("FSTG", [128, 8, 64])
    WOSF = sbs("WOSF", [128, D]); WOS = WOSF[0:64, :]
    RPSF = sbs("RPSF", [128, 8, 128]); RPS = RPSF[0:64, :, :]
    SMS = sbs("SMS", [128, 8, 64])

    for name, dst in [("nq_a", NQ), ("nk_a", NK), ("nv_a", NV), ("ng_a", NG),
                      ("ngk1_a", NGK1), ("ngk2_a", NGK2), ("bgk2_a", BGK2),
                      ("no_a", NO), ("b2_a", B2), ("bout_r", BOUTR),
                      ("triu", TRIU), ("ident", IDENT), ("identf", IDENTF),
                      ("ones_c", ONESC), ("ones_r2", ONESR2), ("ones_cf", ONESCF),
                      ("ones_rf", ONESRF), ("sel", SELT), ("smask", SMASK)]:
        em.dma("d", lambda e, s=din[name], d=dst: e.dma_start(out=d[:], in_=s))
    em.dma("d", lambda e: e.dma_start(out=FSTG[:], in_=din["Win"].rearrange("(k p) m -> p k m", p=128)))
    em.dma("d", lambda e: e.dma_start(out=WOS[:], in_=din["WoutT"]))
    em.dma("d", lambda e: e.dma_start(out=RPS[:], in_=din["repw1"]))
    em.dma("d", lambda e: e.dma_start(out=SMS[:], in_=din["sumw2"]))
    em.dma("d", lambda e: e.dma_start(out=COLS[:, 0:8], in_=din["w1_a"]))
    em.dma("d", lambda e: e.dma_start(out=COLS[:, 8:16], in_=din["b1_a"]))
    em.dma("d", lambda e: e.dma_start(out=COLS[:, 16:24], in_=din["binmi_a"]))
    tld = em.tok("d")
    em.cop("v", lambda e: e.memset(ZEROT[:], 0.0))
    em.cop("v", lambda e: e.memset(CEPS5[:], 1e-5))
    em.cop("v", lambda e: e.memset(CEPS8[:], 1e-8))
    em.cop("v", lambda e: e.memset(S32[:].rearrange("p a b -> p (a b)"), 0.0))
    em.cop("v", lambda e: e.memset(LTAB[:].rearrange("p a b -> p (a b)"), 0.0))
    em.cop("v", lambda e: e.memset(SC[:].rearrange("p a b -> p (a b)"), 0.0))
    em.cop("v", lambda e: e.memset(COLS[:, 24:48], 0.0), waits=[tld])
    em.cop("v", lambda e: e.memset(SROW[:], 0.0))
    em.cop("v", lambda e: e.tensor_copy(WIN[:].rearrange("p a b -> p (a b)"),
                                        FSTG[:].rearrange("p a b -> p (a b)")), waits=[tld])
    em.cop("v", lambda e: e.tensor_copy(WOUT[:], WOS[:]))
    em.cop("v", lambda e: e.tensor_copy(REP[:].rearrange("p a b -> p (a b)"),
                                        RPS[:].rearrange("p a b -> p (a b)")))
    em.cop("v", lambda e: e.tensor_copy(SUM2[:].rearrange("p a b -> p (a b)"),
                                        SMS[:].rearrange("p a b -> p (a b)")))
    em.cop("v", lambda e: e.tensor_mul(B1E[:], COLS[:, 0:8], COLS[:, 16:24]), waits=[tld])
    em.cop("v", lambda e: e.tensor_add(B1E[:], B1E[:], COLS[:, 8:16]))
    em.barrier()

    skipwq = _os.environ.get("SKIPWQ", "0") == "1"
    if skipwq:
        wjobs = []
    else:
        pass
    wjobs = [] if skipwq else [
        (din["wqT"], ("sbuf", WQ), 8, D, 0),
        (din["wkT"], ("sbuf", WK), 8, D, 1),
        (din["wvT"], ("hbm", wvq_st), 8, DV, 2),
        (din["wgT"], ("hbm", wgq_st), 8, DV, 3),
        (din["wgk1T"], ("sbuf", WGK1), 8, 16, 4),
        (din["woT"], ("hbm", woq_st), 16, D, 6),
    ]
    for (hbm, dest, nkt, dout, widx) in wjobs:
        per_unit = max(1, 4096 // dout)
        units = [(k0, min(k0 + per_unit, nkt)) for k0 in range(0, nkt, per_unit)]
        src3 = hbm.rearrange("(k p) o -> p k o", p=128)
        prevred = None
        for ui, (k0, k1) in enumerate(units):
            fl = (k1 - k0) * dout
            slot = ui % 2
            tin = em.dma("d", lambda e, k0=k0, k1=k1, slot=slot, src3=src3, dout=dout:
                         e.dma_start(out=STG[:, slot, 0:(k1 - k0) * dout]
                                     .rearrange("p (a b) -> p a b", a=(k1 - k0)),
                                     in_=src3[:, k0:k1, :]),
                         waits=[prevred] if prevred else ())
            em.cop("v", lambda e, slot=slot, fl=fl: e.reduce_sum(
                COLS[:, 40:41], STG[:, slot, 0:fl], axis=AX.X,
                apply_absolute_value=True), waits=[tin])
            if ui == 0:
                em.cop("v", lambda e: e.tensor_copy(COLS[:, 41:42], COLS[:, 40:41]))
            else:
                em.cop("v", lambda e: e.tensor_add(COLS[:, 41:42], COLS[:, 41:42],
                                                   COLS[:, 40:41]))
            prevred = em.tok("v")
        em.cop("t", lambda e: e.matmul(PA[0:1, 0:1], ONESCF[:], COLS[:, 41:42],
                                       start=True, stop=True), waits=[prevred])
        tmm = em.tok("t")
        numel = nkt * 128 * dout
        em.cop("v", lambda e, numel=numel: e.tensor_scalar(
            SROW[0:1, 0:1], PA[0:1, 0:1], 1.0 / numel, 1e-5, ALU.mult, ALU.max),
            waits=[tmm])
        tms = em.tok("v")
        em.cop("t", lambda e: e.matmul(PB[:, 0:1], ONESRF[:], SROW[0:1, 0:1],
                                       start=True, stop=True), waits=[tms])
        tmb = em.tok("t")
        em.cop("v", lambda e, widx=widx: e.tensor_copy(MW[:, widx:widx + 1], PB[:, 0:1]),
               waits=[tmb])
        em.cop("v", lambda e, widx=widx: e.reciprocal(WSC[:, widx:widx + 1],
                                                      MW[:, widx:widx + 1]))
        em.barrier()
        for ui, (k0, k1) in enumerate(units):
            fl = (k1 - k0) * dout
            slot = ui % 2
            tin = em.dma("d", lambda e, k0=k0, k1=k1, slot=slot, src3=src3, dout=dout:
                         e.dma_start(out=STG[:, slot, 0:(k1 - k0) * dout]
                                     .rearrange("p (a b) -> p a b", a=(k1 - k0)),
                                     in_=src3[:, k0:k1, :]),
                         waits=[em.tok("v"), em.tok("s"), em.tok("d")])
            em.cop("s", lambda e, slot=slot, fl=fl, widx=widx: e.activation(
                QSC[:, 0:fl], STG[:, slot, 0:fl], AF.Identity, scale=WSC[:, widx:widx + 1]),
                waits=[tin])
            tsc = em.tok("s")
            em.cop("v", lambda e, fl=fl: e.tensor_scalar(
                QSC[:, 0:fl], QSC[:, 0:fl], RC, RC, ALU.add, ALU.subtract), waits=[tsc])
            if dest[0] == "sbuf":
                em.cop("v", lambda e, fl=fl, dest=dest, k0=k0, k1=k1: e.tensor_scalar(
                    dest[1][:, k0:k1, :].rearrange("p a b -> p (a b)"),
                    QSC[:, 0:fl], 1.0, -1.0, ALU.min, ALU.max))
            else:
                em.cop("v", lambda e, fl=fl: e.tensor_scalar(
                    QSB[:, 0:fl], QSC[:, 0:fl], 1.0, -1.0, ALU.min, ALU.max))
                em.dma("d", lambda e, fl=fl, dest=dest, k0=k0, k1=k1: e.dma_start(
                    out=dest[1][:, k0:k1, :].rearrange("p a b -> p (a b)"),
                    in_=QSB[:, 0:fl]), waits=[em.tok("v")])
            em.barrier(["v", "s", "d"])
    if skipwq:
        tin = None
    tin = em.dma("d", lambda e: e.dma_start(out=STG[0:16, 0, 0:1024], in_=din["wgk2T"])) if not skipwq else None
    if not skipwq:
      em.cop("v", lambda e: e.reduce_sum(COLS[0:16, 40:41], STG[0:16, 0, 0:1024],
                                       axis=AX.X, apply_absolute_value=True), waits=[tin])
    if not skipwq:
      em.cop("t", lambda e: e.matmul(PA[0:1, 0:1], ONESCF[0:16, :], COLS[0:16, 40:41],
                                   start=True, stop=True), waits=[em.tok("v")])
      em.cop("v", lambda e: e.tensor_scalar(SROW[0:1, 0:1], PA[0:1, 0:1],
                                          1.0 / (16 * 1024), 1e-5, ALU.mult, ALU.max),
           waits=[em.tok("t")])
      em.cop("t", lambda e: e.matmul(PB[:, 0:1], ONESRF[:], SROW[0:1, 0:1],
                                   start=True, stop=True), waits=[em.tok("v")])
      em.cop("v", lambda e: e.tensor_copy(MW[:, 7:8], PB[:, 0:1]), waits=[em.tok("t")])
      em.cop("v", lambda e: e.reciprocal(WSC[:, 7:8], MW[:, 7:8]))
      em.cop("s", lambda e: e.activation(QSC[0:16, 0:1024], STG[0:16, 0, 0:1024],
                                       AF.Identity, scale=WSC[0:16, 7:8]), waits=[em.tok("v")])
      em.cop("v", lambda e: e.tensor_scalar(QSC[0:16, 0:1024], QSC[0:16, 0:1024],
                                          RC, RC, ALU.add, ALU.subtract),
           waits=[em.tok("s")])
      em.cop("v", lambda e: e.tensor_scalar(WGK2[:], QSC[0:16, 0:1024], 1.0, -1.0,
                                          ALU.min, ALU.max))
    em.barrier()
    em.flush()
    stup.close()
    if stage == 0:
        ph1.close()
        return nc

    XA = sb1("XA", [128, 8, T])
    XP = sb1("XP", [128, 8, T])
    U1 = sb1("U1", [128, 8, T])
    QSCR = sb1("QSCR", [128, 8, T])
    XI = sb1("XI", [128, 4, T], I32)
    XQB = sb1("XQB", [128, 8, T], BF16)
    QF = sb1("QF", [128, 8, T], BF16)
    KF = sb1("KF", [128, 8, T], BF16)
    VT = sb1("VT", [128, 2, DV], BF16)
    GFB = sb1("GFB", [128, 16, T], BF16)
    THB = sb1("THB", [128, 16, T], BF16)
    SGB = sb1("SGB", [128, 16, T], BF16)
    OGB = sb1("OGB", [128, 16, T], BF16)
    GL = sb1("GL", [128, 8, T], BF16)
    E1 = sb1("E1", [128, 8, C], BF16)
    E2 = sb1("E2", [128, 8, C], BF16)
    QS = sb1("QS", [128, 8, C], BF16)
    KS = sb1("KS", [128, 8, C], BF16)
    QI = sb1("QI", [128, 8, C], BF16)
    KX = sb1("KX", [128, 8, C], BF16)
    KXT = sb1("KXT", [128, 2, C], BF16)
    ABF = sb1("ABF", [128, 128])
    AB = sb1("AB", [128, 128], BF16)
    S16 = sb1("S16", [128, 2, 512], BF16)
    WSCR = XA
    BB3 = AP(QSCR, 0, [[8 * T, 128], [T, 8], [1, C]])
    BS3 = AP(QSCR, C, [[8 * T, 128], [T, 8], [1, C]])

    def BBs(k):
        return AP(QSCR, k * T, [[8 * T, 128], [1, C]])

    bcslot = [0]
    rdt = {}

    def bcast(j, waits=()):
        s = bcslot[0]
        bcslot[0] = 1 - s
        tk = em.cop("t", lambda e, s=s, j=j: e.matmul(
            PBC[:, s * 256:(s + 1) * 256], SELT[:, j, :],
            SCR[:].rearrange("a b c -> a (b c)"), start=True, stop=True),
            waits=list(waits) + [em.tok("v"), em.tok("s")])
        rdt["SCR_rd"] = tk
        return PBC[:, s * 256:(s + 1) * 256], tk

    def flush_sc(c0, c1, waits):
        tb = em.cop("v", lambda e: e.tensor_copy(
            TMRB[:, :, 0:32].rearrange("p a b -> p (a b)") if False else
            AP(TMRB, 0, [[256, 128], [128, 2], [1, 32]]), SC[:, :, 0:32]),
            waits=list(waits) + [rdt.get("TMRB")])
        tks = []
        for tt in range(2):
            tks.append(em.cop("t", lambda e, tt=tt: e.transpose(
                PTR[0:32, tt * 128:tt * 128 + 128],
                AP(TMRB, tt * 128, [[256, 128], [1, 32]]), IDENT[:]),
                waits=[tb, rdt.get("PTR")]))
        rdt["TMRB"] = tks[-1]
        out_tks = [em.cop("v", lambda e: e.tensor_copy(
            SCR[:].rearrange("a b c -> a (b c)"), PTR[0:32, 0:256]),
            waits=tks + [rdt.get("SCR_rd")])]
        rdt["PTR"] = out_tks[0]
        return out_tks

    def stats_cols(src_bf, src_sq, mcol, sqcol, waits):
        t0 = []
        for (src, r0) in [(src_bf, 0), (src_sq, 1)]:
            for k in range(8):
                t0.append(em.cop("t", lambda e, src=src, k=k, r0=r0: e.matmul(
                    PA[0:1, r0 * 256:(r0 + 1) * 256], ONESC[:], src[:, k, :],
                    start=(k == 0), stop=(k == 7)),
                    waits=list(waits) + [rdt.get("PA"), rdt.get("SROW")]))
        rdt["ST_MM"] = t0[-1]
        tr = em.cop("v", lambda e: e.tensor_scalar(SROW[0:1, 0:512], PA[0:1, 0:512],
                                                   1.0 / 1024.0, None, ALU.mult), waits=t0)
        rdt["PA"] = tr
        tks = []
        for (r0, col) in [(0, mcol), (1, sqcol)]:
            for tt in range(2):
                tk = em.cop("t", lambda e, r0=r0, tt=tt: e.transpose(
                    PO2[0:128, 448 + r0 * 2 + tt:449 + r0 * 2 + tt],
                    SROW[0:1, r0 * 256 + tt * 128:r0 * 256 + (tt + 1) * 128],
                    IDENTF[0:1, 0:1]), waits=[tr, rdt.get("PO2")])
                rdt["SROW"] = tk
                tks.append(em.cop("v", lambda e, col=col, tt=tt, r0=r0: e.tensor_copy(
                    SC[:, tt, col:col + 1],
                    PO2[0:128, 448 + r0 * 2 + tt:449 + r0 * 2 + tt]), waits=[tk]))
        rdt["PO2"] = tks[-1]
        return tks

    def rsqrt_cols(rstdcol, vcols, epsval):
        em.cop("v", lambda e: e.tensor_scalar(COLS[:, vcols:vcols + 2],
                                              SC[:, :, rstdcol], epsval, None, ALU.add))
        em.cop("s", lambda e: e.activation(SC[:, :, rstdcol], COLS[:, vcols:vcols + 2],
                                           AF.Ln), waits=[em.tok("v")])
        em.cop("s", lambda e: e.activation(SC[:, :, rstdcol], SC[:, :, rstdcol],
                                           AF.Exp, scale=-0.5))
        em.cop("v", lambda e: e.tensor_tensor(COLS[:, vcols + 2:vcols + 4],
                                              SC[:, :, rstdcol], SC[:, :, rstdcol],
                                              ALU.mult), waits=[em.tok("s")])
        em.cop("v", lambda e: e.tensor_tensor(COLS[:, vcols + 2:vcols + 4],
                                              COLS[:, vcols + 2:vcols + 4],
                                              COLS[:, vcols:vcols + 2], ALU.mult))
        em.cop("v", lambda e: e.tensor_scalar(COLS[:, vcols + 2:vcols + 4],
                                              COLS[:, vcols + 2:vcols + 4],
                                              -0.5, 1.5, ALU.mult, ALU.add))
        em.cop("v", lambda e: e.tensor_tensor(SC[:, :, rstdcol], SC[:, :, rstdcol],
                                              COLS[:, vcols + 2:vcols + 4], ALU.mult))
        return em.tok("v")

    def rstd_col(mcol, sqcol, rstdcol, epscol, waits):
        for tt in range(2):
            em.cop("v", lambda e, tt=tt: e.tensor_mul(COLS[:, 44:45],
                                                      SC[:, tt, mcol:mcol + 1],
                                                      SC[:, tt, mcol:mcol + 1]), waits=waits)
            em.cop("v", lambda e, tt=tt: e.tensor_sub(SC[:, tt, rstdcol:rstdcol + 1],
                                                      SC[:, tt, sqcol:sqcol + 1],
                                                      COLS[:, 44:45]))
        return rsqrt_cols(rstdcol, 40, 1e-5)

    def rowmax_col(w_ap, ndt, col, waits):
        t1 = em.cop("v", lambda e, w=w_ap: e.reduce_max(
            TMRB[:].rearrange("p a b -> p (a b)"), w.rearrange("p a b -> p b a"),
            axis=AX.X, apply_absolute_value=True),
            waits=list(waits) + [rdt.get("TMRB")])
        tks = []
        for tt in range(2):
            tk = em.cop("t", lambda e, tt=tt: e.transpose(
                PTR[:, tt * 128:(tt + 1) * 128], TMRB[:, tt, :], IDENT[:]),
                waits=[t1, rdt.get("PTR")])
            rdt["TMRB"] = tk
            tks.append(em.cop("v", lambda e, tt=tt, col=col: e.reduce_max(
                SC[:, tt, col:col + 1], PTR[:, tt * 128:(tt + 1) * 128], axis=AX.X),
                waits=[tk]))
        rdt["PTR"] = tks[-1]
        return tks

    def emit_ffn(rstd_sb, x_sb, out_t, ubf, waits):
        t0 = []
        for k in range(8):
            t0.append(em.cop("t", lambda e, k=k, ubf=ubf: e.matmul(
                PO1[0:64, 0:256], WIN[:, k, :], ubf[:, k, :], start=(k == 0),
                stop=(k == 7)), waits=list(waits) + [rdt.get("PO1")]))
        ts_ = em.cop("v", lambda e, r=rstd_sb: e.tensor_tensor(
            FS[:], PO1[0:64, 0:256], r, ALU.mult), waits=t0)
        rdt["PO1"] = ts_
        em.mb()
        trep = []
        for kk in range(8):
            pd = PA if kk < 4 else PB
            off = (kk % 4) * 256
            trep.append(em.cop("t", lambda e, kk=kk, pd=pd, off=off: e.matmul(
                pd[:, off:off + 256], REP[:, kk, :], FS[:], start=True, stop=True),
                waits=[ts_, rdt.get("PA" if kk < 4 else "PB")]))
        em.mb()
        if dbg:
            if not hasattr(em, "_dumped_pa"):
                em._dumped_pa = True
                em.cop("v", lambda e: e.tensor_copy(QSCR[:, 0:4, :].rearrange("p a b -> p (a b)"), PA[:, 0:1024]))
                em.barrier()
                em.dma("d", lambda e: e.dma_start(out=dbg_out["d_pa"], in_=QSCR[:, 0:4, :].rearrange("p a b -> p (a b)")))
                em.dma("d", lambda e: e.dma_start(out=dbg_out["d_rep"],
                                                  in_=REPF[:].rearrange("p a b -> p (a b)")))
                em.barrier()
        trl = []
        for kk in range(8):
            pd = PA if kk < 4 else PB
            off = (kk % 4) * 256
            trl.append(em.cop("s", lambda e, kk=kk, pd=pd, off=off: e.activation(
                FH[:, kk, :], pd[:, off:off + 256], AF.Relu, bias=B1E[:, kk:kk + 1]),
                waits=trep))
        rdt["PA"] = rdt["PB"] = trl[-1]
        em.mb()
        tsum = []
        for kk in range(8):
            tsum.append(em.cop("t", lambda e, kk=kk: e.matmul(
                PO1[0:64, 256:512], SUM2[:, kk, :], FH[:, kk, :],
                start=(kk == 0), stop=(kk == 7)), waits=trl + [rdt.get("PO1")]))
        em.mb()
        tg = em.cop("s", lambda e: e.activation(FG[:], PO1[0:64, 256:512], AF.Identity,
                                                bias=B2[:]), waits=tsum)
        rdt["PO1"] = tg
        em.mb()
        tw = []
        for m in range(8):
            pd = PA if m < 4 else PB
            off = (m % 4) * 256
            tw.append(em.cop("t", lambda e, m=m, pd=pd, off=off: e.matmul(
                pd[:, off:off + 256], WOUT[:, m * 128:(m + 1) * 128], FG[:],
                start=True, stop=False), waits=[tg]))
            tw.append(em.cop("t", lambda e, m=m, pd=pd, off=off: e.matmul(
                pd[:, off:off + 256], BOUTR[:, m, :], ONESR2[:], start=False, stop=True)))
        em.mb()
        outs = []
        for half, pd in [(0, PA), (1, PB)]:
            outs.append(em.cop("v", lambda e, half=half, pd=pd, x_sb=x_sb, on=out_t:
                               e.scalar_tensor_tensor(
                on[:, half * 4:(half + 1) * 4, :].rearrange("p a b -> p (a b)"),
                pd[:, 0:1024], 0.5,
                x_sb[:, half * 4:(half + 1) * 4, :].rearrange("p a b -> p (a b)"),
                ALU.mult, ALU.add), waits=tw))
        rdt["PA"], rdt["PB"] = outs[0], outs[1]
        return outs

    def ln_to_ubf(x_sb, mcol, sqcol, rstdcol, bfb, sqb, ubf, waits=()):
        em.cop("v", lambda e, x=x_sb, bfb=bfb: e.tensor_copy(
            bfb[:].rearrange("p a b -> p (a b)"), x[:].rearrange("p a b -> p (a b)")),
            waits=list(waits) + [rdt.get("ST_MM")])
        em.cop("s", lambda e, bfb=bfb, sqb=sqb: e.activation(
            sqb[:].rearrange("p a b -> p (a b)"), bfb[:].rearrange("p a b -> p (a b)"),
            AF.Square), waits=[em.tok("v"), rdt.get("XQB")])
        em.mb()
        if lncut < 2:
            return
        st_t = stats_cols(bfb, sqb, mcol, sqcol, [em.tok("s")])
        if lncut < 3:
            return
        rs_t = rstd_col(mcol, sqcol, rstdcol, CEPS5, st_t)
        if lncut < 4:
            return
        fl = flush_sc(mcol, mcol + 1, st_t) + flush_sc(rstdcol, rstdcol + 1, [rs_t])
        em.mb()
        if lncut < 5:
            return
        mb_, tkm = bcast(mcol, fl)
        if lncut < 6:
            return
        em.cop("v", lambda e, x=x_sb, mb_=mb_, ubf=ubf: e.tensor_tensor(
            ubf[:].rearrange("p a b -> p (a b)"), x[:].rearrange("p a b -> p (a b)"),
            AP(mb_.tensor, mb_.offset, [mb_.ap[0], [0, 8], mb_.ap[1]]), ALU.subtract),
            waits=[tkm])
        rb, tkr = bcast(rstdcol, fl)
        em.cop("v", lambda e, rb=rb: e.tensor_copy(MBC[:, 0, :], rb), waits=[tkr])
        em.mb()

    NW = {0: NQ, 1: NK, 2: NV, 3: NG, 4: NGK1}
    ICTOT = [0]
    t_xa_free = None
    em.pipe = True
    for c in range(nchunks):
        txa = em.dma("d", lambda e, c=c: e.dma_start(
            out=XA[:],
            in_=xT.rearrange("(k p) t -> p k t", p=128)[:, :, c * T:(c + 1) * T]),
            waits=[t_xa_free] if t_xa_free else [])
        em.mb()
        if p1cut < -2:
            em.barrier()
            continue
        ln_to_ubf(XA, 0, 1, 2, QF, XQB, KF, waits=[txa])
        if p1cut < -1:
            em.barrier()
            continue
        emit_ffn(MBC[0:64, 0, :], XA, XP, KF, [em.tok("v")])
        em.mb()
        if dbg and c == 0:
            em.dma("d", lambda e: e.dma_start(out=dbg_out["d_ub"], in_=KF[:].rearrange("p a b -> p (a b)")))
            em.dma("d", lambda e: e.dma_start(out=dbg_out["d_fs"], in_=FSF[:]))
            em.dma("d", lambda e: e.dma_start(out=dbg_out["d_fh"], in_=FH[:].rearrange("p a b -> p (a b)")))
            em.dma("d", lambda e: e.dma_start(out=dbg_out["d_fg"], in_=FGF[:]))
            em.dma("d", lambda e: e.dma_start(out=dbg_out["d_sc"], in_=SC[:].rearrange("p a b -> p (a b)")))
            em.dma("d", lambda e: e.dma_start(out=dbg_out["d_mbc"], in_=MBC[:].rearrange("p a b -> p (a b)")))
            em.barrier()
        if p1cut < 0:
            em.barrier()
            continue
        em.dma("d", lambda e, c=c: e.dma_start(
            out=xp_st[c], in_=XP[:].rearrange("p a b -> p (a b)")), waits=[em.tok("v")])
        if p1cut < 1:
            em.barrier()
            continue
        em.cop("v", lambda e: e.tensor_copy(QF[:].rearrange("p a b -> p (a b)"),
                                            XP[:].rearrange("p a b -> p (a b)")),
               waits=[rdt.get("ST_MM")])
        em.cop("s", lambda e: e.activation(XQB[:].rearrange("p a b -> p (a b)"),
                                           QF[:].rearrange("p a b -> p (a b)"), AF.Square),
               waits=[em.tok("v"), rdt.get("XQB"), rdt.get("ST_MM")])
        em.mb()
        st_t = stats_cols(QF, XQB, 4, 5, [em.tok("s")])
        rs_t = rstd_col(4, 5, 6, CEPS5, st_t)
        for tt in range(2):
            em.cop("v", lambda e, tt=tt: e.tensor_mul(COLS[:, 44:45], SC[:, tt, 4:5],
                                                      SC[:, tt, 4:5]), waits=[rs_t])
            em.cop("v", lambda e, tt=tt: e.tensor_sub(COLS[:, 45:46], SC[:, tt, 5:6],
                                                      COLS[:, 44:45]))
            em.cop("v", lambda e, tt=tt: e.tensor_scalar(COLS[:, 46:47], COLS[:, 45:46],
                                                         1e-5, None, ALU.add))
            em.cop("v", lambda e, tt=tt: e.reciprocal(COLS[:, 47:48], COLS[:, 46:47]))
            em.cop("v", lambda e, tt=tt: e.tensor_mul(SC[:, tt, 28:29], COLS[:, 47:48],
                                                      COLS[:, 45:46]))
        rsqrt_cols(28, 40, 1e-8)
        for tt in range(2):
            em.cop("v", lambda e, tt=tt: e.tensor_mul(SC[:, tt, 7:8], SC[:, tt, 6:7],
                                                      SC[:, tt, 28:29]))
        fl2 = flush_sc(4, 5, st_t)
        em.mb()
        m2b, tk2 = bcast(4, fl2)
        em.cop("v", lambda e, m2b=m2b: e.tensor_tensor(
            U1[:].rearrange("p a b -> p (a b)"), XP[:].rearrange("p a b -> p (a b)"),
            AP(m2b.tensor, m2b.offset, [m2b.ap[0], [0, 8], m2b.ap[1]]), ALU.subtract),
            waits=[tk2])
        em.mb()

        if p1cut < 2:
            em.barrier()
            continue
        for si in range(5):
            nten = NW[si]
            tw = em.cop("v", lambda e, nten=nten: e.tensor_tensor(
                WSCR[:].rearrange("p a b -> p (a b)"),
                U1[:].rearrange("p a b -> p (a b)"),
                AP(nten, 0, [[8, 128], [1, 8], [0, T]]), ALU.mult))
            rowmax_col(WSCR[:], 8, 8 + si, [tw])
            em.mb()
        for tt in range(2):
            em.cop("v", lambda e, tt=tt: e.scalar_tensor_tensor(
                SC[:, tt, 13:18], SC[:, tt, 8:13], 1.0,
                AP(SC, tt * 32 + 7, [[64, 128], [0, 5]]), ALU.mult, ALU.mult))
            em.cop("v", lambda e, tt=tt: e.tensor_scalar(SC[:, tt, 13:18],
                                                         SC[:, tt, 13:18],
                                                         1e-5, None, ALU.max))
            em.cop("v", lambda e, tt=tt: e.reciprocal(SC[:, tt, 18:23], SC[:, tt, 13:18]))
            em.cop("v", lambda e, tt=tt: e.scalar_tensor_tensor(
                SC[:, tt, 8:13], SC[:, tt, 18:23], 127.0,
                AP(SC, tt * 32 + 7, [[64, 128], [0, 5]]), ALU.mult, ALU.mult))
            em.cop("v", lambda e, tt=tt: e.scalar_tensor_tensor(
                SC[:, tt, 23:26], SC[:, tt, 14:17], 1.0 / 127.0, MW[:, 1:4],
                ALU.mult, ALU.mult))
        fl3 = flush_sc(8, 13, [em.tok("v")]) + flush_sc(23, 26, [em.tok("v")])
        em.mb()

        def quantize(si):
            nten = NW[si]
            phib, tkb = bcast(8 + si, fl3)
            em.cop("v", lambda e, nten=nten: e.tensor_tensor(
                QSCR[:].rearrange("p a b -> p (a b)"),
                U1[:].rearrange("p a b -> p (a b)"),
                AP(nten, 0, [[8, 128], [1, 8], [0, T]]), ALU.mult))
            em.cop("v", lambda e, phib=phib: e.tensor_tensor(
                QSCR[:].rearrange("p a b -> p (a b)"),
                QSCR[:].rearrange("p a b -> p (a b)"),
                AP(phib.tensor, phib.offset, [phib.ap[0], [0, 8], phib.ap[1]]),
                ALU.mult), waits=[tkb])
            for hf in range(2):
                em.cop("v", lambda e, hf=hf: e.tensor_scalar(
                    XI[:].rearrange("p a b -> p (a b)"),
                    QSCR[:, hf * 4:(hf + 1) * 4, :].rearrange("p a b -> p (a b)"),
                    RC, RC, ALU.add, ALU.subtract), waits=[em.tok("s")])
                em.cop("s", lambda e, hf=hf: e.activation(
                    XQB[:, hf * 4:(hf + 1) * 4, :].rearrange("p a b -> p (a b)"),
                    XI[:].rearrange("p a b -> p (a b)"), AF.Copy),
                    waits=[em.tok("v"), rdt.get("XQB"), rdt.get("ST_MM")])
            return em.tok("s")

        if p1cut < 3:
            em.barrier()
            continue
        tq = quantize(0)
        tqm = []
        for m in range(8):
            pd = PA if m < 4 else PB
            off = (m % 4) * 256
            for k in range(8):
                tqm.append(em.cop("t", lambda e, m=m, k=k, pd=pd, off=off: e.matmul(
                    pd[:, off:off + 256], WQ[:, k, m * 128:(m + 1) * 128], XQB[:, k, :],
                    start=(k == 0), stop=(k == 7)),
                    waits=[tq, rdt.get("PA" if m < 4 else "PB")]))
        for half, pd in [(0, PA), (1, PB)]:
            rdt["PA" if half == 0 else "PB"] = em.cop(
                "s", lambda e, half=half, pd=pd: e.activation(
                    QF[:, half * 4:(half + 1) * 4, :].rearrange("p a b -> p (a b)"),
                    pd[:, 0:1024], AF.Copy), waits=tqm)
        rdt["XQB"] = tqm[-1]
        em.mb()
        if p1cut < 4:
            em.barrier()
            continue
        tq = quantize(1)
        tkm = []
        for m in range(8):
            pd = PA if m < 4 else PB
            off = (m % 4) * 256
            for k in range(8):
                tkm.append(em.cop("t", lambda e, m=m, k=k, pd=pd, off=off: e.matmul(
                    pd[:, off:off + 256], WK[:, k, m * 128:(m + 1) * 128], XQB[:, k, :],
                    start=(k == 0), stop=(k == 7)),
                    waits=[tq, rdt.get("PA" if m < 4 else "PB")]))
        for half, pd in [(0, PA), (1, PB)]:
            rdt["PA" if half == 0 else "PB"] = em.cop(
                "s", lambda e, half=half, pd=pd: e.activation(
                    KF[:, half * 4:(half + 1) * 4, :].rearrange("p a b -> p (a b)"),
                    pd[:, 0:1024], AF.Copy), waits=tkm)
        rdt["XQB"] = tkm[-1]
        em.mb()
        if p1cut < 5:
            em.barrier()
            continue
        tq = quantize(2)
        for u in range(8):
            wslot = u % 2
            twl = em.dma("pd", lambda e, u=u, wslot=wslot: e.dma_start(
                out=WST[:, wslot, :, :],
                in_=wvq_st[:, :, u * 256:(u + 1) * 256]),
                waits=[em.tok("t")])
            for tt in range(2):
                pd = [PA, PB][tt]
                pdn = "PA" if tt == 0 else "PB"
                tvm = []
                for k in range(8):
                    tvm.append(em.cop("t", lambda e, tt=tt, k=k, pd=pd, wslot=wslot:
                                      e.matmul(pd[:, 0:256],
                                               XQB[:, k, tt * 128:(tt + 1) * 128],
                                               WST[:, wslot, k, :],
                                               start=(k == 0), stop=(k == 7)),
                                      waits=[tq, twl, rdt.get(pdn)]))
                rdt[pdn] = em.cop("s", lambda e, tt=tt, u=u, pd=pd: e.activation(
                    VT[:, tt, u * 256:(u + 1) * 256], pd[:, 0:256], AF.Identity,
                    scale=SC[:, tt, 24:25]), waits=tvm)
                rdt["XQB"] = tvm[-1]
            em.mb()
        if p1cut < 6:
            em.barrier()
            continue
        tq = quantize(4)
        ti1 = []
        for k in range(8):
            ti1.append(em.cop("t", lambda e, k=k: e.matmul(
                PO1[0:16, 0:256], WGK1[:, k, :], XQB[:, k, :],
                start=(k == 0), stop=(k == 7)), waits=[tq, rdt.get("PO1")]))
        rdt["PO1"] = em.cop("v", lambda e: e.tensor_copy(I1[:], PO1[0:16, 0:256]),
                            waits=ti1)
        em.cop("v", lambda e: e.tensor_copy(I1B[:], I1[:]))
        rdt["XQB"] = ti1[-1]
        em.mb()
        if p1cut < 7:
            em.barrier()
            continue
        tq = quantize(3)
        psigb, tkg = bcast(25, fl3)
        em.cop("v", lambda e, psigb=psigb: e.tensor_copy(MBC[:, 1, :], psigb),
               waits=[tkg])
        for u in range(8):
            wslot = u % 2
            twl = em.dma("pd", lambda e, u=u, wslot=wslot: e.dma_start(
                out=WST[:, wslot, :, :],
                in_=wgq_st[:, :, u * 256:(u + 1) * 256]),
                waits=[em.tok("t")])
            for mm in range(2):
                m = u * 2 + mm
                pd = [PA, PB][mm]
                pdn = "PA" if mm == 0 else "PB"
                tgm = []
                for k in range(8):
                    tgm.append(em.cop("t", lambda e, mm=mm, k=k, pd=pd, wslot=wslot:
                                      e.matmul(pd[:, 0:256],
                                               WST[:, wslot, k, mm * 128:(mm + 1) * 128],
                                               XQB[:, k, :],
                                               start=(k == 0), stop=(k == 7)),
                                      waits=[tq, twl, rdt.get(pdn)]))
                rdt["XQB"] = tgm[-1]
                tgm.append(em.cop("v", lambda e, m=m, pd=pd: e.tensor_tensor(
                    GFB[:, m, :], pd[:, 0:256], MBC[:, 1, :], ALU.mult), waits=tgm))
                rdt[pdn] = tgm[-1]
            em.mb()
        if p1cut < 8:
            em.barrier()
            continue
        em.cop("s", lambda e: e.activation(SQ16[:], I1B[:], AF.Square),
               waits=[em.tok("v")])
        t0 = [em.cop("t", lambda e: e.matmul(PA[0:1, 0:256], ONESC[0:16, :], SQ16[:],
                                             start=True, stop=True),
                     waits=[em.tok("s"), rdt.get("PA")])]
        tr = em.cop("v", lambda e: e.tensor_scalar(SROW[0:1, 0:256], PA[0:1, 0:256],
                                                   1.0 / 16.0, None, ALU.mult), waits=t0)
        rdt["PA"] = tr
        if g2cut < 2:
            em.barrier()
            continue
        for tt in range(2):
            tk = em.cop("t", lambda e, tt=tt: e.transpose(
                PO2[0:128, 448 + tt:449 + tt], SROW[0:1, tt * 128:(tt + 1) * 128],
                IDENTF[0:1, 0:1]), waits=[tr, rdt.get("PO2")])
            rdt["SROW"] = tk
            rdt["PO2"] = em.cop("v", lambda e, tt=tt: e.tensor_copy(
                SC[:, tt, 26:27], PO2[0:128, 448 + tt:449 + tt]), waits=[tk])
        em.cop("s", lambda e: e.activation(SC[:, :, 26], SC[:, :, 26], AF.Ln,
                                           bias=CEPS8[:]), waits=[em.tok("v")])
        em.cop("s", lambda e: e.activation(SC[:, :, 26], SC[:, :, 26], AF.Exp, scale=-0.5))
        if g2cut < 3:
            em.barrier()
            continue
        em.cop("v", lambda e: e.tensor_scalar(I1[:], I1[:], NGK2[:], None, ALU.mult),
               waits=[em.tok("s")])
        em.cop("v", lambda e: e.tensor_copy(AP(TMRB, 0, [[256, 16], [1, 256]]), I1[:]),
               waits=[rdt.get("TMRB")])
        for tt in range(2):
            tk = em.cop("t", lambda e, tt=tt: e.transpose(
                PTR[:, tt * 128:tt * 128 + 16],
                AP(TMRB, tt * 128, [[256, 16], [1, 128]]), IDENT[0:16, 0:16]),
                waits=[em.tok("v"), rdt.get("PTR")])
            rdt["TMRB"] = tk
            rdt["PTR"] = em.cop("v", lambda e, tt=tt: e.reduce_max(
                SC[:, tt, 27:28], PTR[:, tt * 128:tt * 128 + 16], axis=AX.X,
                apply_absolute_value=True), waits=[tk])
        if g2cut < 4:
            em.barrier()
            continue
        for tt in range(2):
            em.cop("v", lambda e, tt=tt: e.tensor_mul(SC[:, tt, 27:28], SC[:, tt, 27:28],
                                                      SC[:, tt, 26:27]))
            em.cop("v", lambda e, tt=tt: e.tensor_scalar(SC[:, tt, 27:28],
                                                         SC[:, tt, 27:28],
                                                         1e-5, None, ALU.max))
            em.cop("v", lambda e, tt=tt: e.reciprocal(COLS[:, 44:45], SC[:, tt, 27:28]))
            em.cop("v", lambda e, tt=tt: e.scalar_tensor_tensor(
                SC[:, tt, 19:20], COLS[:, 44:45], 127.0, SC[:, tt, 26:27],
                ALU.mult, ALU.mult))
            em.cop("v", lambda e, tt=tt: e.scalar_tensor_tensor(
                SC[:, tt, 20:21], SC[:, tt, 27:28], 1.0 / 127.0, MW[:, 7:8],
                ALU.mult, ALU.mult))
        fl4 = flush_sc(19, 21, [em.tok("v")])
        em.mb()
        if g2cut < 5:
            em.barrier()
            continue
        phi2b, tka = bcast(19, fl4)
        em.cop("v", lambda e, phi2b=phi2b: e.tensor_tensor(I1[:], I1[:], phi2b[0:16, :],
                                                           ALU.mult), waits=[tka])
        em.cop("v", lambda e: e.tensor_scalar(XI[0:16, 0, 0:256], I1[:], RC, RC,
                                              ALU.add, ALU.subtract),
               waits=[em.tok("s")])
        em.cop("s", lambda e: e.activation(I1B[:], XI[0:16, 0, 0:256], AF.Copy),
               waits=[em.tok("v")])
        if g2cut < 6:
            em.barrier()
            continue
        psi2b, tkb2 = bcast(20, fl4)
        em.cop("v", lambda e, psi2b=psi2b: e.tensor_copy(MBC[:, 1, :], psi2b),
               waits=[tkb2])
        tz = []
        for m in range(8):
            pd = PA if m < 4 else PB
            off = (m % 4) * 256
            if g2cut < 7:
                continue
            tz.append(em.cop("t", lambda e, m=m, pd=pd, off=off: e.matmul(
                pd[:, off:off + 256], WGK2[:, m * 128:(m + 1) * 128], I1B[:],
                start=True, stop=True),
                waits=[em.tok("s"), rdt.get("PA" if m < 4 else "PB")]))
        for m in range(8):
            pd = PA if m < 4 else PB
            off = (m % 4) * 256
            if g2cut < 8:
                continue
            tz.append(em.cop("v", lambda e, m=m, pd=pd, off=off: e.tensor_tensor(
                U1[:, m, :], pd[:, off:off + 256], MBC[:, 1, :], ALU.mult), waits=tz))
            rdt["PA" if m < 4 else "PB"] = tz[-1]
            if g2cut < 9:
                continue
            tz.append(em.cop("p", lambda e, m=m: e.tensor_scalar(
                U1[:, m, :], U1[:, m, :], BGK2[:, m:m + 1], None, ALU.add), waits=tz))
        em.mb()
        if p1cut < 9:
            em.barrier()
            continue
        em.cop("s", lambda e: e.activation(WSCR[:].rearrange("p a b -> p (a b)"),
                                           U1[:].rearrange("p a b -> p (a b)"),
                                           AF.Abs), waits=[em.tok("p"), em.tok("v")])
        em.cop("s", lambda e: e.activation(WSCR[:].rearrange("p a b -> p (a b)"),
                                           WSCR[:].rearrange("p a b -> p (a b)"),
                                           AF.Exp, scale=-1.0))
        em.cop("s", lambda e: e.activation(WSCR[:].rearrange("p a b -> p (a b)"),
                                           WSCR[:].rearrange("p a b -> p (a b)"),
                                           AF.Ln, bias=1.0))
        em.cop("v", lambda e: e.tensor_scalar(QSCR[:].rearrange("p a b -> p (a b)"),
                                              U1[:].rearrange("p a b -> p (a b)"),
                                              0.0, 1.0 / 16.0, ALU.min, ALU.mult),
               waits=[em.tok("p")])
        em.cop("v", lambda e: e.scalar_tensor_tensor(
            GL[:].rearrange("p a b -> p (a b)"),
            WSCR[:].rearrange("p a b -> p (a b)"), -1.0 / 16.0,
            QSCR[:].rearrange("p a b -> p (a b)"), ALU.mult, ALU.add),
            waits=[em.tok("s")])
        t_xa_free = em.tok("v")
        em.mb()
        if p1cut < 10:
            em.barrier()
            continue
        em.cop("s", lambda e: e.activation(THB[:].rearrange("p a b -> p (a b)"),
                                           GFB[:].rearrange("p a b -> p (a b)"),
                                           AF.Tanh, scale=0.5), waits=[em.tok("v")])
        em.cop("p", lambda e: e.tensor_scalar(THB[:].rearrange("p a b -> p (a b)"),
                                              THB[:].rearrange("p a b -> p (a b)"),
                                              0.5, 0.5, ALU.mult, ALU.add),
               waits=[em.tok("s")])
        em.cop("v", lambda e: e.tensor_tensor(SGB[:].rearrange("p a b -> p (a b)"),
                                              THB[:].rearrange("p a b -> p (a b)"),
                                              GFB[:].rearrange("p a b -> p (a b)"),
                                              ALU.mult), waits=[em.tok("p")])
        em.mb()

        if p1cut < 11:
            em.barrier()
            continue
        for ic in range(icpc):
            gic = ICTOT[0]
            ICTOT[0] += 1
            tsl = slice(ic * C, (ic + 1) * C)
            tsc = []
            for k in range(8):
                tsc.append(em.cop("v", lambda e, k=k, tsl=tsl: e.tensor_tensor_scan(
                    BBs(k), GL[:, k, tsl], ZEROT[:], 0.0, ALU.add, ALU.bypass)))
            em.cop("v", lambda e: e.tensor_copy(
                COLS[:, 0:8], AP(QSCR, C - 1, [[8 * T, 128], [T, 8]])),
                waits=tsc + [rdt.get("COLS")])
            em.cop("v", lambda e: e.tensor_copy(
                COLS[:, 8:16], AP(QSCR, C // 2 - 1, [[8 * T, 128], [T, 8]])))
            em.cop("v", lambda e: e.tensor_scalar(COLS[:, 16:24], COLS[:, 8:16],
                                                  -1.0, None, ALU.mult))
            em.cop("v", lambda e, gic=gic: e.tensor_copy(
                AP(LTAB, gic % 32, [[8 * 32, 128], [32, 8]]), COLS[:, 24:32]))
            em.cop("v", lambda e: e.tensor_add(COLS[:, 24:32], COLS[:, 24:32],
                                               COLS[:, 0:8]))
            em.cop("v", lambda e: e.tensor_sub(COLS[:, 32:40], COLS[:, 0:8],
                                               COLS[:, 8:16]))
            em.cop("s", lambda e: e.activation(ELT[:], COLS[:, 0:8], AF.Exp),
                   waits=[em.tok("v"), rdt.get("ELT")])
            em.cop("s", lambda e: e.activation(COLS[:, 8:16], COLS[:, 8:16], AF.Exp))
            em.cop("s", lambda e: e.activation(COLS[:, 32:40], COLS[:, 32:40], AF.Exp))
            rdt["COLS"] = em.tok("s")
            em.cop("v", lambda e: e.tensor_tensor(
                BS3, BB3, AP(COLS, 16, [[48, 128], [1, 8], [0, C]]), ALU.add))
            em.cop("s", lambda e: e.activation(E1[:], BS3, AF.Exp),
                   waits=[em.tok("v")])
            em.cop("s", lambda e: e.activation(E2[:], BS3, AF.Exp, scale=-1.0))
            em.mb()
            em.cop("v", lambda e, tsl=tsl: e.tensor_tensor(
                QS[:], QF[:, :, tsl], E1[:], ALU.mult),
                waits=[em.tok("s"), rdt.get("GLA_T"), rdt.get("QI")])
            em.cop("v", lambda e, tsl=tsl: e.tensor_tensor(
                KS[:], KF[:, :, tsl], E2[:], ALU.mult))
            em.cop("v", lambda e: e.tensor_tensor(
                QI[:].rearrange("p a b -> p (a b)"),
                QS[:].rearrange("p a b -> p (a b)"),
                AP(COLS, 8, [[48, 128], [1, 8], [0, C]]), ALU.mult))
            em.cop("v", lambda e: e.tensor_tensor(
                KX[:].rearrange("p a b -> p (a b)"),
                KS[:].rearrange("p a b -> p (a b)"),
                AP(COLS, 32, [[48, 128], [1, 8], [0, C]]), ALU.mult))
            tqs = em.tok("v")
            em.mb()
            rdt["QI"] = em.dma("d", lambda e, gic=gic: e.dma_start(
                out=qt_st[gic], in_=QI[:].rearrange("p a b -> p (a b)")),
                waits=[tqs])
            for h in range(H):
                po = [PO1, PO2][h % 2]
                pon = "PO1" if h % 2 == 0 else "PO2"
                ts16 = em.cop("p", lambda e, h=h: e.tensor_copy(
                    S16[:].rearrange("p a b -> p (a b)"),
                    S32[:, 2 * h:2 * h + 2, :].rearrange("p a b -> p (a b)")),
                    waits=[rdt.get("S16"), rdt.get(("S32", h))])
                ta = []
                for kk in range(2):
                    ta.append(em.cop("t", lambda e, h=h, kk=kk: e.matmul(
                        PA[:, 0:128], KS[:, 2 * h + kk, :], QS[:, 2 * h + kk, :],
                        start=(kk == 0), stop=(kk == 1)),
                        waits=[rdt.get("PA"), tqs]))
                rdt["PA"] = em.cop("v", lambda e: e.tensor_tensor(
                    ABF[:], PA[:, 0:128], TRIU[:], ALU.mult), waits=ta)
                em.cop("v", lambda e, ic=ic: e.tensor_scalar(
                    AB[:], ABF[:], SC[:, ic, 23:24], None, ALU.mult))
                tab = em.tok("v")
                to = []
                for j in range(4):
                    to.append(em.cop("t", lambda e, h=h, j=j, ic=ic, po=po: e.matmul(
                        po[:, j * 128:(j + 1) * 128],
                        VT[:, ic, h * 512 + j * 128:h * 512 + (j + 1) * 128],
                        AB[:], start=True, stop=False),
                        waits=[tab, ts16, rdt.get(pon)]))
                    for kk in range(2):
                        to.append(em.cop("t", lambda e, h=h, j=j, kk=kk, po=po: e.matmul(
                            po[:, j * 128:(j + 1) * 128],
                            S16[:, kk, j * 128:(j + 1) * 128],
                            QI[:, 2 * h + kk, :], start=False, stop=(kk == 1))))
                rdt["S16"] = to[-1]
                rdt[pon] = em.cop("v", lambda e, h=h, po=po, tsl=tsl: e.tensor_tensor(
                    OGB[:, 4 * h:4 * h + 4, tsl],
                    po[:, 0:512].rearrange("p (a b) -> p a b", a=4),
                    SGB[:, 4 * h:4 * h + 4, tsl], ALU.mult), waits=to)
                tkt = []
                for kk in range(2):
                    tkt.append(em.cop("t", lambda e, h=h, kk=kk: e.transpose(
                        PTR[:, kk * 128:(kk + 1) * 128], KX[:, 2 * h + kk, :],
                        IDENT[:]), waits=[rdt.get("PTR")]))
                for kk in range(2):
                    tkt.append(em.cop("v", lambda e, kk=kk, ic=ic: e.tensor_scalar(
                        KXT[:, kk, :], PTR[:, kk * 128:(kk + 1) * 128],
                        SC[:, ic, 23:24], None, ALU.mult), waits=tkt))
                rdt["PTR"] = tkt[-1]
                tu = []
                for kk in range(2):
                    tu.append(em.cop("t", lambda e, h=h, kk=kk, ic=ic: e.matmul(
                        PB[:, kk * 512:(kk + 1) * 512], KXT[:, kk, :],
                        VT[:, ic, h * 512:(h + 1) * 512], start=True, stop=True),
                        waits=tkt + [rdt.get("PB")]))
                rdt["GLA_T"] = tu[-1]
                for kk in range(2):
                    rdt["PB"] = rdt[("S32", h)] = rdt["ELT"] = em.cop(
                        "v", lambda e, h=h, kk=kk: e.scalar_tensor_tensor(
                            S32[:, 2 * h + kk, :], S32[:, 2 * h + kk, :],
                            ELT[:, 2 * h + kk:2 * h + kk + 1],
                            PB[:, kk * 512:(kk + 1) * 512], ALU.mult, ALU.add),
                        waits=tu + [ts16])
                em.mb()
        em.dma("d", lambda e, c=c: e.dma_start(
            out=og_st[c], in_=OGB[:].rearrange("p a b -> p (a b)")),
            waits=[em.tok("v")])
        em.dma("d", lambda e, c=c: e.dma_start(
            out=sg_st[c], in_=SGB[:].rearrange("p a b -> p (a b)")))
        if dbg and c == 0:
            for nm, src in [("d_xp", XP), ("d_q", QF), ("d_k", KF), ("d_gl", GL),
                            ("d_og", OGB), ("d_v", VT)]:
                em.dma("d", lambda e, nm=nm, src=src: e.dma_start(
                    out=dbg_out[nm], in_=src[:].rearrange("p a b -> p (a b)")))
        em.barrier()
    em.pipe = False

    if stage == 1:
        em.flush()
        ph1.close()
        return nc

    em.dma("d", lambda e: e.dma_start(
        out=AP(cc_in, 0, [[512, 128], [512 * 128, 8], [1, 512]]), in_=S32[:]))
    em.barrier()
    em.op("p", lambda e: e.collective_compute(
        "AllGather", ALU.bypass, replica_groups=[[0, 1], [2, 3], [4, 5], [6, 7]],
        ins=[cc_in.ap().opt()], outs=[cc_out.ap().opt()]), inc=1)
    em.barrier()
    em.dma("d", lambda e: e.dma_start(
        out=S32[:], in_=AP(cc_out, 0, [[512, 128], [512 * 128, 8], [1, 512]])))
    em.barrier()
    em.cop("v", lambda e: e.tensor_scalar(S32[:].rearrange("p a b -> p (a b)"),
                                          S32[:].rearrange("p a b -> p (a b)"),
                                          SMASK[:], None, ALU.mult))
    if dbg:
        em.dma("d", lambda e: e.dma_start(out=dbg_out["d_s32"],
                                          in_=S32[:].rearrange("p a b -> p (a b)")),
               waits=[em.tok("v")])
    em.barrier()
    em.flush()
    ph1.close()
    if stage == 2:
        return nc

    ph3 = contextlib.ExitStack()
    sb3 = lambda n, s, d=F32: ph3.enter_context(nc.sbuf_tensor(n, s, d))
    WO = sb3("WO", [128, 16, D], BF16)
    OF = sb3("OF", [128, 16, T])
    TMQ = sb3("TMQ", [128, 16, T])
    OQ = sb3("OQ", [128, 16, T], BF16)
    XI3 = sb3("XI3", [128, 8, T], I32)
    T16 = sb3("T16", [128, 2, 512], BF16)
    QTL = sb3("QTL", [128, 8, C], BF16)
    OGL = sb3("OGL", [128, 16, T], BF16)
    SGL = sb3("SGL", [128, 16, T], BF16)
    XPL = sb3("XPL", [128, 8, T])
    XP2 = sb3("XP2", [128, 8, T])
    OUTB = sb3("OUTB", [128, 8, T])
    TP3 = sb3("TP3", [128, 8, T])
    UB3 = sb3("UB3", [128, 8, T], BF16)

    em.dma("d", lambda e: e.dma_start(out=WO[:].rearrange("p a b -> p (a b)"),
                                      in_=woq_st.rearrange("p a b -> p (a b)")))
    em.barrier()

    for c in range(nchunks):
        tl = []
        tl.append(em.dma("d", lambda e, c=c: e.dma_start(
            out=OGL[:].rearrange("p a b -> p (a b)"), in_=og_st[c])))
        tl.append(em.dma("d", lambda e, c=c: e.dma_start(
            out=SGL[:].rearrange("p a b -> p (a b)"), in_=sg_st[c])))
        tl.append(em.dma("d", lambda e, c=c: e.dma_start(
            out=XPL[:].rearrange("p a b -> p (a b)"), in_=xp_st[c])))
        em.mb()
        for ic in range(icpc):
            gic = c * icpc + ic
            tsl = slice(ic * C, (ic + 1) * C)
            tq3 = em.dma("pd", lambda e, gic=gic: e.dma_start(
                out=QTL[:].rearrange("p a b -> p (a b)"), in_=qt_st[gic]))
            tse = em.cop("s", lambda e, gic=gic: e.activation(
                ELT[:], AP(LTAB, gic % 32, [[8 * 32, 128], [32, 8]]), AF.Exp))
            for h in range(H):
                po = [PO1, PO2][h % 2]
                for kk in range(2):
                    em.cop("v", lambda e, h=h, kk=kk: e.tensor_scalar(
                        T16[:, kk, :], S32[:, 2 * h + kk, :],
                        ELT[:, 2 * h + kk:2 * h + kk + 1], None, ALU.mult),
                        waits=[tse])
                tt16 = em.tok("v")
                tc = []
                for j in range(4):
                    for kk in range(2):
                        tc.append(em.cop("t", lambda e, h=h, j=j, kk=kk, po=po:
                                         e.matmul(
                            po[:, j * 128:(j + 1) * 128],
                            T16[:, kk, j * 128:(j + 1) * 128],
                            QTL[:, 2 * h + kk, :], start=(kk == 0), stop=(kk == 1)),
                            waits=[tt16, tq3]))
                em.cop("v", lambda e, h=h, po=po, tsl=tsl: e.tensor_tensor(
                    OF[:, 4 * h:4 * h + 4, tsl],
                    po[:, 0:512].rearrange("p (a b) -> p a b", a=4),
                    SGL[:, 4 * h:4 * h + 4, tsl], ALU.mult), waits=tc + tl)
                em.mb()
        em.cop("v", lambda e: e.tensor_tensor(OF[:].rearrange("p a b -> p (a b)"),
                                              OF[:].rearrange("p a b -> p (a b)"),
                                              OGL[:].rearrange("p a b -> p (a b)"),
                                              ALU.add))
        em.mb()
        em.cop("s", lambda e: e.activation(OGL[:].rearrange("p a b -> p (a b)"),
                                           OF[:].rearrange("p a b -> p (a b)"),
                                           AF.Square))
        t0 = []
        for k in range(16):
            t0.append(em.cop("t", lambda e, k=k: e.matmul(
                PA[0:1, 0:256], ONESC[:], OGL[:, k, :], start=(k == 0),
                stop=(k == 15)), waits=[em.tok("s")]))
        tr = em.cop("v", lambda e: e.tensor_scalar(SROW[0:1, 0:256], PA[0:1, 0:256],
                                                   1.0 / DV, None, ALU.mult), waits=t0)
        for tt in range(2):
            tk = em.cop("t", lambda e, tt=tt: e.transpose(
                PO2[0:128, 448 + tt:449 + tt], SROW[0:1, tt * 128:(tt + 1) * 128],
                IDENTF[0:1, 0:1]), waits=[tr])
            em.cop("v", lambda e, tt=tt: e.tensor_copy(
                SC[:, tt, 0:1], PO2[0:128, 448 + tt:449 + tt]), waits=[tk])
        rsqrt_cols(0, 40, 1e-8)
        tw3 = em.cop("v", lambda e: e.tensor_tensor(
            TMQ[:].rearrange("p a b -> p (a b)"), OF[:].rearrange("p a b -> p (a b)"),
            AP(NO, 0, [[16, 128], [1, 16], [0, T]]), ALU.mult), waits=[em.tok("s")])
        rowmax_col(TMQ[:], 16, 1, [tw3])
        for tt in range(2):
            em.cop("v", lambda e, tt=tt: e.tensor_mul(SC[:, tt, 1:2], SC[:, tt, 1:2],
                                                      SC[:, tt, 0:1]))
            em.cop("v", lambda e, tt=tt: e.tensor_scalar(SC[:, tt, 1:2], SC[:, tt, 1:2],
                                                         1e-5, None, ALU.max))
            em.cop("v", lambda e, tt=tt: e.reciprocal(COLS[:, 44:45], SC[:, tt, 1:2]))
            em.cop("v", lambda e, tt=tt: e.scalar_tensor_tensor(
                SC[:, tt, 3:4], COLS[:, 44:45], 127.0, SC[:, tt, 0:1],
                ALU.mult, ALU.mult))
            em.cop("v", lambda e, tt=tt: e.scalar_tensor_tensor(
                SC[:, tt, 4:5], SC[:, tt, 1:2], 1.0 / 127.0, MW[:, 6:7],
                ALU.mult, ALU.mult))
        em.cop("v", lambda e: e.tensor_copy(AP(TMRB, 0, [[256, 128], [1, 2]]),
                                            SC[:, :, 4]))
        em.cop("v", lambda e: e.tensor_copy(SC[:, :, 5], AP(TMRB, 0, [[256, 128], [1, 2]])))
        em.cop("v", lambda e: e.tensor_sub(SC[:, :, 6], SC[:, :, 4], SC[:, :, 5]))
        fl5 = flush_sc(3, 7, [em.tok("v")])
        em.mb()
        phiob, tko = bcast(3, fl5)
        em.cop("v", lambda e, phiob=phiob: e.tensor_tensor(
            TMQ[:].rearrange("p a b -> p (a b)"), TMQ[:].rearrange("p a b -> p (a b)"),
            AP(phiob.tensor, phiob.offset, [phiob.ap[0], [0, 16], phiob.ap[1]]),
            ALU.mult), waits=[tko])
        for half in range(2):
            em.cop("v", lambda e, half=half: e.tensor_scalar(
                XI3[:].rearrange("p a b -> p (a b)"),
                TMQ[:, half * 8:(half + 1) * 8, :].rearrange("p a b -> p (a b)"),
                RC, RC, ALU.add, ALU.subtract))
            em.cop("s", lambda e, half=half: e.activation(
                OQ[:, half * 8:(half + 1) * 8, :].rearrange("p a b -> p (a b)"),
                XI3[:].rearrange("p a b -> p (a b)"), AF.Copy), waits=[em.tok("v")])
            em.mb()
        s_ = bcslot[0]
        bcslot[0] = 1 - s_
        tp1 = em.cop("t", lambda e, s_=s_: e.matmul(
            PBC[:, s_ * 256:(s_ + 1) * 256], SELT[:, 5, :],
            SCR[:].rearrange("a b c -> a (b c)"), start=True, stop=False), waits=fl5)
        tp2 = em.cop("t", lambda e, s_=s_: e.matmul(
            PBC[:, s_ * 256:(s_ + 1) * 256], SELT[:, 6, :],
            SCR[:].rearrange("a b c -> a (b c)"), start=False, stop=True))
        em.cop("v", lambda e, s_=s_: e.tensor_copy(
            MBC[:, 0, :], PBC[:, s_ * 256:(s_ + 1) * 256]), waits=[tp1, tp2])
        two = []
        for m in range(8):
            pd = PA if m < 4 else PB
            off = (m % 4) * 256
            for k in range(16):
                two.append(em.cop("t", lambda e, m=m, k=k, pd=pd, off=off: e.matmul(
                    pd[:, off:off + 256], WO[:, k, m * 128:(m + 1) * 128], OQ[:, k, :],
                    start=(k == 0), stop=(k == 15)), waits=[em.tok("s")]))
        for half, pd in [(0, PA), (1, PB)]:
            em.cop("v", lambda e, half=half, pd=pd: e.tensor_tensor(
                TP3[:, half * 4:(half + 1) * 4, :].rearrange("p a b -> p (a b)"),
                pd[:, 0:1024],
                AP(MBC, 0, [[2 * T, 128], [0, 4], [1, T]]), ALU.mult), waits=two)
        em.cop("v", lambda e: e.tensor_add(XP2[:].rearrange("p a b -> p (a b)"),
                                           TP3[:].rearrange("p a b -> p (a b)"),
                                           XPL[:].rearrange("p a b -> p (a b)")))
        em.mb()
        ln_to_ubf(XP2, 8, 9, 10, OQ[:, 0:8, :], OQ[:, 8:16, :], UB3)
        emit_ffn(MBC[0:64, 0, :], XP2, OUTB, UB3, [em.tok("v")])
        if dbg and c == 0:
            em.dma("d", lambda e: e.dma_start(out=dbg_out["d_of"],
                                              in_=OF[:].rearrange("p a b -> p (a b)")),
                   waits=[em.tok("v")])
        em.mb()
        em.dma("d", lambda e, c=c: e.dma_start(
            out=yT.rearrange("(k p) t -> p k t", p=128)[:, :, c * T:(c + 1) * T],
            in_=OUTB[:]), waits=[em.tok("v")])
        em.barrier()

    em.flush()
    ph3.close()
    return nc


_CACHE = {}


def _host_inputs(inputs):
    f32 = np.float32
    wq = inputs["L0_attn_wq"]; wk = inputs["L0_attn_wk"]
    wv = inputs["L0_attn_wv"]; wg = inputs["L0_attn_wg"]
    wgk1 = inputs["L0_attn_wgk1"]; wgk2 = inputs["L0_attn_wgk2"]
    wo = inputs["L0_attn_wo"]
    W1 = np.asarray(inputs["L0_ffn_W1"], f32); b1 = np.asarray(inputs["L0_ffn_b1"], f32)
    W2 = np.asarray(inputs["L0_ffn_W2"], f32)
    bin_ = np.asarray(inputs["L0_ffn_bin_"], f32)
    com = {}
    com["wqT"] = np.ascontiguousarray(np.asarray(wq, f32).T)
    com["wkT"] = np.ascontiguousarray(np.asarray(wk, f32).T)
    com["wvT"] = np.ascontiguousarray(np.asarray(wv, f32).T)
    com["wgT"] = np.ascontiguousarray(np.asarray(wg, f32).T)
    com["wgk1T"] = np.ascontiguousarray(np.asarray(wgk1, f32).T)
    com["wgk2T"] = np.ascontiguousarray(np.asarray(wgk2, f32).T)
    com["woT"] = np.ascontiguousarray(np.asarray(wo, f32).T)
    arr = lambda v, nt: np.ascontiguousarray(np.asarray(v, f32).reshape(nt, 128).T)
    com["nq_a"] = arr(inputs["L0_attn_nq"], 8)
    com["nk_a"] = arr(inputs["L0_attn_nk"], 8)
    com["nv_a"] = arr(inputs["L0_attn_nv"], 8)
    com["ng_a"] = arr(inputs["L0_attn_ng"], 8)
    com["ngk1_a"] = arr(inputs["L0_attn_ngk1"], 8)
    com["ngk2_a"] = np.ascontiguousarray(np.asarray(inputs["L0_attn_ngk2"], f32).reshape(16, 1))
    com["bgk2_a"] = arr(inputs["L0_attn_bgk2"], 8)
    com["no_a"] = arr(inputs["L0_attn_no"], 16)
    com["Win"] = np.ascontiguousarray(np.asarray(inputs["L0_ffn_Win"], f32))
    com["WoutT"] = np.ascontiguousarray(np.asarray(inputs["L0_ffn_Wout"], f32))
    repw1 = np.zeros((64, 8, 128), f32)
    sumw2 = np.zeros((128, 8, 64), f32)
    w1a = np.zeros((128, 8), f32); b1a = np.zeros((128, 8), f32)
    bina = np.zeros((128, 8), f32)
    for k in range(8):
        for mloc in range(8):
            m = 8 * k + mloc
            for i in range(16):
                p = mloc * 16 + i
                repw1[m, k, p] = W1[m, i]
                sumw2[p, k, m] = W2[m, i]
                w1a[p, k] = W1[m, i]
                b1a[p, k] = b1[m, i]
                bina[p, k] = bin_[m]
    com["repw1"] = repw1; com["sumw2"] = sumw2
    com["w1_a"] = w1a; com["b1_a"] = b1a; com["binmi_a"] = bina
    com["b2_a"] = np.ascontiguousarray(np.asarray(inputs["L0_ffn_b2"], f32).reshape(64, 1))
    com["bout_r"] = np.ascontiguousarray(
        np.asarray(inputs["L0_ffn_bout"], f32).reshape(1, 8, 128).astype(NB))
    com["triu"] = np.ascontiguousarray(np.triu(np.ones((128, 128), np.float32)))
    com["ident"] = np.ascontiguousarray(np.eye(128, dtype=f32).astype(NB))
    com["identf"] = np.ascontiguousarray(np.eye(128, dtype=f32))
    com["ones_c"] = np.ones((128, 1), f32).astype(NB)
    com["ones_r2"] = np.ones((1, 256), f32).astype(NB)
    com["ones_cf"] = np.ones((128, 1), f32)
    com["ones_rf"] = np.ones((1, 128), f32)
    sel = np.zeros((32, 32, 128), f32)
    for j in range(32):
        sel[j, j, :] = 1.0
    com["sel"] = np.ascontiguousarray(sel.astype(NB))
    return com


def kernel(**inputs):
    x = np.asarray(inputs["L0_attn_hidden_states"], np.float32)
    B, L, Dm = x.shape
    half = L // 2
    com = _host_inputs(inputs)
    if "nc" not in _CACHE:
        _CACHE["nc"] = build_nc(16)
    nc = _CACHE["nc"]
    in_maps = []
    for core in range(8):
        b, s = core // 2, core % 2
        m = dict(com)
        m["xT"] = np.ascontiguousarray(x[b, s * half:(s + 1) * half, :].T)
        m["smask"] = np.full((128, 1), float(s), np.float32)
        in_maps.append(m)
    res = run_bass_kernel_spmd(nc, in_maps, core_ids=list(range(8)))
    out = np.empty_like(x)
    for core in range(8):
        b, s = core // 2, core % 2
        out[b, s * half:(s + 1) * half, :] = np.asarray(res.results[core]["yT"]).T
    return out


# version 03
# speedup vs baseline: 1.1137x; 1.0689x over previous
import numpy as np
import ml_dtypes
import contextlib
import concourse.bass as bass
import concourse.mybir as mybir
from concourse.bass_utils import run_bass_kernel_spmd

F32 = mybir.dt.float32
BF16 = mybir.dt.bfloat16
I32 = mybir.dt.int32
U8 = mybir.dt.uint8
AF = mybir.ActivationFunctionType
ALU = mybir.AluOpType
AX = mybir.AxisListType
AP = bass.AP
NB = ml_dtypes.bfloat16

RC = 12582912.0
D = 1024
DV = 2048
H = 4
T = 256
C = 128
SERIAL = False


class EM:
    ENGS = ("t", "v", "s", "p", "d")
    SEMK = ("t", "v", "s", "p", "d", "pd")

    def __init__(self, nc, stack):
        self.nc = nc
        self.q = {e: [] for e in self.ENGS}
        self.cnt = {e: 0 for e in self.SEMK}
        self.sems = {e: stack.enter_context(nc.semaphore("sem_" + e)) for e in self.SEMK}
        self.waited = {e: {e2: 0 for e2 in self.SEMK} for e in self.ENGS}

    def op(self, eng, fn, waits=(), inc=0):
        qk = "p" if eng == "pd" else eng
        wl = []
        if eng in ("v", "s", "p", "d", "pd") and self.cnt[eng] > 0:
            wl.append((self.sems[eng], self.cnt[eng]))
        for w in waits:
            if w is None:
                continue
            se, sv = w
            if se == eng:
                continue
            if self.waited[qk][se] < sv:
                self.waited[qk][se] = sv
                wl.append((self.sems[se], sv))
        sem = self.sems[eng]

        def run(e, wl=wl, fn=fn, inc=inc, sem=sem):
            for (s, v) in wl:
                e.wait_ge(s, v)
            ins = fn(e)
            if inc:
                ins.then_inc(sem, inc)
            return ins
        self.q[qk].append(run)
        if inc:
            self.cnt[eng] += inc
        return (eng, self.cnt[eng])

    def cop(self, eng, fn, waits=()):
        return self.op(eng, fn, waits=waits, inc=1)

    def dma(self, eng, fn, waits=()):
        return self.op(eng, fn, waits=waits, inc=16)

    def tok(self, eng):
        return (eng, self.cnt[eng])

    def barrier(self, engs=None):
        engs = engs or self.ENGS
        toks = [(e, self.cnt[e]) for e in (self.SEMK if engs is self.ENGS or engs == self.ENGS else engs)]
        for e in engs:
            for (se, sv) in toks:
                if se != e and self.waited[e][se] < sv:
                    self.waited[e][se] = sv
                    sem = self.sems[se]
                    self.q[e].append(lambda en, s=sem, v=sv: en.wait_ge(s, v))

    def mb(self):
        if SERIAL or not getattr(self, "pipe", False):
            self.barrier()

    def flush(self):
        with self.nc.Block() as blk:
            for ename, attr in [("t", "tensor"), ("v", "vector"), ("s", "scalar"),
                                ("p", "gpsimd"), ("d", "sync")]:
                ops = self.q[ename]
                if not ops:
                    continue
                getattr(blk, attr)(lambda e, ops=tuple(ops): [f(e) for f in ops])
        self.q = {e: [] for e in self.ENGS}


def build_nc(nchunks=16, dbg=False, stage=3, p1cut=99, lncut=99, fcut=99, g2cut=99):
    ntok = nchunks * T
    nic = ntok // C
    icpc = T // C
    import os as _os
    nc = bass.Bass("TRN2", target_bir_lowering=False,
                   detect_race_conditions=_os.environ.get("KRACE", "1") == "1")
    din = {}

    def inp(name, shape, dt=F32):
        din[name] = nc.dram_tensor(name, shape, dt, kind="ExternalInput").ap()
        return din[name]

    xT = inp("xT", [D, ntok])
    for nm, shp in [("wqT", [D, D]), ("wkT", [D, D]), ("wvT", [D, DV]),
                    ("wgT", [D, DV]), ("wgk1T", [D, 16]), ("wgk2T", [16, D]),
                    ("woT", [DV, D]), ("nq_a", [128, 8]), ("nk_a", [128, 8]),
                    ("nv_a", [128, 8]), ("ng_a", [128, 8]), ("ngk1_a", [128, 8]),
                    ("ngk2_a", [16, 1]), ("bgk2_a", [128, 8]), ("no_a", [128, 16]),
                    ("Win", [D, 64]), ("WoutT", [64, D]), ("repw1", [64, 8, 128]),
                    ("w1_a", [128, 8]), ("b1_a", [128, 8]), ("binmi_a", [128, 8]),
                    ("sumw2", [128, 8, 64]), ("b2_a", [64, 1]),
                    ("identf", [128, 128]), ("ones_cf", [128, 1]), ("ones_rf", [1, 128]),
                    ("smask", [128, 1])]:
        inp(nm, shp)
    inp("triu", [128, 128])
    inp("ident", [128, 128], BF16)
    inp("ones_c", [128, 1], BF16)
    inp("ones_r2", [1, 256], BF16)
    inp("sel", [32, 32, 128], BF16)
    inp("bout_r", [1, 8, 128], BF16)

    yT = nc.dram_tensor("yT", [D, ntok], F32, kind="ExternalOutput").ap()
    dbg_out = {}
    if dbg:
        for nm, shp, dt in [("d_rep", [128, 8 * 128], BF16), ("d_pa", [128, 1024], F32),
                            ("d_ub", [128, 8 * T], BF16), ("d_fs", [128, T], BF16),
                            ("d_fh", [128, 8 * T], BF16), ("d_fg", [128, T], BF16),
                            ("d_sc", [128, 2 * 32], F32), ("d_mbc", [128, 2 * T], F32),
                            ("d_xp", [128, 8 * T], F32), ("d_q", [128, 8 * T], BF16),
                            ("d_k", [128, 8 * T], BF16), ("d_gl", [128, 8 * T], BF16),
                            ("d_og", [128, 16 * T], BF16), ("d_of", [128, 16 * T], F32),
                            ("d_s32", [128, 8 * 512], F32), ("d_v", [128, 2 * 2048], BF16)]:
            dbg_out[nm] = nc.dram_tensor(nm, shp, dt, kind="ExternalOutput").ap()

    og_st = nc.dram_tensor("og_st", [nchunks, 128, 16 * T], BF16).ap()
    sg_st = nc.dram_tensor("sg_st", [nchunks, 128, 16 * T], BF16).ap()
    qt_st = nc.dram_tensor("qt_st", [nic, 128, 8 * C], BF16).ap()
    xp_st = nc.dram_tensor("xp_st", [nchunks, 128, 8 * T], F32).ap()
    wvq_st = nc.dram_tensor("wvq_st", [128, 8, DV], BF16).ap()
    wgq_st = nc.dram_tensor("wgq_st", [128, 8, DV], BF16).ap()
    woq_st = nc.dram_tensor("woq_st", [128, 16, D], BF16).ap()
    cc_in = nc.dram_tensor("cc_in", [1024, 512], F32)
    cc_out = nc.dram_tensor("cc_out", [2048, 512], F32)

    perm = contextlib.ExitStack()
    em = EM(nc, perm)
    sbp = lambda n, s, d=F32: perm.enter_context(nc.sbuf_tensor(n, s, d))

    WGK1 = sbp("WGK1", [128, 8, 16], BF16)
    WGK2F = sbp("WGK2F", [128, D], BF16)
    WGK2 = WGK2F[0:16, :]
    WIN = sbp("WIN", [128, 8, 64], BF16)
    WOUTF = sbp("WOUTF", [128, D], BF16)
    WOUT = WOUTF[0:64, :]
    REPF = sbp("REPF", [128, 8, 128], BF16)
    REP = REPF[0:64, :, :]
    SUM2 = sbp("SUM2", [128, 8, 64], BF16)
    B1E = sbp("B1E", [128, 8])
    NQ = sbp("NQ", [128, 8]); NK = sbp("NK", [128, 8]); NV = sbp("NV", [128, 8])
    NG = sbp("NG", [128, 8]); NGK1 = sbp("NGK1", [128, 8]); NGK2F = sbp("NGK2F", [128, 1]); NGK2 = NGK2F[0:16, :]
    BGK2 = sbp("BGK2", [128, 8]); NO = sbp("NO", [128, 16])
    B2F = sbp("B2F", [128, 1]); B2 = B2F[0:64, :]
    BOUTRF = sbp("BOUTRF", [128, 8, 128], BF16); BOUTR = BOUTRF[0:1, :, :]
    TRIU = sbp("TRIU", [128, 128])
    IDENT = sbp("IDENT", [128, 128], BF16)
    IDENTF = sbp("IDENTF", [128, 128], F32)
    ONESC = sbp("ONESC", [128, 1], BF16)
    ONESR2F = sbp("ONESR2F", [128, 256], BF16); ONESR2 = ONESR2F[0:1, :]
    ONESCF = sbp("ONESCF", [128, 1], F32)
    ONESRFF = sbp("ONESRFF", [128, 128], F32); ONESRF = ONESRFF[0:1, :]
    SELTF = sbp("SELTF", [128, 32, 128], BF16); SELT = SELTF[0:32, :, :]
    SMASK = sbp("SMASK", [128, 1])
    ZEROT = sbp("ZEROT", [128, 128])
    CEPS5 = sbp("CEPS5", [128, 1]); CEPS8 = sbp("CEPS8", [128, 1])
    MW = sbp("MW", [128, 8]); WSC = sbp("WSC", [128, 8])
    S32 = sbp("S32", [128, 8, 512])
    LTAB = sbp("LTAB", [128, 8, 32])
    COLS = sbp("COLS", [128, 48])
    SC = sbp("SC", [128, 2, 32])
    SCRF = sbp("SCRF", [128, 2, 128], BF16); SCR = SCRF[0:32, :, :]
    SROWF = sbp("SROWF", [128, 512]); SROW = SROWF[0:1, :]
    MBC = sbp("MBC", [128, 2, T])
    FSF = sbp("FSF", [128, T], BF16); FS = FSF[0:64, :]
    FH = sbp("FH", [128, 8, T], BF16)
    FGF = sbp("FGF", [128, T], BF16); FG = FGF[0:64, :]
    I1F = sbp("I1F", [128, T]); I1 = I1F[0:16, :]
    I1BF = sbp("I1BF", [128, T], BF16); I1B = I1BF[0:16, :]
    SQ16F = sbp("SQ16F", [128, T], BF16); SQ16 = SQ16F[0:16, :]
    TMRB = sbp("TMRB", [128, 2, 128], BF16)
    ELT = sbp("ELT", [128, 8])

    PA = perm.enter_context(nc.psum_tensor("PA", [128, 1024], F32))
    PB = perm.enter_context(nc.psum_tensor("PB", [128, 1024], F32))
    PO1 = perm.enter_context(nc.psum_tensor("PO1", [128, 512], F32))
    PO2 = perm.enter_context(nc.psum_tensor("PO2", [128, 512], F32))
    PBC = perm.enter_context(nc.psum_tensor("PBC", [128, 512], F32))
    PTR = perm.enter_context(nc.psum_tensor("PTR", [128, 256], BF16))

    ph1 = contextlib.ExitStack()
    sb1 = lambda n, s, d=F32: ph1.enter_context(nc.sbuf_tensor(n, s, d))
    WQ = sb1("WQ", [128, 8, D], BF16)
    WK = sb1("WK", [128, 8, D], BF16)
    WST = sb1("WST", [128, 2, 8, 256], BF16)

    stup = contextlib.ExitStack()
    sbs = lambda n, s, d=F32: stup.enter_context(nc.sbuf_tensor(n, s, d))
    STG = sbs("STG", [128, 2, 4096])
    QSC = sbs("QSC", [128, 4096])
    QSB = sbs("QSB", [128, 4096], BF16)
    FSTG = sbs("FSTG", [128, 8, 64])
    WOSF = sbs("WOSF", [128, D]); WOS = WOSF[0:64, :]
    RPSF = sbs("RPSF", [128, 8, 128]); RPS = RPSF[0:64, :, :]
    SMS = sbs("SMS", [128, 8, 64])

    for name, dst in [("nq_a", NQ), ("nk_a", NK), ("nv_a", NV), ("ng_a", NG),
                      ("ngk1_a", NGK1), ("ngk2_a", NGK2), ("bgk2_a", BGK2),
                      ("no_a", NO), ("b2_a", B2), ("bout_r", BOUTR),
                      ("triu", TRIU), ("ident", IDENT), ("identf", IDENTF),
                      ("ones_c", ONESC), ("ones_r2", ONESR2), ("ones_cf", ONESCF),
                      ("ones_rf", ONESRF), ("sel", SELT), ("smask", SMASK)]:
        em.dma("d", lambda e, s=din[name], d=dst: e.dma_start(out=d[:], in_=s))
    em.dma("d", lambda e: e.dma_start(out=FSTG[:], in_=din["Win"].rearrange("(k p) m -> p k m", p=128)))
    em.dma("d", lambda e: e.dma_start(out=WOS[:], in_=din["WoutT"]))
    em.dma("d", lambda e: e.dma_start(out=RPS[:], in_=din["repw1"]))
    em.dma("d", lambda e: e.dma_start(out=SMS[:], in_=din["sumw2"]))
    em.dma("d", lambda e: e.dma_start(out=COLS[:, 0:8], in_=din["w1_a"]))
    em.dma("d", lambda e: e.dma_start(out=COLS[:, 8:16], in_=din["b1_a"]))
    em.dma("d", lambda e: e.dma_start(out=COLS[:, 16:24], in_=din["binmi_a"]))
    tld = em.tok("d")
    em.cop("v", lambda e: e.memset(ZEROT[:], 0.0))
    em.cop("v", lambda e: e.memset(CEPS5[:], 1e-5))
    em.cop("v", lambda e: e.memset(CEPS8[:], 1e-8))
    em.cop("v", lambda e: e.memset(S32[:].rearrange("p a b -> p (a b)"), 0.0))
    em.cop("v", lambda e: e.memset(LTAB[:].rearrange("p a b -> p (a b)"), 0.0))
    em.cop("v", lambda e: e.memset(SC[:].rearrange("p a b -> p (a b)"), 0.0))
    em.cop("v", lambda e: e.memset(COLS[:, 24:48], 0.0), waits=[tld])
    em.cop("v", lambda e: e.memset(SROW[:], 0.0))
    em.cop("v", lambda e: e.tensor_copy(WIN[:].rearrange("p a b -> p (a b)"),
                                        FSTG[:].rearrange("p a b -> p (a b)")), waits=[tld])
    em.cop("v", lambda e: e.tensor_copy(WOUT[:], WOS[:]))
    em.cop("v", lambda e: e.tensor_copy(REP[:].rearrange("p a b -> p (a b)"),
                                        RPS[:].rearrange("p a b -> p (a b)")))
    em.cop("v", lambda e: e.tensor_copy(SUM2[:].rearrange("p a b -> p (a b)"),
                                        SMS[:].rearrange("p a b -> p (a b)")))
    em.cop("v", lambda e: e.tensor_mul(B1E[:], COLS[:, 0:8], COLS[:, 16:24]), waits=[tld])
    em.cop("v", lambda e: e.tensor_add(B1E[:], B1E[:], COLS[:, 8:16]))
    em.barrier()

    skipwq = _os.environ.get("SKIPWQ", "0") == "1"
    if skipwq:
        wjobs = []
    else:
        pass
    wjobs = [] if skipwq else [
        (din["wqT"], ("sbuf", WQ), 8, D, 0),
        (din["wkT"], ("sbuf", WK), 8, D, 1),
        (din["wvT"], ("hbm", wvq_st), 8, DV, 2),
        (din["wgT"], ("hbm", wgq_st), 8, DV, 3),
        (din["wgk1T"], ("sbuf", WGK1), 8, 16, 4),
        (din["woT"], ("hbm", woq_st), 16, D, 6),
    ]
    for (hbm, dest, nkt, dout, widx) in wjobs:
        per_unit = max(1, 4096 // dout)
        units = [(k0, min(k0 + per_unit, nkt)) for k0 in range(0, nkt, per_unit)]
        src3 = hbm.rearrange("(k p) o -> p k o", p=128)
        prevred = None
        for ui, (k0, k1) in enumerate(units):
            fl = (k1 - k0) * dout
            slot = ui % 2
            tin = em.dma("d", lambda e, k0=k0, k1=k1, slot=slot, src3=src3, dout=dout:
                         e.dma_start(out=STG[:, slot, 0:(k1 - k0) * dout]
                                     .rearrange("p (a b) -> p a b", a=(k1 - k0)),
                                     in_=src3[:, k0:k1, :]),
                         waits=[prevred] if prevred else ())
            em.cop("v", lambda e, slot=slot, fl=fl: e.reduce_sum(
                COLS[:, 40:41], STG[:, slot, 0:fl], axis=AX.X,
                apply_absolute_value=True), waits=[tin])
            if ui == 0:
                em.cop("v", lambda e: e.tensor_copy(COLS[:, 41:42], COLS[:, 40:41]))
            else:
                em.cop("v", lambda e: e.tensor_add(COLS[:, 41:42], COLS[:, 41:42],
                                                   COLS[:, 40:41]))
            prevred = em.tok("v")
        em.cop("t", lambda e: e.matmul(PA[0:1, 0:1], ONESCF[:], COLS[:, 41:42],
                                       start=True, stop=True), waits=[prevred])
        tmm = em.tok("t")
        numel = nkt * 128 * dout
        em.cop("v", lambda e, numel=numel: e.tensor_scalar(
            SROW[0:1, 0:1], PA[0:1, 0:1], 1.0 / numel, 1e-5, ALU.mult, ALU.max),
            waits=[tmm])
        tms = em.tok("v")
        em.cop("t", lambda e: e.matmul(PB[:, 0:1], ONESRF[:], SROW[0:1, 0:1],
                                       start=True, stop=True), waits=[tms])
        tmb = em.tok("t")
        em.cop("v", lambda e, widx=widx: e.tensor_copy(MW[:, widx:widx + 1], PB[:, 0:1]),
               waits=[tmb])
        em.cop("v", lambda e, widx=widx: e.reciprocal(WSC[:, widx:widx + 1],
                                                      MW[:, widx:widx + 1]))
        em.barrier()
        for ui, (k0, k1) in enumerate(units):
            fl = (k1 - k0) * dout
            slot = ui % 2
            tin = em.dma("d", lambda e, k0=k0, k1=k1, slot=slot, src3=src3, dout=dout:
                         e.dma_start(out=STG[:, slot, 0:(k1 - k0) * dout]
                                     .rearrange("p (a b) -> p a b", a=(k1 - k0)),
                                     in_=src3[:, k0:k1, :]),
                         waits=[em.tok("v"), em.tok("s"), em.tok("d")])
            em.cop("s", lambda e, slot=slot, fl=fl, widx=widx: e.activation(
                QSC[:, 0:fl], STG[:, slot, 0:fl], AF.Identity, scale=WSC[:, widx:widx + 1]),
                waits=[tin])
            tsc = em.tok("s")
            em.cop("v", lambda e, fl=fl: e.tensor_scalar(
                QSC[:, 0:fl], QSC[:, 0:fl], RC, RC, ALU.add, ALU.subtract), waits=[tsc])
            if dest[0] == "sbuf":
                em.cop("v", lambda e, fl=fl, dest=dest, k0=k0, k1=k1: e.tensor_scalar(
                    dest[1][:, k0:k1, :].rearrange("p a b -> p (a b)"),
                    QSC[:, 0:fl], 1.0, -1.0, ALU.min, ALU.max))
            else:
                em.cop("v", lambda e, fl=fl: e.tensor_scalar(
                    QSB[:, 0:fl], QSC[:, 0:fl], 1.0, -1.0, ALU.min, ALU.max))
                em.dma("d", lambda e, fl=fl, dest=dest, k0=k0, k1=k1: e.dma_start(
                    out=dest[1][:, k0:k1, :].rearrange("p a b -> p (a b)"),
                    in_=QSB[:, 0:fl]), waits=[em.tok("v")])
            em.barrier(["v", "s", "d"])
    if skipwq:
        tin = None
    tin = em.dma("d", lambda e: e.dma_start(out=STG[0:16, 0, 0:1024], in_=din["wgk2T"])) if not skipwq else None
    if not skipwq:
      em.cop("v", lambda e: e.reduce_sum(COLS[0:16, 40:41], STG[0:16, 0, 0:1024],
                                       axis=AX.X, apply_absolute_value=True), waits=[tin])
    if not skipwq:
      em.cop("t", lambda e: e.matmul(PA[0:1, 0:1], ONESCF[0:16, :], COLS[0:16, 40:41],
                                   start=True, stop=True), waits=[em.tok("v")])
      em.cop("v", lambda e: e.tensor_scalar(SROW[0:1, 0:1], PA[0:1, 0:1],
                                          1.0 / (16 * 1024), 1e-5, ALU.mult, ALU.max),
           waits=[em.tok("t")])
      em.cop("t", lambda e: e.matmul(PB[:, 0:1], ONESRF[:], SROW[0:1, 0:1],
                                   start=True, stop=True), waits=[em.tok("v")])
      em.cop("v", lambda e: e.tensor_copy(MW[:, 7:8], PB[:, 0:1]), waits=[em.tok("t")])
      em.cop("v", lambda e: e.reciprocal(WSC[:, 7:8], MW[:, 7:8]))
      em.cop("s", lambda e: e.activation(QSC[0:16, 0:1024], STG[0:16, 0, 0:1024],
                                       AF.Identity, scale=WSC[0:16, 7:8]), waits=[em.tok("v")])
      em.cop("v", lambda e: e.tensor_scalar(QSC[0:16, 0:1024], QSC[0:16, 0:1024],
                                          RC, RC, ALU.add, ALU.subtract),
           waits=[em.tok("s")])
      em.cop("v", lambda e: e.tensor_scalar(WGK2[:], QSC[0:16, 0:1024], 1.0, -1.0,
                                          ALU.min, ALU.max))
    em.barrier()
    em.flush()
    stup.close()
    if stage == 0:
        ph1.close()
        return nc

    XA = sb1("XA", [128, 8, T])
    XP = sb1("XP", [128, 8, T])
    U1 = sb1("U1", [128, 8, T])
    QSCR = sb1("QSCR", [128, 8, T])
    XI = sb1("XI", [128, 4, T], I32)
    XQB = sb1("XQB", [128, 8, T], BF16)
    QF = sb1("QF", [128, 8, T], BF16)
    KF = sb1("KF", [128, 8, T], BF16)
    VT = sb1("VT", [128, 2, DV], BF16)
    GFB = sb1("GFB", [128, 16, T], BF16)
    THB = sb1("THB", [128, 16, T], BF16)
    SGB = sb1("SGB", [128, 16, T], BF16)
    OGB = sb1("OGB", [128, 16, T], BF16)
    GL = sb1("GL", [128, 8, T], BF16)
    E1 = sb1("E1", [128, 8, C], BF16)
    E2 = sb1("E2", [128, 8, C], BF16)
    QS = sb1("QS", [128, 8, C], BF16)
    KS = sb1("KS", [128, 8, C], BF16)
    QI = sb1("QI", [128, 8, C], BF16)
    KX = sb1("KX", [128, 8, C], BF16)
    KXT = sb1("KXT", [128, 2, C], BF16)
    ABF = sb1("ABF", [128, 128])
    AB = sb1("AB", [128, 128], BF16)
    S16 = sb1("S16", [128, 2, 512], BF16)
    WSCR = XA
    BB3 = AP(QSCR, 0, [[8 * T, 128], [T, 8], [1, C]])
    BS3 = AP(QSCR, C, [[8 * T, 128], [T, 8], [1, C]])

    def BBs(k):
        return AP(QSCR, k * T, [[8 * T, 128], [1, C]])

    bcslot = [0]
    rdt = {}

    def bcast(j, waits=()):
        s = bcslot[0]
        bcslot[0] = 1 - s
        tk = em.cop("t", lambda e, s=s, j=j: e.matmul(
            PBC[:, s * 256:(s + 1) * 256], SELT[:, j, :],
            SCR[:].rearrange("a b c -> a (b c)"), start=True, stop=True),
            waits=list(waits) + [em.tok("v"), em.tok("s")])
        rdt["SCR_rd"] = tk
        return PBC[:, s * 256:(s + 1) * 256], tk

    def flush_sc(c0, c1, waits):
        tb = em.cop("v", lambda e: e.tensor_copy(
            TMRB[:, :, 0:32].rearrange("p a b -> p (a b)") if False else
            AP(TMRB, 0, [[256, 128], [128, 2], [1, 32]]), SC[:, :, 0:32]),
            waits=list(waits) + [rdt.get("TMRB")])
        tks = []
        for tt in range(2):
            tks.append(em.cop("t", lambda e, tt=tt: e.transpose(
                PTR[0:32, tt * 128:tt * 128 + 128],
                AP(TMRB, tt * 128, [[256, 128], [1, 32]]), IDENT[:]),
                waits=[tb, rdt.get("PTR")]))
        rdt["TMRB"] = tks[-1]
        out_tks = [em.cop("v", lambda e: e.tensor_copy(
            SCR[:].rearrange("a b c -> a (b c)"), PTR[0:32, 0:256]),
            waits=tks + [rdt.get("SCR_rd")])]
        rdt["PTR"] = out_tks[0]
        return out_tks

    def stats_cols(src_bf, src_sq, mcol, sqcol, waits):
        t0 = []
        for (src, r0) in [(src_bf, 0), (src_sq, 1)]:
            for k in range(8):
                t0.append(em.cop("t", lambda e, src=src, k=k, r0=r0: e.matmul(
                    PA[0:1, r0 * 256:(r0 + 1) * 256], ONESC[:], src[:, k, :],
                    start=(k == 0), stop=(k == 7)),
                    waits=list(waits) + [rdt.get("PA"), rdt.get("SROW")]))
        rdt["ST_MM"] = t0[-1]
        tr = em.cop("v", lambda e: e.tensor_scalar(SROW[0:1, 0:512], PA[0:1, 0:512],
                                                   1.0 / 1024.0, None, ALU.mult), waits=t0)
        rdt["PA"] = tr
        tks = []
        for (r0, col) in [(0, mcol), (1, sqcol)]:
            for tt in range(2):
                tk = em.cop("t", lambda e, r0=r0, tt=tt: e.transpose(
                    PO2[0:128, 448 + r0 * 2 + tt:449 + r0 * 2 + tt],
                    SROW[0:1, r0 * 256 + tt * 128:r0 * 256 + (tt + 1) * 128],
                    IDENTF[0:1, 0:1]), waits=[tr, rdt.get("PO2")])
                rdt["SROW"] = tk
                tks.append(em.cop("v", lambda e, col=col, tt=tt, r0=r0: e.tensor_copy(
                    SC[:, tt, col:col + 1],
                    PO2[0:128, 448 + r0 * 2 + tt:449 + r0 * 2 + tt]), waits=[tk]))
        rdt["PO2"] = tks[-1]
        return tks

    def rsqrt_cols(rstdcol, vcols, epsval):
        em.cop("v", lambda e: e.tensor_scalar(COLS[:, vcols:vcols + 2],
                                              SC[:, :, rstdcol], epsval, None, ALU.add))
        em.cop("s", lambda e: e.activation(SC[:, :, rstdcol], COLS[:, vcols:vcols + 2],
                                           AF.Ln), waits=[em.tok("v")])
        em.cop("s", lambda e: e.activation(SC[:, :, rstdcol], SC[:, :, rstdcol],
                                           AF.Exp, scale=-0.5))
        em.cop("v", lambda e: e.tensor_tensor(COLS[:, vcols + 2:vcols + 4],
                                              SC[:, :, rstdcol], SC[:, :, rstdcol],
                                              ALU.mult), waits=[em.tok("s")])
        em.cop("v", lambda e: e.tensor_tensor(COLS[:, vcols + 2:vcols + 4],
                                              COLS[:, vcols + 2:vcols + 4],
                                              COLS[:, vcols:vcols + 2], ALU.mult))
        em.cop("v", lambda e: e.tensor_scalar(COLS[:, vcols + 2:vcols + 4],
                                              COLS[:, vcols + 2:vcols + 4],
                                              -0.5, 1.5, ALU.mult, ALU.add))
        em.cop("v", lambda e: e.tensor_tensor(SC[:, :, rstdcol], SC[:, :, rstdcol],
                                              COLS[:, vcols + 2:vcols + 4], ALU.mult))
        return em.tok("v")

    def rstd_col(mcol, sqcol, rstdcol, epscol, waits):
        for tt in range(2):
            em.cop("v", lambda e, tt=tt: e.tensor_mul(COLS[:, 44:45],
                                                      SC[:, tt, mcol:mcol + 1],
                                                      SC[:, tt, mcol:mcol + 1]), waits=waits)
            em.cop("v", lambda e, tt=tt: e.tensor_sub(SC[:, tt, rstdcol:rstdcol + 1],
                                                      SC[:, tt, sqcol:sqcol + 1],
                                                      COLS[:, 44:45]))
        return rsqrt_cols(rstdcol, 40, 1e-5)

    def rowmax_col(w_ap, ndt, col, waits):
        t1 = em.cop("v", lambda e, w=w_ap: e.reduce_max(
            TMRB[:].rearrange("p a b -> p (a b)"), w.rearrange("p a b -> p b a"),
            axis=AX.X, apply_absolute_value=True),
            waits=list(waits) + [rdt.get("TMRB")])
        tks = []
        for tt in range(2):
            tk = em.cop("t", lambda e, tt=tt: e.transpose(
                PTR[:, tt * 128:(tt + 1) * 128], TMRB[:, tt, :], IDENT[:]),
                waits=[t1, rdt.get("PTR")])
            rdt["TMRB"] = tk
            tks.append(em.cop("v", lambda e, tt=tt, col=col: e.reduce_max(
                SC[:, tt, col:col + 1], PTR[:, tt * 128:(tt + 1) * 128], axis=AX.X),
                waits=[tk]))
        rdt["PTR"] = tks[-1]
        return tks

    def emit_ffn(rstd_sb, x_sb, out_t, ubf, waits):
        t0 = []
        for k in range(8):
            t0.append(em.cop("t", lambda e, k=k, ubf=ubf: e.matmul(
                PO1[0:64, 0:256], WIN[:, k, :], ubf[:, k, :], start=(k == 0),
                stop=(k == 7)), waits=list(waits) + [rdt.get("PO1")]))
        ts_ = em.cop("v", lambda e, r=rstd_sb: e.tensor_tensor(
            FS[:], PO1[0:64, 0:256], r, ALU.mult), waits=t0)
        rdt["PO1"] = ts_
        em.mb()
        trep = []
        for kk in range(8):
            pd = PA if kk < 4 else PB
            off = (kk % 4) * 256
            trep.append(em.cop("t", lambda e, kk=kk, pd=pd, off=off: e.matmul(
                pd[:, off:off + 256], REP[:, kk, :], FS[:], start=True, stop=True),
                waits=[ts_, rdt.get("PA" if kk < 4 else "PB")]))
        em.mb()
        if dbg:
            if not hasattr(em, "_dumped_pa"):
                em._dumped_pa = True
                em.cop("v", lambda e: e.tensor_copy(QSCR[:, 0:4, :].rearrange("p a b -> p (a b)"), PA[:, 0:1024]))
                em.barrier()
                em.dma("d", lambda e: e.dma_start(out=dbg_out["d_pa"], in_=QSCR[:, 0:4, :].rearrange("p a b -> p (a b)")))
                em.dma("d", lambda e: e.dma_start(out=dbg_out["d_rep"],
                                                  in_=REPF[:].rearrange("p a b -> p (a b)")))
                em.barrier()
        trl = []
        for kk in range(8):
            pd = PA if kk < 4 else PB
            off = (kk % 4) * 256
            trl.append(em.cop("s", lambda e, kk=kk, pd=pd, off=off: e.activation(
                FH[:, kk, :], pd[:, off:off + 256], AF.Relu, bias=B1E[:, kk:kk + 1]),
                waits=trep))
        rdt["PA"] = rdt["PB"] = trl[-1]
        em.mb()
        tsum = []
        for kk in range(8):
            tsum.append(em.cop("t", lambda e, kk=kk: e.matmul(
                PO1[0:64, 256:512], SUM2[:, kk, :], FH[:, kk, :],
                start=(kk == 0), stop=(kk == 7)), waits=trl + [rdt.get("PO1")]))
        em.mb()
        tg = em.cop("s", lambda e: e.activation(FG[:], PO1[0:64, 256:512], AF.Identity,
                                                bias=B2[:]), waits=tsum)
        rdt["PO1"] = tg
        em.mb()
        tw = []
        for m in range(8):
            pd = PA if m < 4 else PB
            off = (m % 4) * 256
            tw.append(em.cop("t", lambda e, m=m, pd=pd, off=off: e.matmul(
                pd[:, off:off + 256], WOUT[:, m * 128:(m + 1) * 128], FG[:],
                start=True, stop=False), waits=[tg]))
            tw.append(em.cop("t", lambda e, m=m, pd=pd, off=off: e.matmul(
                pd[:, off:off + 256], BOUTR[:, m, :], ONESR2[:], start=False, stop=True)))
        em.mb()
        outs = []
        for half, pd in [(0, PA), (1, PB)]:
            outs.append(em.cop("v", lambda e, half=half, pd=pd, x_sb=x_sb, on=out_t:
                               e.scalar_tensor_tensor(
                on[:, half * 4:(half + 1) * 4, :].rearrange("p a b -> p (a b)"),
                pd[:, 0:1024], 0.5,
                x_sb[:, half * 4:(half + 1) * 4, :].rearrange("p a b -> p (a b)"),
                ALU.mult, ALU.add), waits=tw))
        rdt["PA"], rdt["PB"] = outs[0], outs[1]
        return outs

    def ln_to_ubf(x_sb, mcol, sqcol, rstdcol, bfb, sqb, ubf, waits=()):
        em.cop("v", lambda e, x=x_sb, bfb=bfb: e.tensor_copy(
            bfb[:].rearrange("p a b -> p (a b)"), x[:].rearrange("p a b -> p (a b)")),
            waits=list(waits) + [rdt.get("ST_MM")])
        em.cop("s", lambda e, bfb=bfb, sqb=sqb: e.activation(
            sqb[:].rearrange("p a b -> p (a b)"), bfb[:].rearrange("p a b -> p (a b)"),
            AF.Square), waits=[em.tok("v"), rdt.get("XQB")])
        em.mb()
        if lncut < 2:
            return
        st_t = stats_cols(bfb, sqb, mcol, sqcol, [em.tok("s")])
        if lncut < 3:
            return
        rs_t = rstd_col(mcol, sqcol, rstdcol, CEPS5, st_t)
        if lncut < 4:
            return
        fl = flush_sc(mcol, mcol + 1, st_t) + flush_sc(rstdcol, rstdcol + 1, [rs_t])
        em.mb()
        if lncut < 5:
            return
        mb_, tkm = bcast(mcol, fl)
        if lncut < 6:
            return
        em.cop("v", lambda e, x=x_sb, mb_=mb_, ubf=ubf: e.tensor_tensor(
            ubf[:].rearrange("p a b -> p (a b)"), x[:].rearrange("p a b -> p (a b)"),
            AP(mb_.tensor, mb_.offset, [mb_.ap[0], [0, 8], mb_.ap[1]]), ALU.subtract),
            waits=[tkm])
        rb, tkr = bcast(rstdcol, fl)
        em.cop("v", lambda e, rb=rb: e.tensor_copy(MBC[:, 0, :], rb), waits=[tkr])
        em.mb()

    NW = {0: NQ, 1: NK, 2: NV, 3: NG, 4: NGK1}
    ICTOT = [0]
    t_xa_free = None
    em.pipe = True
    for c in range(nchunks):
        txa = em.dma("d", lambda e, c=c: e.dma_start(
            out=XA[:],
            in_=xT.rearrange("(k p) t -> p k t", p=128)[:, :, c * T:(c + 1) * T]),
            waits=[t_xa_free] if t_xa_free else [])
        em.mb()
        if p1cut < -2:
            em.barrier()
            continue
        ln_to_ubf(XA, 0, 1, 2, QF, XQB, KF, waits=[txa])
        if p1cut < -1:
            em.barrier()
            continue
        emit_ffn(MBC[0:64, 0, :], XA, XP, KF, [em.tok("v")])
        em.mb()
        if dbg and c == 0:
            em.dma("d", lambda e: e.dma_start(out=dbg_out["d_ub"], in_=KF[:].rearrange("p a b -> p (a b)")))
            em.dma("d", lambda e: e.dma_start(out=dbg_out["d_fs"], in_=FSF[:]))
            em.dma("d", lambda e: e.dma_start(out=dbg_out["d_fh"], in_=FH[:].rearrange("p a b -> p (a b)")))
            em.dma("d", lambda e: e.dma_start(out=dbg_out["d_fg"], in_=FGF[:]))
            em.dma("d", lambda e: e.dma_start(out=dbg_out["d_sc"], in_=SC[:].rearrange("p a b -> p (a b)")))
            em.dma("d", lambda e: e.dma_start(out=dbg_out["d_mbc"], in_=MBC[:].rearrange("p a b -> p (a b)")))
            em.barrier()
        if p1cut < 0:
            em.barrier()
            continue
        em.dma("d", lambda e, c=c: e.dma_start(
            out=xp_st[c], in_=XP[:].rearrange("p a b -> p (a b)")), waits=[em.tok("v")])
        if p1cut < 1:
            em.barrier()
            continue
        em.cop("v", lambda e: e.tensor_copy(QF[:].rearrange("p a b -> p (a b)"),
                                            XP[:].rearrange("p a b -> p (a b)")),
               waits=[rdt.get("ST_MM")])
        em.cop("s", lambda e: e.activation(XQB[:].rearrange("p a b -> p (a b)"),
                                           QF[:].rearrange("p a b -> p (a b)"), AF.Square),
               waits=[em.tok("v"), rdt.get("XQB"), rdt.get("ST_MM")])
        em.mb()
        st_t = stats_cols(QF, XQB, 4, 5, [em.tok("s")])
        rs_t = rstd_col(4, 5, 6, CEPS5, st_t)
        for tt in range(2):
            em.cop("v", lambda e, tt=tt: e.tensor_mul(COLS[:, 44:45], SC[:, tt, 4:5],
                                                      SC[:, tt, 4:5]), waits=[rs_t])
            em.cop("v", lambda e, tt=tt: e.tensor_sub(COLS[:, 45:46], SC[:, tt, 5:6],
                                                      COLS[:, 44:45]))
            em.cop("v", lambda e, tt=tt: e.tensor_scalar(COLS[:, 46:47], COLS[:, 45:46],
                                                         1e-5, None, ALU.add))
            em.cop("v", lambda e, tt=tt: e.reciprocal(COLS[:, 47:48], COLS[:, 46:47]))
            em.cop("v", lambda e, tt=tt: e.tensor_mul(SC[:, tt, 28:29], COLS[:, 47:48],
                                                      COLS[:, 45:46]))
        rsqrt_cols(28, 40, 1e-8)
        for tt in range(2):
            em.cop("v", lambda e, tt=tt: e.tensor_mul(SC[:, tt, 7:8], SC[:, tt, 6:7],
                                                      SC[:, tt, 28:29]))
        fl2 = flush_sc(4, 5, st_t)
        em.mb()
        m2b, tk2 = bcast(4, fl2)
        em.cop("v", lambda e, m2b=m2b: e.tensor_tensor(
            U1[:].rearrange("p a b -> p (a b)"), XP[:].rearrange("p a b -> p (a b)"),
            AP(m2b.tensor, m2b.offset, [m2b.ap[0], [0, 8], m2b.ap[1]]), ALU.subtract),
            waits=[tk2])
        em.mb()

        if p1cut < 2:
            em.barrier()
            continue
        for si in range(5):
            nten = NW[si]
            tw = em.cop("v", lambda e, nten=nten: e.tensor_tensor(
                WSCR[:].rearrange("p a b -> p (a b)"),
                U1[:].rearrange("p a b -> p (a b)"),
                AP(nten, 0, [[8, 128], [1, 8], [0, T]]), ALU.mult))
            rowmax_col(WSCR[:], 8, 8 + si, [tw])
            em.mb()
        for tt in range(2):
            em.cop("v", lambda e, tt=tt: e.scalar_tensor_tensor(
                SC[:, tt, 13:18], SC[:, tt, 8:13], 1.0,
                AP(SC, tt * 32 + 7, [[64, 128], [0, 5]]), ALU.mult, ALU.mult))
            em.cop("v", lambda e, tt=tt: e.tensor_scalar(SC[:, tt, 13:18],
                                                         SC[:, tt, 13:18],
                                                         1e-5, None, ALU.max))
            em.cop("v", lambda e, tt=tt: e.reciprocal(SC[:, tt, 18:23], SC[:, tt, 13:18]))
            em.cop("v", lambda e, tt=tt: e.scalar_tensor_tensor(
                SC[:, tt, 8:13], SC[:, tt, 18:23], 127.0,
                AP(SC, tt * 32 + 7, [[64, 128], [0, 5]]), ALU.mult, ALU.mult))
            em.cop("v", lambda e, tt=tt: e.scalar_tensor_tensor(
                SC[:, tt, 23:26], SC[:, tt, 14:17], 1.0 / 127.0, MW[:, 1:4],
                ALU.mult, ALU.mult))
        fl3 = flush_sc(8, 13, [em.tok("v")]) + flush_sc(23, 26, [em.tok("v")])
        em.mb()

        def quantize(si):
            nten = NW[si]
            phib, tkb = bcast(8 + si, fl3)
            em.cop("v", lambda e, nten=nten: e.tensor_tensor(
                QSCR[:].rearrange("p a b -> p (a b)"),
                U1[:].rearrange("p a b -> p (a b)"),
                AP(nten, 0, [[8, 128], [1, 8], [0, T]]), ALU.mult))
            em.cop("v", lambda e, phib=phib: e.tensor_tensor(
                QSCR[:].rearrange("p a b -> p (a b)"),
                QSCR[:].rearrange("p a b -> p (a b)"),
                AP(phib.tensor, phib.offset, [phib.ap[0], [0, 8], phib.ap[1]]),
                ALU.mult), waits=[tkb])
            for hf in range(2):
                em.cop("v", lambda e, hf=hf: e.tensor_scalar(
                    XI[:].rearrange("p a b -> p (a b)"),
                    QSCR[:, hf * 4:(hf + 1) * 4, :].rearrange("p a b -> p (a b)"),
                    RC, RC, ALU.add, ALU.subtract), waits=[em.tok("s")])
                em.cop("s", lambda e, hf=hf: e.activation(
                    XQB[:, hf * 4:(hf + 1) * 4, :].rearrange("p a b -> p (a b)"),
                    XI[:].rearrange("p a b -> p (a b)"), AF.Copy),
                    waits=[em.tok("v"), rdt.get("XQB"), rdt.get("ST_MM")])
            return em.tok("s")

        if p1cut < 3:
            em.barrier()
            continue
        tq = quantize(0)
        tqm = []
        for m in range(8):
            pd = PA if m < 4 else PB
            off = (m % 4) * 256
            for k in range(8):
                tqm.append(em.cop("t", lambda e, m=m, k=k, pd=pd, off=off: e.matmul(
                    pd[:, off:off + 256], WQ[:, k, m * 128:(m + 1) * 128], XQB[:, k, :],
                    start=(k == 0), stop=(k == 7)),
                    waits=[tq, rdt.get("PA" if m < 4 else "PB")]))
        for half, pd in [(0, PA), (1, PB)]:
            rdt["PA" if half == 0 else "PB"] = em.cop(
                "s", lambda e, half=half, pd=pd: e.activation(
                    QF[:, half * 4:(half + 1) * 4, :].rearrange("p a b -> p (a b)"),
                    pd[:, 0:1024], AF.Copy), waits=tqm)
        rdt["XQB"] = tqm[-1]
        em.mb()
        if p1cut < 4:
            em.barrier()
            continue
        tq = quantize(1)
        tkm = []
        for m in range(8):
            pd = PA if m < 4 else PB
            off = (m % 4) * 256
            for k in range(8):
                tkm.append(em.cop("t", lambda e, m=m, k=k, pd=pd, off=off: e.matmul(
                    pd[:, off:off + 256], WK[:, k, m * 128:(m + 1) * 128], XQB[:, k, :],
                    start=(k == 0), stop=(k == 7)),
                    waits=[tq, rdt.get("PA" if m < 4 else "PB")]))
        for half, pd in [(0, PA), (1, PB)]:
            rdt["PA" if half == 0 else "PB"] = em.cop(
                "s", lambda e, half=half, pd=pd: e.activation(
                    KF[:, half * 4:(half + 1) * 4, :].rearrange("p a b -> p (a b)"),
                    pd[:, 0:1024], AF.Copy), waits=tkm)
        rdt["XQB"] = tkm[-1]
        em.mb()
        if p1cut < 5:
            em.barrier()
            continue
        tq = quantize(2)
        wst_rd = {0: None, 1: None}
        for u in range(8):
            wslot = u % 2
            twl = em.dma("pd", lambda e, u=u, wslot=wslot: e.dma_start(
                out=WST[:, wslot, :, :],
                in_=wvq_st[:, :, u * 256:(u + 1) * 256]),
                waits=[wst_rd[wslot] or em.tok("t")])
            for tt in range(2):
                pd = [PA, PB][tt]
                pdn = "PA" if tt == 0 else "PB"
                tvm = []
                for k in range(8):
                    tvm.append(em.cop("t", lambda e, tt=tt, k=k, pd=pd, wslot=wslot:
                                      e.matmul(pd[:, 0:256],
                                               XQB[:, k, tt * 128:(tt + 1) * 128],
                                               WST[:, wslot, k, :],
                                               start=(k == 0), stop=(k == 7)),
                                      waits=[tq, twl, rdt.get(pdn)]))
                rdt[pdn] = em.cop("s", lambda e, tt=tt, u=u, pd=pd: e.activation(
                    VT[:, tt, u * 256:(u + 1) * 256], pd[:, 0:256], AF.Identity,
                    scale=SC[:, tt, 24:25]), waits=tvm)
                rdt["XQB"] = tvm[-1]
                wst_rd[wslot] = tvm[-1]
            em.mb()
        if p1cut < 6:
            em.barrier()
            continue
        tq = quantize(4)
        ti1 = []
        for k in range(8):
            ti1.append(em.cop("t", lambda e, k=k: e.matmul(
                PO1[0:16, 0:256], WGK1[:, k, :], XQB[:, k, :],
                start=(k == 0), stop=(k == 7)), waits=[tq, rdt.get("PO1")]))
        rdt["PO1"] = em.cop("v", lambda e: e.tensor_copy(I1[:], PO1[0:16, 0:256]),
                            waits=ti1)
        em.cop("v", lambda e: e.tensor_copy(I1B[:], I1[:]))
        rdt["XQB"] = ti1[-1]
        em.mb()
        if p1cut < 7:
            em.barrier()
            continue
        tq = quantize(3)
        psigb, tkg = bcast(25, fl3)
        em.cop("v", lambda e, psigb=psigb: e.tensor_copy(MBC[:, 1, :], psigb),
               waits=[tkg])
        wst_rd = {0: None, 1: None}
        for u in range(8):
            wslot = u % 2
            twl = em.dma("pd", lambda e, u=u, wslot=wslot: e.dma_start(
                out=WST[:, wslot, :, :],
                in_=wgq_st[:, :, u * 256:(u + 1) * 256]),
                waits=[wst_rd[wslot] or em.tok("t")])
            for mm in range(2):
                m = u * 2 + mm
                pd = [PA, PB][mm]
                pdn = "PA" if mm == 0 else "PB"
                tgm = []
                for k in range(8):
                    tgm.append(em.cop("t", lambda e, mm=mm, k=k, pd=pd, wslot=wslot:
                                      e.matmul(pd[:, 0:256],
                                               WST[:, wslot, k, mm * 128:(mm + 1) * 128],
                                               XQB[:, k, :],
                                               start=(k == 0), stop=(k == 7)),
                                      waits=[tq, twl, rdt.get(pdn)]))
                rdt["XQB"] = tgm[-1]
                wst_rd[wslot] = tgm[-1]
                tgm.append(em.cop("v", lambda e, m=m, pd=pd: e.tensor_tensor(
                    GFB[:, m, :], pd[:, 0:256], MBC[:, 1, :], ALU.mult), waits=tgm))
                rdt[pdn] = tgm[-1]
            em.mb()
        if p1cut < 8:
            em.barrier()
            continue
        em.cop("s", lambda e: e.activation(SQ16[:], I1B[:], AF.Square),
               waits=[em.tok("v")])
        t0 = [em.cop("t", lambda e: e.matmul(PA[0:1, 0:256], ONESC[0:16, :], SQ16[:],
                                             start=True, stop=True),
                     waits=[em.tok("s"), rdt.get("PA")])]
        tr = em.cop("v", lambda e: e.tensor_scalar(SROW[0:1, 0:256], PA[0:1, 0:256],
                                                   1.0 / 16.0, None, ALU.mult), waits=t0)
        rdt["PA"] = tr
        if g2cut < 2:
            em.barrier()
            continue
        for tt in range(2):
            tk = em.cop("t", lambda e, tt=tt: e.transpose(
                PO2[0:128, 448 + tt:449 + tt], SROW[0:1, tt * 128:(tt + 1) * 128],
                IDENTF[0:1, 0:1]), waits=[tr, rdt.get("PO2")])
            rdt["SROW"] = tk
            rdt["PO2"] = em.cop("v", lambda e, tt=tt: e.tensor_copy(
                SC[:, tt, 26:27], PO2[0:128, 448 + tt:449 + tt]), waits=[tk])
        em.cop("s", lambda e: e.activation(SC[:, :, 26], SC[:, :, 26], AF.Ln,
                                           bias=CEPS8[:]), waits=[em.tok("v")])
        em.cop("s", lambda e: e.activation(SC[:, :, 26], SC[:, :, 26], AF.Exp, scale=-0.5))
        if g2cut < 3:
            em.barrier()
            continue
        em.cop("v", lambda e: e.tensor_scalar(I1[:], I1[:], NGK2[:], None, ALU.mult),
               waits=[em.tok("s")])
        em.cop("v", lambda e: e.tensor_copy(AP(TMRB, 0, [[256, 16], [1, 256]]), I1[:]),
               waits=[rdt.get("TMRB")])
        for tt in range(2):
            tk = em.cop("t", lambda e, tt=tt: e.transpose(
                PTR[:, tt * 128:tt * 128 + 16],
                AP(TMRB, tt * 128, [[256, 16], [1, 128]]), IDENT[0:16, 0:16]),
                waits=[em.tok("v"), rdt.get("PTR")])
            rdt["TMRB"] = tk
            rdt["PTR"] = em.cop("v", lambda e, tt=tt: e.reduce_max(
                SC[:, tt, 27:28], PTR[:, tt * 128:tt * 128 + 16], axis=AX.X,
                apply_absolute_value=True), waits=[tk])
        if g2cut < 4:
            em.barrier()
            continue
        for tt in range(2):
            em.cop("v", lambda e, tt=tt: e.tensor_mul(SC[:, tt, 27:28], SC[:, tt, 27:28],
                                                      SC[:, tt, 26:27]))
            em.cop("v", lambda e, tt=tt: e.tensor_scalar(SC[:, tt, 27:28],
                                                         SC[:, tt, 27:28],
                                                         1e-5, None, ALU.max))
            em.cop("v", lambda e, tt=tt: e.reciprocal(COLS[:, 44:45], SC[:, tt, 27:28]))
            em.cop("v", lambda e, tt=tt: e.scalar_tensor_tensor(
                SC[:, tt, 19:20], COLS[:, 44:45], 127.0, SC[:, tt, 26:27],
                ALU.mult, ALU.mult))
            em.cop("v", lambda e, tt=tt: e.scalar_tensor_tensor(
                SC[:, tt, 20:21], SC[:, tt, 27:28], 1.0 / 127.0, MW[:, 7:8],
                ALU.mult, ALU.mult))
        fl4 = flush_sc(19, 21, [em.tok("v")])
        em.mb()
        if g2cut < 5:
            em.barrier()
            continue
        phi2b, tka = bcast(19, fl4)
        em.cop("v", lambda e, phi2b=phi2b: e.tensor_tensor(I1[:], I1[:], phi2b[0:16, :],
                                                           ALU.mult), waits=[tka])
        em.cop("v", lambda e: e.tensor_scalar(XI[0:16, 0, 0:256], I1[:], RC, RC,
                                              ALU.add, ALU.subtract),
               waits=[em.tok("s")])
        em.cop("s", lambda e: e.activation(I1B[:], XI[0:16, 0, 0:256], AF.Copy),
               waits=[em.tok("v")])
        if g2cut < 6:
            em.barrier()
            continue
        psi2b, tkb2 = bcast(20, fl4)
        em.cop("v", lambda e, psi2b=psi2b: e.tensor_copy(MBC[:, 1, :], psi2b),
               waits=[tkb2])
        tz = []
        for m in range(8):
            pd = PA if m < 4 else PB
            off = (m % 4) * 256
            if g2cut < 7:
                continue
            tz.append(em.cop("t", lambda e, m=m, pd=pd, off=off: e.matmul(
                pd[:, off:off + 256], WGK2[:, m * 128:(m + 1) * 128], I1B[:],
                start=True, stop=True),
                waits=[em.tok("s"), rdt.get("PA" if m < 4 else "PB")]))
        for m in range(8):
            pd = PA if m < 4 else PB
            off = (m % 4) * 256
            if g2cut < 8:
                continue
            tz.append(em.cop("v", lambda e, m=m, pd=pd, off=off: e.tensor_tensor(
                U1[:, m, :], pd[:, off:off + 256], MBC[:, 1, :], ALU.mult), waits=tz))
            rdt["PA" if m < 4 else "PB"] = tz[-1]
            if g2cut < 9:
                continue
            tz.append(em.cop("p", lambda e, m=m: e.tensor_scalar(
                U1[:, m, :], U1[:, m, :], BGK2[:, m:m + 1], None, ALU.add), waits=tz))
        em.mb()
        if p1cut < 9:
            em.barrier()
            continue
        em.cop("s", lambda e: e.activation(WSCR[:].rearrange("p a b -> p (a b)"),
                                           U1[:].rearrange("p a b -> p (a b)"),
                                           AF.Abs), waits=[em.tok("p"), em.tok("v")])
        em.cop("s", lambda e: e.activation(WSCR[:].rearrange("p a b -> p (a b)"),
                                           WSCR[:].rearrange("p a b -> p (a b)"),
                                           AF.Exp, scale=-1.0))
        em.cop("s", lambda e: e.activation(WSCR[:].rearrange("p a b -> p (a b)"),
                                           WSCR[:].rearrange("p a b -> p (a b)"),
                                           AF.Ln, bias=1.0))
        em.cop("v", lambda e: e.tensor_scalar(QSCR[:].rearrange("p a b -> p (a b)"),
                                              U1[:].rearrange("p a b -> p (a b)"),
                                              0.0, 1.0 / 16.0, ALU.min, ALU.mult),
               waits=[em.tok("p")])
        em.cop("v", lambda e: e.scalar_tensor_tensor(
            GL[:].rearrange("p a b -> p (a b)"),
            WSCR[:].rearrange("p a b -> p (a b)"), -1.0 / 16.0,
            QSCR[:].rearrange("p a b -> p (a b)"), ALU.mult, ALU.add),
            waits=[em.tok("s")])
        t_xa_free = em.tok("v")
        em.mb()
        if p1cut < 10:
            em.barrier()
            continue
        em.cop("s", lambda e: e.activation(THB[:].rearrange("p a b -> p (a b)"),
                                           GFB[:].rearrange("p a b -> p (a b)"),
                                           AF.Tanh, scale=0.5), waits=[em.tok("v")])
        em.cop("p", lambda e: e.tensor_scalar(THB[:].rearrange("p a b -> p (a b)"),
                                              THB[:].rearrange("p a b -> p (a b)"),
                                              0.5, 0.5, ALU.mult, ALU.add),
               waits=[em.tok("s")])
        em.cop("v", lambda e: e.tensor_tensor(SGB[:].rearrange("p a b -> p (a b)"),
                                              THB[:].rearrange("p a b -> p (a b)"),
                                              GFB[:].rearrange("p a b -> p (a b)"),
                                              ALU.mult), waits=[em.tok("p")])
        em.mb()

        if p1cut < 11:
            em.barrier()
            continue
        for ic in range(icpc):
            gic = ICTOT[0]
            ICTOT[0] += 1
            tsl = slice(ic * C, (ic + 1) * C)
            tsc = []
            for k in range(8):
                tsc.append(em.cop("v", lambda e, k=k, tsl=tsl: e.tensor_tensor_scan(
                    BBs(k), GL[:, k, tsl], ZEROT[:], 0.0, ALU.add, ALU.bypass)))
            em.cop("v", lambda e: e.tensor_copy(
                COLS[:, 0:8], AP(QSCR, C - 1, [[8 * T, 128], [T, 8]])),
                waits=tsc + [rdt.get("COLS")])
            em.cop("v", lambda e: e.tensor_copy(
                COLS[:, 8:16], AP(QSCR, C // 2 - 1, [[8 * T, 128], [T, 8]])))
            em.cop("v", lambda e: e.tensor_scalar(COLS[:, 16:24], COLS[:, 8:16],
                                                  -1.0, None, ALU.mult))
            em.cop("v", lambda e, gic=gic: e.tensor_copy(
                AP(LTAB, gic % 32, [[8 * 32, 128], [32, 8]]), COLS[:, 24:32]))
            em.cop("v", lambda e: e.tensor_add(COLS[:, 24:32], COLS[:, 24:32],
                                               COLS[:, 0:8]))
            em.cop("v", lambda e: e.tensor_sub(COLS[:, 32:40], COLS[:, 0:8],
                                               COLS[:, 8:16]))
            em.cop("s", lambda e: e.activation(ELT[:], COLS[:, 0:8], AF.Exp),
                   waits=[em.tok("v"), rdt.get("ELT")])
            em.cop("s", lambda e: e.activation(COLS[:, 8:16], COLS[:, 8:16], AF.Exp))
            em.cop("s", lambda e: e.activation(COLS[:, 32:40], COLS[:, 32:40], AF.Exp))
            rdt["COLS"] = em.tok("s")
            em.cop("v", lambda e: e.tensor_tensor(
                BS3, BB3, AP(COLS, 16, [[48, 128], [1, 8], [0, C]]), ALU.add))
            em.cop("s", lambda e: e.activation(E1[:], BS3, AF.Exp),
                   waits=[em.tok("v")])
            em.cop("s", lambda e: e.activation(E2[:], BS3, AF.Exp, scale=-1.0))
            em.mb()
            em.cop("v", lambda e, tsl=tsl: e.tensor_tensor(
                QS[:], QF[:, :, tsl], E1[:], ALU.mult),
                waits=[em.tok("s"), rdt.get("GLA_T"), rdt.get("QI")])
            em.cop("v", lambda e, tsl=tsl: e.tensor_tensor(
                KS[:], KF[:, :, tsl], E2[:], ALU.mult))
            em.cop("v", lambda e: e.tensor_tensor(
                QI[:].rearrange("p a b -> p (a b)"),
                QS[:].rearrange("p a b -> p (a b)"),
                AP(COLS, 8, [[48, 128], [1, 8], [0, C]]), ALU.mult))
            em.cop("v", lambda e: e.tensor_tensor(
                KX[:].rearrange("p a b -> p (a b)"),
                KS[:].rearrange("p a b -> p (a b)"),
                AP(COLS, 32, [[48, 128], [1, 8], [0, C]]), ALU.mult))
            tqs = em.tok("v")
            em.mb()
            rdt["QI"] = em.dma("d", lambda e, gic=gic: e.dma_start(
                out=qt_st[gic], in_=QI[:].rearrange("p a b -> p (a b)")),
                waits=[tqs])
            for h in range(H):
                po = [PO1, PO2][h % 2]
                pon = "PO1" if h % 2 == 0 else "PO2"
                ts16 = em.cop("p", lambda e, h=h: e.tensor_copy(
                    S16[:].rearrange("p a b -> p (a b)"),
                    S32[:, 2 * h:2 * h + 2, :].rearrange("p a b -> p (a b)")),
                    waits=[rdt.get("S16"), rdt.get(("S32", h))])
                ta = []
                for kk in range(2):
                    ta.append(em.cop("t", lambda e, h=h, kk=kk: e.matmul(
                        PA[:, 0:128], KS[:, 2 * h + kk, :], QS[:, 2 * h + kk, :],
                        start=(kk == 0), stop=(kk == 1)),
                        waits=[rdt.get("PA"), tqs]))
                rdt["PA"] = em.cop("v", lambda e: e.tensor_tensor(
                    ABF[:], PA[:, 0:128], TRIU[:], ALU.mult), waits=ta)
                em.cop("v", lambda e, ic=ic: e.tensor_scalar(
                    AB[:], ABF[:], SC[:, ic, 23:24], None, ALU.mult))
                tab = em.tok("v")
                to = []
                for j in range(4):
                    to.append(em.cop("t", lambda e, h=h, j=j, ic=ic, po=po: e.matmul(
                        po[:, j * 128:(j + 1) * 128],
                        VT[:, ic, h * 512 + j * 128:h * 512 + (j + 1) * 128],
                        AB[:], start=True, stop=False),
                        waits=[tab, ts16, rdt.get(pon)]))
                    for kk in range(2):
                        to.append(em.cop("t", lambda e, h=h, j=j, kk=kk, po=po: e.matmul(
                            po[:, j * 128:(j + 1) * 128],
                            S16[:, kk, j * 128:(j + 1) * 128],
                            QI[:, 2 * h + kk, :], start=False, stop=(kk == 1))))
                rdt["S16"] = to[-1]
                rdt[pon] = em.cop("v", lambda e, h=h, po=po, tsl=tsl: e.tensor_tensor(
                    OGB[:, 4 * h:4 * h + 4, tsl],
                    po[:, 0:512].rearrange("p (a b) -> p a b", a=4),
                    SGB[:, 4 * h:4 * h + 4, tsl], ALU.mult), waits=to)
                tkt = []
                for kk in range(2):
                    tkt.append(em.cop("t", lambda e, h=h, kk=kk: e.transpose(
                        PTR[:, kk * 128:(kk + 1) * 128], KX[:, 2 * h + kk, :],
                        IDENT[:]), waits=[rdt.get("PTR")]))
                for kk in range(2):
                    tkt.append(em.cop("v", lambda e, kk=kk, ic=ic: e.tensor_scalar(
                        KXT[:, kk, :], PTR[:, kk * 128:(kk + 1) * 128],
                        SC[:, ic, 23:24], None, ALU.mult), waits=tkt))
                rdt["PTR"] = tkt[-1]
                tu = []
                for kk in range(2):
                    tu.append(em.cop("t", lambda e, h=h, kk=kk, ic=ic: e.matmul(
                        PB[:, kk * 512:(kk + 1) * 512], KXT[:, kk, :],
                        VT[:, ic, h * 512:(h + 1) * 512], start=True, stop=True),
                        waits=tkt + [rdt.get("PB")]))
                rdt["GLA_T"] = tu[-1]
                for kk in range(2):
                    rdt["PB"] = rdt[("S32", h)] = rdt["ELT"] = em.cop(
                        "v", lambda e, h=h, kk=kk: e.scalar_tensor_tensor(
                            S32[:, 2 * h + kk, :], S32[:, 2 * h + kk, :],
                            ELT[:, 2 * h + kk:2 * h + kk + 1],
                            PB[:, kk * 512:(kk + 1) * 512], ALU.mult, ALU.add),
                        waits=tu + [ts16])
                em.mb()
        em.dma("d", lambda e, c=c: e.dma_start(
            out=og_st[c], in_=OGB[:].rearrange("p a b -> p (a b)")),
            waits=[em.tok("v")])
        em.dma("d", lambda e, c=c: e.dma_start(
            out=sg_st[c], in_=SGB[:].rearrange("p a b -> p (a b)")))
        if dbg and c == 0:
            for nm, src in [("d_xp", XP), ("d_q", QF), ("d_k", KF), ("d_gl", GL),
                            ("d_og", OGB), ("d_v", VT)]:
                em.dma("d", lambda e, nm=nm, src=src: e.dma_start(
                    out=dbg_out[nm], in_=src[:].rearrange("p a b -> p (a b)")))
        em.barrier()
    em.pipe = False

    if stage == 1:
        em.flush()
        ph1.close()
        return nc

    em.dma("d", lambda e: e.dma_start(
        out=AP(cc_in, 0, [[512, 128], [512 * 128, 8], [1, 512]]), in_=S32[:]))
    em.barrier()
    em.op("p", lambda e: e.collective_compute(
        "AllGather", ALU.bypass, replica_groups=[[0, 1], [2, 3], [4, 5], [6, 7]],
        ins=[cc_in.ap().opt()], outs=[cc_out.ap().opt()]), inc=1)
    em.barrier()
    em.dma("d", lambda e: e.dma_start(
        out=S32[:], in_=AP(cc_out, 0, [[512, 128], [512 * 128, 8], [1, 512]])))
    em.barrier()
    em.cop("v", lambda e: e.tensor_scalar(S32[:].rearrange("p a b -> p (a b)"),
                                          S32[:].rearrange("p a b -> p (a b)"),
                                          SMASK[:], None, ALU.mult))
    if dbg:
        em.dma("d", lambda e: e.dma_start(out=dbg_out["d_s32"],
                                          in_=S32[:].rearrange("p a b -> p (a b)")),
               waits=[em.tok("v")])
    em.barrier()
    em.flush()
    ph1.close()
    if stage == 2:
        return nc

    ph3 = contextlib.ExitStack()
    sb3 = lambda n, s, d=F32: ph3.enter_context(nc.sbuf_tensor(n, s, d))
    WO = sb3("WO", [128, 16, D], BF16)
    OF = sb3("OF", [128, 16, T])
    TMQ = sb3("TMQ", [128, 16, T])
    OQ = sb3("OQ", [128, 16, T], BF16)
    XI3 = sb3("XI3", [128, 8, T], I32)
    T16 = sb3("T16", [128, 2, 512], BF16)
    QTL = sb3("QTL", [128, 8, C], BF16)
    OGL = sb3("OGL", [128, 16, T], BF16)
    SGL = sb3("SGL", [128, 16, T], BF16)
    XPL = sb3("XPL", [128, 8, T])
    XP2 = sb3("XP2", [128, 8, T])
    OUTB = sb3("OUTB", [128, 8, T])
    TP3 = sb3("TP3", [128, 8, T])
    UB3 = sb3("UB3", [128, 8, T], BF16)

    em.dma("d", lambda e: e.dma_start(out=WO[:].rearrange("p a b -> p (a b)"),
                                      in_=woq_st.rearrange("p a b -> p (a b)")))
    em.barrier()

    for c in range(nchunks):
        tl = []
        tl.append(em.dma("d", lambda e, c=c: e.dma_start(
            out=OGL[:].rearrange("p a b -> p (a b)"), in_=og_st[c])))
        tl.append(em.dma("d", lambda e, c=c: e.dma_start(
            out=SGL[:].rearrange("p a b -> p (a b)"), in_=sg_st[c])))
        tl.append(em.dma("d", lambda e, c=c: e.dma_start(
            out=XPL[:].rearrange("p a b -> p (a b)"), in_=xp_st[c])))
        em.mb()
        for ic in range(icpc):
            gic = c * icpc + ic
            tsl = slice(ic * C, (ic + 1) * C)
            tq3 = em.dma("pd", lambda e, gic=gic: e.dma_start(
                out=QTL[:].rearrange("p a b -> p (a b)"), in_=qt_st[gic]))
            tse = em.cop("s", lambda e, gic=gic: e.activation(
                ELT[:], AP(LTAB, gic % 32, [[8 * 32, 128], [32, 8]]), AF.Exp))
            for h in range(H):
                po = [PO1, PO2][h % 2]
                for kk in range(2):
                    em.cop("v", lambda e, h=h, kk=kk: e.tensor_scalar(
                        T16[:, kk, :], S32[:, 2 * h + kk, :],
                        ELT[:, 2 * h + kk:2 * h + kk + 1], None, ALU.mult),
                        waits=[tse])
                tt16 = em.tok("v")
                tc = []
                for j in range(4):
                    for kk in range(2):
                        tc.append(em.cop("t", lambda e, h=h, j=j, kk=kk, po=po:
                                         e.matmul(
                            po[:, j * 128:(j + 1) * 128],
                            T16[:, kk, j * 128:(j + 1) * 128],
                            QTL[:, 2 * h + kk, :], start=(kk == 0), stop=(kk == 1)),
                            waits=[tt16, tq3]))
                em.cop("v", lambda e, h=h, po=po, tsl=tsl: e.tensor_tensor(
                    OF[:, 4 * h:4 * h + 4, tsl],
                    po[:, 0:512].rearrange("p (a b) -> p a b", a=4),
                    SGL[:, 4 * h:4 * h + 4, tsl], ALU.mult), waits=tc + tl)
                em.mb()
        em.cop("v", lambda e: e.tensor_tensor(OF[:].rearrange("p a b -> p (a b)"),
                                              OF[:].rearrange("p a b -> p (a b)"),
                                              OGL[:].rearrange("p a b -> p (a b)"),
                                              ALU.add))
        em.mb()
        em.cop("s", lambda e: e.activation(OGL[:].rearrange("p a b -> p (a b)"),
                                           OF[:].rearrange("p a b -> p (a b)"),
                                           AF.Square))
        t0 = []
        for k in range(16):
            t0.append(em.cop("t", lambda e, k=k: e.matmul(
                PA[0:1, 0:256], ONESC[:], OGL[:, k, :], start=(k == 0),
                stop=(k == 15)), waits=[em.tok("s")]))
        tr = em.cop("v", lambda e: e.tensor_scalar(SROW[0:1, 0:256], PA[0:1, 0:256],
                                                   1.0 / DV, None, ALU.mult), waits=t0)
        for tt in range(2):
            tk = em.cop("t", lambda e, tt=tt: e.transpose(
                PO2[0:128, 448 + tt:449 + tt], SROW[0:1, tt * 128:(tt + 1) * 128],
                IDENTF[0:1, 0:1]), waits=[tr])
            em.cop("v", lambda e, tt=tt: e.tensor_copy(
                SC[:, tt, 0:1], PO2[0:128, 448 + tt:449 + tt]), waits=[tk])
        rsqrt_cols(0, 40, 1e-8)
        tw3 = em.cop("v", lambda e: e.tensor_tensor(
            TMQ[:].rearrange("p a b -> p (a b)"), OF[:].rearrange("p a b -> p (a b)"),
            AP(NO, 0, [[16, 128], [1, 16], [0, T]]), ALU.mult), waits=[em.tok("s")])
        rowmax_col(TMQ[:], 16, 1, [tw3])
        for tt in range(2):
            em.cop("v", lambda e, tt=tt: e.tensor_mul(SC[:, tt, 1:2], SC[:, tt, 1:2],
                                                      SC[:, tt, 0:1]))
            em.cop("v", lambda e, tt=tt: e.tensor_scalar(SC[:, tt, 1:2], SC[:, tt, 1:2],
                                                         1e-5, None, ALU.max))
            em.cop("v", lambda e, tt=tt: e.reciprocal(COLS[:, 44:45], SC[:, tt, 1:2]))
            em.cop("v", lambda e, tt=tt: e.scalar_tensor_tensor(
                SC[:, tt, 3:4], COLS[:, 44:45], 127.0, SC[:, tt, 0:1],
                ALU.mult, ALU.mult))
            em.cop("v", lambda e, tt=tt: e.scalar_tensor_tensor(
                SC[:, tt, 4:5], SC[:, tt, 1:2], 1.0 / 127.0, MW[:, 6:7],
                ALU.mult, ALU.mult))
        em.cop("v", lambda e: e.tensor_copy(AP(TMRB, 0, [[256, 128], [1, 2]]),
                                            SC[:, :, 4]))
        em.cop("v", lambda e: e.tensor_copy(SC[:, :, 5], AP(TMRB, 0, [[256, 128], [1, 2]])))
        em.cop("v", lambda e: e.tensor_sub(SC[:, :, 6], SC[:, :, 4], SC[:, :, 5]))
        fl5 = flush_sc(3, 7, [em.tok("v")])
        em.mb()
        phiob, tko = bcast(3, fl5)
        em.cop("v", lambda e, phiob=phiob: e.tensor_tensor(
            TMQ[:].rearrange("p a b -> p (a b)"), TMQ[:].rearrange("p a b -> p (a b)"),
            AP(phiob.tensor, phiob.offset, [phiob.ap[0], [0, 16], phiob.ap[1]]),
            ALU.mult), waits=[tko])
        for half in range(2):
            em.cop("v", lambda e, half=half: e.tensor_scalar(
                XI3[:].rearrange("p a b -> p (a b)"),
                TMQ[:, half * 8:(half + 1) * 8, :].rearrange("p a b -> p (a b)"),
                RC, RC, ALU.add, ALU.subtract))
            em.cop("s", lambda e, half=half: e.activation(
                OQ[:, half * 8:(half + 1) * 8, :].rearrange("p a b -> p (a b)"),
                XI3[:].rearrange("p a b -> p (a b)"), AF.Copy), waits=[em.tok("v")])
            em.mb()
        s_ = bcslot[0]
        bcslot[0] = 1 - s_
        tp1 = em.cop("t", lambda e, s_=s_: e.matmul(
            PBC[:, s_ * 256:(s_ + 1) * 256], SELT[:, 5, :],
            SCR[:].rearrange("a b c -> a (b c)"), start=True, stop=False), waits=fl5)
        tp2 = em.cop("t", lambda e, s_=s_: e.matmul(
            PBC[:, s_ * 256:(s_ + 1) * 256], SELT[:, 6, :],
            SCR[:].rearrange("a b c -> a (b c)"), start=False, stop=True))
        em.cop("v", lambda e, s_=s_: e.tensor_copy(
            MBC[:, 0, :], PBC[:, s_ * 256:(s_ + 1) * 256]), waits=[tp1, tp2])
        two = []
        for m in range(8):
            pd = PA if m < 4 else PB
            off = (m % 4) * 256
            for k in range(16):
                two.append(em.cop("t", lambda e, m=m, k=k, pd=pd, off=off: e.matmul(
                    pd[:, off:off + 256], WO[:, k, m * 128:(m + 1) * 128], OQ[:, k, :],
                    start=(k == 0), stop=(k == 15)), waits=[em.tok("s")]))
        for half, pd in [(0, PA), (1, PB)]:
            em.cop("v", lambda e, half=half, pd=pd: e.tensor_tensor(
                TP3[:, half * 4:(half + 1) * 4, :].rearrange("p a b -> p (a b)"),
                pd[:, 0:1024],
                AP(MBC, 0, [[2 * T, 128], [0, 4], [1, T]]), ALU.mult), waits=two)
        em.cop("v", lambda e: e.tensor_add(XP2[:].rearrange("p a b -> p (a b)"),
                                           TP3[:].rearrange("p a b -> p (a b)"),
                                           XPL[:].rearrange("p a b -> p (a b)")))
        em.mb()
        ln_to_ubf(XP2, 8, 9, 10, OQ[:, 0:8, :], OQ[:, 8:16, :], UB3)
        emit_ffn(MBC[0:64, 0, :], XP2, OUTB, UB3, [em.tok("v")])
        if dbg and c == 0:
            em.dma("d", lambda e: e.dma_start(out=dbg_out["d_of"],
                                              in_=OF[:].rearrange("p a b -> p (a b)")),
                   waits=[em.tok("v")])
        em.mb()
        em.dma("d", lambda e, c=c: e.dma_start(
            out=yT.rearrange("(k p) t -> p k t", p=128)[:, :, c * T:(c + 1) * T],
            in_=OUTB[:]), waits=[em.tok("v")])
        em.barrier()

    em.flush()
    ph3.close()
    return nc


_CACHE = {}


def _host_inputs(inputs):
    f32 = np.float32
    wq = inputs["L0_attn_wq"]; wk = inputs["L0_attn_wk"]
    wv = inputs["L0_attn_wv"]; wg = inputs["L0_attn_wg"]
    wgk1 = inputs["L0_attn_wgk1"]; wgk2 = inputs["L0_attn_wgk2"]
    wo = inputs["L0_attn_wo"]
    W1 = np.asarray(inputs["L0_ffn_W1"], f32); b1 = np.asarray(inputs["L0_ffn_b1"], f32)
    W2 = np.asarray(inputs["L0_ffn_W2"], f32)
    bin_ = np.asarray(inputs["L0_ffn_bin_"], f32)
    com = {}
    com["wqT"] = np.ascontiguousarray(np.asarray(wq, f32).T)
    com["wkT"] = np.ascontiguousarray(np.asarray(wk, f32).T)
    com["wvT"] = np.ascontiguousarray(np.asarray(wv, f32).T)
    com["wgT"] = np.ascontiguousarray(np.asarray(wg, f32).T)
    com["wgk1T"] = np.ascontiguousarray(np.asarray(wgk1, f32).T)
    com["wgk2T"] = np.ascontiguousarray(np.asarray(wgk2, f32).T)
    com["woT"] = np.ascontiguousarray(np.asarray(wo, f32).T)
    arr = lambda v, nt: np.ascontiguousarray(np.asarray(v, f32).reshape(nt, 128).T)
    com["nq_a"] = arr(inputs["L0_attn_nq"], 8)
    com["nk_a"] = arr(inputs["L0_attn_nk"], 8)
    com["nv_a"] = arr(inputs["L0_attn_nv"], 8)
    com["ng_a"] = arr(inputs["L0_attn_ng"], 8)
    com["ngk1_a"] = arr(inputs["L0_attn_ngk1"], 8)
    com["ngk2_a"] = np.ascontiguousarray(np.asarray(inputs["L0_attn_ngk2"], f32).reshape(16, 1))
    com["bgk2_a"] = arr(inputs["L0_attn_bgk2"], 8)
    com["no_a"] = arr(inputs["L0_attn_no"], 16)
    com["Win"] = np.ascontiguousarray(np.asarray(inputs["L0_ffn_Win"], f32))
    com["WoutT"] = np.ascontiguousarray(np.asarray(inputs["L0_ffn_Wout"], f32))
    repw1 = np.zeros((64, 8, 128), f32)
    sumw2 = np.zeros((128, 8, 64), f32)
    w1a = np.zeros((128, 8), f32); b1a = np.zeros((128, 8), f32)
    bina = np.zeros((128, 8), f32)
    for k in range(8):
        for mloc in range(8):
            m = 8 * k + mloc
            for i in range(16):
                p = mloc * 16 + i
                repw1[m, k, p] = W1[m, i]
                sumw2[p, k, m] = W2[m, i]
                w1a[p, k] = W1[m, i]
                b1a[p, k] = b1[m, i]
                bina[p, k] = bin_[m]
    com["repw1"] = repw1; com["sumw2"] = sumw2
    com["w1_a"] = w1a; com["b1_a"] = b1a; com["binmi_a"] = bina
    com["b2_a"] = np.ascontiguousarray(np.asarray(inputs["L0_ffn_b2"], f32).reshape(64, 1))
    com["bout_r"] = np.ascontiguousarray(
        np.asarray(inputs["L0_ffn_bout"], f32).reshape(1, 8, 128).astype(NB))
    com["triu"] = np.ascontiguousarray(np.triu(np.ones((128, 128), np.float32)))
    com["ident"] = np.ascontiguousarray(np.eye(128, dtype=f32).astype(NB))
    com["identf"] = np.ascontiguousarray(np.eye(128, dtype=f32))
    com["ones_c"] = np.ones((128, 1), f32).astype(NB)
    com["ones_r2"] = np.ones((1, 256), f32).astype(NB)
    com["ones_cf"] = np.ones((128, 1), f32)
    com["ones_rf"] = np.ones((1, 128), f32)
    sel = np.zeros((32, 32, 128), f32)
    for j in range(32):
        sel[j, j, :] = 1.0
    com["sel"] = np.ascontiguousarray(sel.astype(NB))
    return com


def kernel(**inputs):
    x = np.asarray(inputs["L0_attn_hidden_states"], np.float32)
    B, L, Dm = x.shape
    half = L // 2
    com = _host_inputs(inputs)
    if "nc" not in _CACHE:
        _CACHE["nc"] = build_nc(16)
    nc = _CACHE["nc"]
    in_maps = []
    for core in range(8):
        b, s = core // 2, core % 2
        m = dict(com)
        m["xT"] = np.ascontiguousarray(x[b, s * half:(s + 1) * half, :].T)
        m["smask"] = np.full((128, 1), float(s), np.float32)
        in_maps.append(m)
    res = run_bass_kernel_spmd(nc, in_maps, core_ids=list(range(8)))
    out = np.empty_like(x)
    for core in range(8):
        b, s = core // 2, core % 2
        out[b, s * half:(s + 1) * half, :] = np.asarray(res.results[core]["yT"]).T
    return out


# version 04 (final)
# speedup vs baseline: 1.1235x; 1.0087x over previous
import numpy as np
import ml_dtypes
import contextlib
import concourse.bass as bass
import concourse.mybir as mybir
from concourse.bass_utils import run_bass_kernel_spmd

F32 = mybir.dt.float32
BF16 = mybir.dt.bfloat16
I32 = mybir.dt.int32
U8 = mybir.dt.uint8
AF = mybir.ActivationFunctionType
ALU = mybir.AluOpType
AX = mybir.AxisListType
AP = bass.AP
NB = ml_dtypes.bfloat16

RC = 12582912.0
D = 1024
DV = 2048
H = 4
T = 256
C = 128
SERIAL = False


class EM:
    ENGS = ("t", "v", "s", "p", "d")
    SEMK = ("t", "v", "s", "p", "d", "pd")

    def __init__(self, nc, stack):
        self.nc = nc
        self.q = {e: [] for e in self.ENGS}
        self.cnt = {e: 0 for e in self.SEMK}
        self.sems = {e: stack.enter_context(nc.semaphore("sem_" + e)) for e in self.SEMK}
        self.waited = {e: {e2: 0 for e2 in self.SEMK} for e in self.ENGS}

    def op(self, eng, fn, waits=(), inc=0):
        qk = "p" if eng == "pd" else eng
        wl = []
        if eng in ("v", "s", "p", "d", "pd") and self.cnt[eng] > 0:
            wl.append((self.sems[eng], self.cnt[eng]))
        for w in waits:
            if w is None:
                continue
            se, sv = w
            if se == eng:
                continue
            if self.waited[qk][se] < sv:
                self.waited[qk][se] = sv
                wl.append((self.sems[se], sv))
        sem = self.sems[eng]

        def run(e, wl=wl, fn=fn, inc=inc, sem=sem):
            for (s, v) in wl:
                e.wait_ge(s, v)
            ins = fn(e)
            if inc:
                ins.then_inc(sem, inc)
            return ins
        self.q[qk].append(run)
        if inc:
            self.cnt[eng] += inc
        return (eng, self.cnt[eng])

    def cop(self, eng, fn, waits=()):
        return self.op(eng, fn, waits=waits, inc=1)

    def dma(self, eng, fn, waits=()):
        return self.op(eng, fn, waits=waits, inc=16)

    def tok(self, eng):
        return (eng, self.cnt[eng])

    def barrier(self, engs=None):
        engs = engs or self.ENGS
        toks = [(e, self.cnt[e]) for e in (self.SEMK if engs is self.ENGS or engs == self.ENGS else engs)]
        for e in engs:
            for (se, sv) in toks:
                if se != e and self.waited[e][se] < sv:
                    self.waited[e][se] = sv
                    sem = self.sems[se]
                    self.q[e].append(lambda en, s=sem, v=sv: en.wait_ge(s, v))

    def mb(self):
        if SERIAL or not getattr(self, "pipe", False):
            self.barrier()

    def flush(self):
        with self.nc.Block() as blk:
            for ename, attr in [("t", "tensor"), ("v", "vector"), ("s", "scalar"),
                                ("p", "gpsimd"), ("d", "sync")]:
                ops = self.q[ename]
                if not ops:
                    continue
                getattr(blk, attr)(lambda e, ops=tuple(ops): [f(e) for f in ops])
        self.q = {e: [] for e in self.ENGS}


def build_nc(nchunks=16, dbg=False, stage=3, p1cut=99, lncut=99, fcut=99, g2cut=99):
    ntok = nchunks * T
    nic = ntok // C
    icpc = T // C
    import os as _os
    nc = bass.Bass("TRN2", target_bir_lowering=False,
                   detect_race_conditions=_os.environ.get("KRACE", "1") == "1")
    din = {}

    def inp(name, shape, dt=F32):
        din[name] = nc.dram_tensor(name, shape, dt, kind="ExternalInput").ap()
        return din[name]

    xT = inp("xT", [D, ntok])
    for nm, shp in [("wqT", [D, D]), ("wkT", [D, D]), ("wvT", [D, DV]),
                    ("wgT", [D, DV]), ("wgk1T", [D, 16]), ("wgk2T", [16, D]),
                    ("woT", [DV, D]), ("nq_a", [128, 8]), ("nk_a", [128, 8]),
                    ("nv_a", [128, 8]), ("ng_a", [128, 8]), ("ngk1_a", [128, 8]),
                    ("ngk2_a", [16, 1]), ("bgk2_a", [128, 8]), ("no_a", [128, 16]),
                    ("Win", [D, 64]), ("WoutT", [64, D]), ("repw1", [64, 8, 128]),
                    ("w1_a", [128, 8]), ("b1_a", [128, 8]), ("binmi_a", [128, 8]),
                    ("sumw2", [128, 8, 64]), ("b2_a", [64, 1]),
                    ("identf", [128, 128]), ("ones_cf", [128, 1]), ("ones_rf", [1, 128]),
                    ("smask", [128, 1])]:
        inp(nm, shp)
    inp("triu", [128, 128])
    inp("ident", [128, 128], BF16)
    inp("ones_c", [128, 1], BF16)
    inp("ones_r2", [1, 256], BF16)
    inp("sel", [32, 32, 128], BF16)
    inp("bout_r", [1, 8, 128], BF16)

    yT = nc.dram_tensor("yT", [D, ntok], F32, kind="ExternalOutput").ap()
    dbg_out = {}
    if dbg:
        for nm, shp, dt in [("d_rep", [128, 8 * 128], BF16), ("d_pa", [128, 1024], F32),
                            ("d_ub", [128, 8 * T], BF16), ("d_fs", [128, T], BF16),
                            ("d_fh", [128, 8 * T], BF16), ("d_fg", [128, T], BF16),
                            ("d_sc", [128, 2 * 32], F32), ("d_mbc", [128, 2 * T], F32),
                            ("d_xp", [128, 8 * T], F32), ("d_q", [128, 8 * T], BF16),
                            ("d_k", [128, 8 * T], BF16), ("d_gl", [128, 8 * T], BF16),
                            ("d_og", [128, 16 * T], BF16), ("d_of", [128, 16 * T], F32),
                            ("d_s32", [128, 8 * 512], F32), ("d_v", [128, 2 * 2048], BF16)]:
            dbg_out[nm] = nc.dram_tensor(nm, shp, dt, kind="ExternalOutput").ap()

    og_st = nc.dram_tensor("og_st", [nchunks, 128, 16 * T], BF16).ap()
    sg_st = nc.dram_tensor("sg_st", [nchunks, 128, 16 * T], BF16).ap()
    qt_st = nc.dram_tensor("qt_st", [nic, 128, 8 * C], BF16).ap()
    xp_st = nc.dram_tensor("xp_st", [nchunks, 128, 8 * T], F32).ap()
    wvq_st = nc.dram_tensor("wvq_st", [128, 8, DV], BF16).ap()
    wgq_st = nc.dram_tensor("wgq_st", [128, 8, DV], BF16).ap()
    woq_st = nc.dram_tensor("woq_st", [128, 16, D], BF16).ap()
    cc_in = nc.dram_tensor("cc_in", [1024, 512], F32)
    cc_out = nc.dram_tensor("cc_out", [2048, 512], F32)

    perm = contextlib.ExitStack()
    em = EM(nc, perm)
    sbp = lambda n, s, d=F32: perm.enter_context(nc.sbuf_tensor(n, s, d))

    WGK1 = sbp("WGK1", [128, 8, 16], BF16)
    WGK2F = sbp("WGK2F", [128, D], BF16)
    WGK2 = WGK2F[0:16, :]
    WIN = sbp("WIN", [128, 8, 64], BF16)
    WOUTF = sbp("WOUTF", [128, D], BF16)
    WOUT = WOUTF[0:64, :]
    REPF = sbp("REPF", [128, 8, 128], BF16)
    REP = REPF[0:64, :, :]
    SUM2 = sbp("SUM2", [128, 8, 64], BF16)
    B1E = sbp("B1E", [128, 8])
    NQ = sbp("NQ", [128, 8]); NK = sbp("NK", [128, 8]); NV = sbp("NV", [128, 8])
    NG = sbp("NG", [128, 8]); NGK1 = sbp("NGK1", [128, 8]); NGK2F = sbp("NGK2F", [128, 1]); NGK2 = NGK2F[0:16, :]
    BGK2 = sbp("BGK2", [128, 8]); NO = sbp("NO", [128, 16])
    B2F = sbp("B2F", [128, 1]); B2 = B2F[0:64, :]
    BOUTRF = sbp("BOUTRF", [128, 8, 128], BF16); BOUTR = BOUTRF[0:1, :, :]
    TRIU = sbp("TRIU", [128, 128])
    IDENT = sbp("IDENT", [128, 128], BF16)
    IDENTF = sbp("IDENTF", [128, 128], F32)
    ONESC = sbp("ONESC", [128, 1], BF16)
    ONESR2F = sbp("ONESR2F", [128, 256], BF16); ONESR2 = ONESR2F[0:1, :]
    ONESCF = sbp("ONESCF", [128, 1], F32)
    ONESRFF = sbp("ONESRFF", [128, 128], F32); ONESRF = ONESRFF[0:1, :]
    SELTF = sbp("SELTF", [128, 32, 128], BF16); SELT = SELTF[0:32, :, :]
    SMASK = sbp("SMASK", [128, 1])
    ZEROT = sbp("ZEROT", [128, 128])
    CEPS5 = sbp("CEPS5", [128, 1]); CEPS8 = sbp("CEPS8", [128, 1])
    MW = sbp("MW", [128, 8]); WSC = sbp("WSC", [128, 8])
    S32 = sbp("S32", [128, 8, 512])
    LTAB = sbp("LTAB", [128, 8, 32])
    COLS = sbp("COLS", [128, 48])
    SC = sbp("SC", [128, 2, 32])
    SCRF = sbp("SCRF", [128, 2, 128], BF16); SCR = SCRF[0:32, :, :]
    SROWF = sbp("SROWF", [128, 512]); SROW = SROWF[0:1, :]
    MBC = sbp("MBC", [128, 2, T])
    FSF = sbp("FSF", [128, T], BF16); FS = FSF[0:64, :]
    FH = sbp("FH", [128, 8, T], BF16)
    FGF = sbp("FGF", [128, T], BF16); FG = FGF[0:64, :]
    I1F = sbp("I1F", [128, T]); I1 = I1F[0:16, :]
    I1BF = sbp("I1BF", [128, T], BF16); I1B = I1BF[0:16, :]
    SQ16F = sbp("SQ16F", [128, T], BF16); SQ16 = SQ16F[0:16, :]
    TMRB = sbp("TMRB", [128, 2, 128], BF16)
    ELT = sbp("ELT", [128, 8])

    PA = perm.enter_context(nc.psum_tensor("PA", [128, 1024], F32))
    PB = perm.enter_context(nc.psum_tensor("PB", [128, 1024], F32))
    PO1 = perm.enter_context(nc.psum_tensor("PO1", [128, 512], F32))
    PO2 = perm.enter_context(nc.psum_tensor("PO2", [128, 512], F32))
    PBC = perm.enter_context(nc.psum_tensor("PBC", [128, 512], F32))
    PTR = perm.enter_context(nc.psum_tensor("PTR", [128, 256], BF16))

    ph1 = contextlib.ExitStack()
    sb1 = lambda n, s, d=F32: ph1.enter_context(nc.sbuf_tensor(n, s, d))
    WQ = sb1("WQ", [128, 8, D], BF16)
    WK = sb1("WK", [128, 8, D], BF16)
    WST = sb1("WST", [128, 2, 8, 256], BF16)

    stup = contextlib.ExitStack()
    sbs = lambda n, s, d=F32: stup.enter_context(nc.sbuf_tensor(n, s, d))
    STG = sbs("STG", [128, 2, 4096])
    QSC = sbs("QSC", [128, 4096])
    QSB = sbs("QSB", [128, 4096], BF16)
    FSTG = sbs("FSTG", [128, 8, 64])
    WOSF = sbs("WOSF", [128, D]); WOS = WOSF[0:64, :]
    RPSF = sbs("RPSF", [128, 8, 128]); RPS = RPSF[0:64, :, :]
    SMS = sbs("SMS", [128, 8, 64])

    for name, dst in [("nq_a", NQ), ("nk_a", NK), ("nv_a", NV), ("ng_a", NG),
                      ("ngk1_a", NGK1), ("ngk2_a", NGK2), ("bgk2_a", BGK2),
                      ("no_a", NO), ("b2_a", B2), ("bout_r", BOUTR),
                      ("triu", TRIU), ("ident", IDENT), ("identf", IDENTF),
                      ("ones_c", ONESC), ("ones_r2", ONESR2), ("ones_cf", ONESCF),
                      ("ones_rf", ONESRF), ("sel", SELT), ("smask", SMASK)]:
        em.dma("d", lambda e, s=din[name], d=dst: e.dma_start(out=d[:], in_=s))
    em.dma("d", lambda e: e.dma_start(out=FSTG[:], in_=din["Win"].rearrange("(k p) m -> p k m", p=128)))
    em.dma("d", lambda e: e.dma_start(out=WOS[:], in_=din["WoutT"]))
    em.dma("d", lambda e: e.dma_start(out=RPS[:], in_=din["repw1"]))
    em.dma("d", lambda e: e.dma_start(out=SMS[:], in_=din["sumw2"]))
    em.dma("d", lambda e: e.dma_start(out=COLS[:, 0:8], in_=din["w1_a"]))
    em.dma("d", lambda e: e.dma_start(out=COLS[:, 8:16], in_=din["b1_a"]))
    em.dma("d", lambda e: e.dma_start(out=COLS[:, 16:24], in_=din["binmi_a"]))
    tld = em.tok("d")
    em.cop("v", lambda e: e.memset(ZEROT[:], 0.0))
    em.cop("v", lambda e: e.memset(CEPS5[:], 1e-5))
    em.cop("v", lambda e: e.memset(CEPS8[:], 1e-8))
    em.cop("v", lambda e: e.memset(S32[:].rearrange("p a b -> p (a b)"), 0.0))
    em.cop("v", lambda e: e.memset(LTAB[:].rearrange("p a b -> p (a b)"), 0.0))
    em.cop("v", lambda e: e.memset(SC[:].rearrange("p a b -> p (a b)"), 0.0))
    em.cop("v", lambda e: e.memset(COLS[:, 24:48], 0.0), waits=[tld])
    em.cop("v", lambda e: e.memset(SROW[:], 0.0))
    em.cop("v", lambda e: e.tensor_copy(WIN[:].rearrange("p a b -> p (a b)"),
                                        FSTG[:].rearrange("p a b -> p (a b)")), waits=[tld])
    em.cop("v", lambda e: e.tensor_copy(WOUT[:], WOS[:]))
    em.cop("v", lambda e: e.tensor_copy(REP[:].rearrange("p a b -> p (a b)"),
                                        RPS[:].rearrange("p a b -> p (a b)")))
    em.cop("v", lambda e: e.tensor_copy(SUM2[:].rearrange("p a b -> p (a b)"),
                                        SMS[:].rearrange("p a b -> p (a b)")))
    em.cop("v", lambda e: e.tensor_mul(B1E[:], COLS[:, 0:8], COLS[:, 16:24]), waits=[tld])
    em.cop("v", lambda e: e.tensor_add(B1E[:], B1E[:], COLS[:, 8:16]))
    em.barrier()

    skipwq = _os.environ.get("SKIPWQ", "0") == "1"
    if skipwq:
        wjobs = []
    else:
        pass
    wjobs = [] if skipwq else [
        (din["wqT"], ("sbuf", WQ), 8, D, 0),
        (din["wkT"], ("sbuf", WK), 8, D, 1),
        (din["wvT"], ("hbm", wvq_st), 8, DV, 2),
        (din["wgT"], ("hbm", wgq_st), 8, DV, 3),
        (din["wgk1T"], ("sbuf", WGK1), 8, 16, 4),
        (din["woT"], ("hbm", woq_st), 16, D, 6),
    ]
    for (hbm, dest, nkt, dout, widx) in wjobs:
        per_unit = max(1, 4096 // dout)
        units = [(k0, min(k0 + per_unit, nkt)) for k0 in range(0, nkt, per_unit)]
        src3 = hbm.rearrange("(k p) o -> p k o", p=128)
        prevred = None
        for ui, (k0, k1) in enumerate(units):
            fl = (k1 - k0) * dout
            slot = ui % 2
            tin = em.dma("d", lambda e, k0=k0, k1=k1, slot=slot, src3=src3, dout=dout:
                         e.dma_start(out=STG[:, slot, 0:(k1 - k0) * dout]
                                     .rearrange("p (a b) -> p a b", a=(k1 - k0)),
                                     in_=src3[:, k0:k1, :]),
                         waits=[prevred] if prevred else ())
            em.cop("v", lambda e, slot=slot, fl=fl: e.reduce_sum(
                COLS[:, 40:41], STG[:, slot, 0:fl], axis=AX.X,
                apply_absolute_value=True), waits=[tin])
            if ui == 0:
                em.cop("v", lambda e: e.tensor_copy(COLS[:, 41:42], COLS[:, 40:41]))
            else:
                em.cop("v", lambda e: e.tensor_add(COLS[:, 41:42], COLS[:, 41:42],
                                                   COLS[:, 40:41]))
            prevred = em.tok("v")
        em.cop("t", lambda e: e.matmul(PA[0:1, 0:1], ONESCF[:], COLS[:, 41:42],
                                       start=True, stop=True), waits=[prevred])
        tmm = em.tok("t")
        numel = nkt * 128 * dout
        em.cop("v", lambda e, numel=numel: e.tensor_scalar(
            SROW[0:1, 0:1], PA[0:1, 0:1], 1.0 / numel, 1e-5, ALU.mult, ALU.max),
            waits=[tmm])
        tms = em.tok("v")
        em.cop("t", lambda e: e.matmul(PB[:, 0:1], ONESRF[:], SROW[0:1, 0:1],
                                       start=True, stop=True), waits=[tms])
        tmb = em.tok("t")
        em.cop("v", lambda e, widx=widx: e.tensor_copy(MW[:, widx:widx + 1], PB[:, 0:1]),
               waits=[tmb])
        em.cop("v", lambda e, widx=widx: e.reciprocal(WSC[:, widx:widx + 1],
                                                      MW[:, widx:widx + 1]))
        em.barrier()
        for ui, (k0, k1) in enumerate(units):
            fl = (k1 - k0) * dout
            slot = ui % 2
            tin = em.dma("d", lambda e, k0=k0, k1=k1, slot=slot, src3=src3, dout=dout:
                         e.dma_start(out=STG[:, slot, 0:(k1 - k0) * dout]
                                     .rearrange("p (a b) -> p a b", a=(k1 - k0)),
                                     in_=src3[:, k0:k1, :]),
                         waits=[em.tok("v"), em.tok("s"), em.tok("d")])
            em.cop("s", lambda e, slot=slot, fl=fl, widx=widx: e.activation(
                QSC[:, 0:fl], STG[:, slot, 0:fl], AF.Identity, scale=WSC[:, widx:widx + 1]),
                waits=[tin])
            tsc = em.tok("s")
            em.cop("v", lambda e, fl=fl: e.tensor_scalar(
                QSC[:, 0:fl], QSC[:, 0:fl], RC, RC, ALU.add, ALU.subtract), waits=[tsc])
            if dest[0] == "sbuf":
                em.cop("v", lambda e, fl=fl, dest=dest, k0=k0, k1=k1: e.tensor_scalar(
                    dest[1][:, k0:k1, :].rearrange("p a b -> p (a b)"),
                    QSC[:, 0:fl], 1.0, -1.0, ALU.min, ALU.max))
            else:
                em.cop("v", lambda e, fl=fl: e.tensor_scalar(
                    QSB[:, 0:fl], QSC[:, 0:fl], 1.0, -1.0, ALU.min, ALU.max))
                em.dma("d", lambda e, fl=fl, dest=dest, k0=k0, k1=k1: e.dma_start(
                    out=dest[1][:, k0:k1, :].rearrange("p a b -> p (a b)"),
                    in_=QSB[:, 0:fl]), waits=[em.tok("v")])
            em.barrier(["v", "s", "d"])
    if skipwq:
        tin = None
    tin = em.dma("d", lambda e: e.dma_start(out=STG[0:16, 0, 0:1024], in_=din["wgk2T"])) if not skipwq else None
    if not skipwq:
      em.cop("v", lambda e: e.reduce_sum(COLS[0:16, 40:41], STG[0:16, 0, 0:1024],
                                       axis=AX.X, apply_absolute_value=True), waits=[tin])
    if not skipwq:
      em.cop("t", lambda e: e.matmul(PA[0:1, 0:1], ONESCF[0:16, :], COLS[0:16, 40:41],
                                   start=True, stop=True), waits=[em.tok("v")])
      em.cop("v", lambda e: e.tensor_scalar(SROW[0:1, 0:1], PA[0:1, 0:1],
                                          1.0 / (16 * 1024), 1e-5, ALU.mult, ALU.max),
           waits=[em.tok("t")])
      em.cop("t", lambda e: e.matmul(PB[:, 0:1], ONESRF[:], SROW[0:1, 0:1],
                                   start=True, stop=True), waits=[em.tok("v")])
      em.cop("v", lambda e: e.tensor_copy(MW[:, 7:8], PB[:, 0:1]), waits=[em.tok("t")])
      em.cop("v", lambda e: e.reciprocal(WSC[:, 7:8], MW[:, 7:8]))
      em.cop("s", lambda e: e.activation(QSC[0:16, 0:1024], STG[0:16, 0, 0:1024],
                                       AF.Identity, scale=WSC[0:16, 7:8]), waits=[em.tok("v")])
      em.cop("v", lambda e: e.tensor_scalar(QSC[0:16, 0:1024], QSC[0:16, 0:1024],
                                          RC, RC, ALU.add, ALU.subtract),
           waits=[em.tok("s")])
      em.cop("v", lambda e: e.tensor_scalar(WGK2[:], QSC[0:16, 0:1024], 1.0, -1.0,
                                          ALU.min, ALU.max))
    em.barrier()
    em.flush()
    stup.close()
    if stage == 0:
        ph1.close()
        return nc

    XA = sb1("XA", [128, 8, T])
    XP = sb1("XP", [128, 8, T])
    U1 = sb1("U1", [128, 8, T])
    QSCR = sb1("QSCR", [128, 8, T])
    XI = sb1("XI", [128, 4, T], I32)
    XQB = sb1("XQB", [128, 8, T], BF16)
    QF = sb1("QF", [128, 8, T], BF16)
    KF = sb1("KF", [128, 8, T], BF16)
    VT = sb1("VT", [128, 2, DV], BF16)
    GFB = sb1("GFB", [128, 16, T], BF16)
    THB = sb1("THB", [128, 16, T], BF16)
    SGB = sb1("SGB", [128, 16, T], BF16)
    OGB = sb1("OGB", [128, 16, T], BF16)
    GL = sb1("GL", [128, 8, T], BF16)
    E1 = sb1("E1", [128, 8, C], BF16)
    E2 = sb1("E2", [128, 8, C], BF16)
    QS = sb1("QS", [128, 8, C], BF16)
    KS = sb1("KS", [128, 8, C], BF16)
    QI = sb1("QI", [128, 8, C], BF16)
    KX = sb1("KX", [128, 8, C], BF16)
    KXT = sb1("KXT", [128, 2, C], BF16)
    ABF = sb1("ABF", [128, 128])
    AB = sb1("AB", [128, 128], BF16)
    S16 = sb1("S16", [128, 2, 512], BF16)
    WSCR = XA
    BB3 = AP(QSCR, 0, [[8 * T, 128], [T, 8], [1, C]])
    BS3 = AP(QSCR, C, [[8 * T, 128], [T, 8], [1, C]])

    def BBs(k):
        return AP(QSCR, k * T, [[8 * T, 128], [1, C]])

    bcslot = [0]
    rdt = {}

    def bcast(j, waits=()):
        s = bcslot[0]
        bcslot[0] = 1 - s
        tk = em.cop("t", lambda e, s=s, j=j: e.matmul(
            PBC[:, s * 256:(s + 1) * 256], SELT[:, j, :],
            SCR[:].rearrange("a b c -> a (b c)"), start=True, stop=True),
            waits=list(waits) + [em.tok("v"), em.tok("s")])
        rdt["SCR_rd"] = tk
        return PBC[:, s * 256:(s + 1) * 256], tk

    def flush_sc(c0, c1, waits):
        tb = em.cop("v", lambda e: e.tensor_copy(
            TMRB[:, :, 0:32].rearrange("p a b -> p (a b)") if False else
            AP(TMRB, 0, [[256, 128], [128, 2], [1, 32]]), SC[:, :, 0:32]),
            waits=list(waits) + [rdt.get("TMRB")])
        tks = []
        for tt in range(2):
            tks.append(em.cop("t", lambda e, tt=tt: e.transpose(
                PTR[0:32, tt * 128:tt * 128 + 128],
                AP(TMRB, tt * 128, [[256, 128], [1, 32]]), IDENT[:]),
                waits=[tb, rdt.get("PTR")]))
        rdt["TMRB"] = tks[-1]
        out_tks = [em.cop("v", lambda e: e.tensor_copy(
            SCR[:].rearrange("a b c -> a (b c)"), PTR[0:32, 0:256]),
            waits=tks + [rdt.get("SCR_rd")])]
        rdt["PTR"] = out_tks[0]
        return out_tks

    def stats_cols(src_bf, src_sq, mcol, sqcol, waits):
        t0 = []
        for (src, r0) in [(src_bf, 0), (src_sq, 1)]:
            for k in range(8):
                t0.append(em.cop("t", lambda e, src=src, k=k, r0=r0: e.matmul(
                    PA[0:1, r0 * 256:(r0 + 1) * 256], ONESC[:], src[:, k, :],
                    start=(k == 0), stop=(k == 7)),
                    waits=list(waits) + [rdt.get("PA"), rdt.get("SROW")]))
        rdt["ST_MM"] = t0[-1]
        tr = em.cop("v", lambda e: e.tensor_scalar(SROW[0:1, 0:512], PA[0:1, 0:512],
                                                   1.0 / 1024.0, None, ALU.mult), waits=t0)
        rdt["PA"] = tr
        tks = []
        for (r0, col) in [(0, mcol), (1, sqcol)]:
            for tt in range(2):
                tk = em.cop("t", lambda e, r0=r0, tt=tt: e.transpose(
                    PO2[0:128, 448 + r0 * 2 + tt:449 + r0 * 2 + tt],
                    SROW[0:1, r0 * 256 + tt * 128:r0 * 256 + (tt + 1) * 128],
                    IDENTF[0:1, 0:1]), waits=[tr, rdt.get("PO2")])
                rdt["SROW"] = tk
                tks.append(em.cop("v", lambda e, col=col, tt=tt, r0=r0: e.tensor_copy(
                    SC[:, tt, col:col + 1],
                    PO2[0:128, 448 + r0 * 2 + tt:449 + r0 * 2 + tt]), waits=[tk]))
        rdt["PO2"] = tks[-1]
        return tks

    def rsqrt_cols(rstdcol, vcols, epsval):
        em.cop("v", lambda e: e.tensor_scalar(COLS[:, vcols:vcols + 2],
                                              SC[:, :, rstdcol], epsval, None, ALU.add))
        em.cop("s", lambda e: e.activation(SC[:, :, rstdcol], COLS[:, vcols:vcols + 2],
                                           AF.Ln), waits=[em.tok("v")])
        em.cop("s", lambda e: e.activation(SC[:, :, rstdcol], SC[:, :, rstdcol],
                                           AF.Exp, scale=-0.5))
        em.cop("v", lambda e: e.tensor_tensor(COLS[:, vcols + 2:vcols + 4],
                                              SC[:, :, rstdcol], SC[:, :, rstdcol],
                                              ALU.mult), waits=[em.tok("s")])
        em.cop("v", lambda e: e.tensor_tensor(COLS[:, vcols + 2:vcols + 4],
                                              COLS[:, vcols + 2:vcols + 4],
                                              COLS[:, vcols:vcols + 2], ALU.mult))
        em.cop("v", lambda e: e.tensor_scalar(COLS[:, vcols + 2:vcols + 4],
                                              COLS[:, vcols + 2:vcols + 4],
                                              -0.5, 1.5, ALU.mult, ALU.add))
        em.cop("v", lambda e: e.tensor_tensor(SC[:, :, rstdcol], SC[:, :, rstdcol],
                                              COLS[:, vcols + 2:vcols + 4], ALU.mult))
        return em.tok("v")

    def rstd_col(mcol, sqcol, rstdcol, epscol, waits):
        for tt in range(2):
            em.cop("v", lambda e, tt=tt: e.tensor_mul(COLS[:, 44:45],
                                                      SC[:, tt, mcol:mcol + 1],
                                                      SC[:, tt, mcol:mcol + 1]), waits=waits)
            em.cop("v", lambda e, tt=tt: e.tensor_sub(SC[:, tt, rstdcol:rstdcol + 1],
                                                      SC[:, tt, sqcol:sqcol + 1],
                                                      COLS[:, 44:45]))
        return rsqrt_cols(rstdcol, 40, 1e-5)

    def rowmax_col(w_ap, ndt, col, waits):
        t1 = em.cop("v", lambda e, w=w_ap: e.reduce_max(
            TMRB[:].rearrange("p a b -> p (a b)"), w.rearrange("p a b -> p b a"),
            axis=AX.X, apply_absolute_value=True),
            waits=list(waits) + [rdt.get("TMRB")])
        tks = []
        for tt in range(2):
            tk = em.cop("t", lambda e, tt=tt: e.transpose(
                PTR[:, tt * 128:(tt + 1) * 128], TMRB[:, tt, :], IDENT[:]),
                waits=[t1, rdt.get("PTR")])
            rdt["TMRB"] = tk
            tks.append(em.cop("v", lambda e, tt=tt, col=col: e.reduce_max(
                SC[:, tt, col:col + 1], PTR[:, tt * 128:(tt + 1) * 128], axis=AX.X),
                waits=[tk]))
        rdt["PTR"] = tks[-1]
        return tks

    def emit_ffn(rstd_sb, x_sb, out_t, ubf, waits):
        t0 = []
        for k in range(8):
            t0.append(em.cop("t", lambda e, k=k, ubf=ubf: e.matmul(
                PO1[0:64, 0:256], WIN[:, k, :], ubf[:, k, :], start=(k == 0),
                stop=(k == 7)), waits=list(waits) + [rdt.get("PO1")]))
        ts_ = em.cop("v", lambda e, r=rstd_sb: e.tensor_tensor(
            FS[:], PO1[0:64, 0:256], r, ALU.mult), waits=t0)
        rdt["PO1"] = ts_
        em.mb()
        trep = []
        for kk in range(8):
            pd = PA if kk < 4 else PB
            off = (kk % 4) * 256
            trep.append(em.cop("t", lambda e, kk=kk, pd=pd, off=off: e.matmul(
                pd[:, off:off + 256], REP[:, kk, :], FS[:], start=True, stop=True),
                waits=[ts_, rdt.get("PA" if kk < 4 else "PB")]))
        em.mb()
        if dbg:
            if not hasattr(em, "_dumped_pa"):
                em._dumped_pa = True
                em.cop("v", lambda e: e.tensor_copy(QSCR[:, 0:4, :].rearrange("p a b -> p (a b)"), PA[:, 0:1024]))
                em.barrier()
                em.dma("d", lambda e: e.dma_start(out=dbg_out["d_pa"], in_=QSCR[:, 0:4, :].rearrange("p a b -> p (a b)")))
                em.dma("d", lambda e: e.dma_start(out=dbg_out["d_rep"],
                                                  in_=REPF[:].rearrange("p a b -> p (a b)")))
                em.barrier()
        trl = []
        for kk in range(8):
            pd = PA if kk < 4 else PB
            off = (kk % 4) * 256
            trl.append(em.cop("s", lambda e, kk=kk, pd=pd, off=off: e.activation(
                FH[:, kk, :], pd[:, off:off + 256], AF.Relu, bias=B1E[:, kk:kk + 1]),
                waits=trep))
        rdt["PA"] = rdt["PB"] = trl[-1]
        em.mb()
        tsum = []
        for kk in range(8):
            tsum.append(em.cop("t", lambda e, kk=kk: e.matmul(
                PO1[0:64, 256:512], SUM2[:, kk, :], FH[:, kk, :],
                start=(kk == 0), stop=(kk == 7)), waits=trl + [rdt.get("PO1")]))
        em.mb()
        tg = em.cop("s", lambda e: e.activation(FG[:], PO1[0:64, 256:512], AF.Identity,
                                                bias=B2[:]), waits=tsum)
        rdt["PO1"] = tg
        em.mb()
        tw = []
        for m in range(8):
            pd = PA if m < 4 else PB
            off = (m % 4) * 256
            tw.append(em.cop("t", lambda e, m=m, pd=pd, off=off: e.matmul(
                pd[:, off:off + 256], WOUT[:, m * 128:(m + 1) * 128], FG[:],
                start=True, stop=False), waits=[tg]))
            tw.append(em.cop("t", lambda e, m=m, pd=pd, off=off: e.matmul(
                pd[:, off:off + 256], BOUTR[:, m, :], ONESR2[:], start=False, stop=True)))
        em.mb()
        outs = []
        for half, pd in [(0, PA), (1, PB)]:
            outs.append(em.cop("v", lambda e, half=half, pd=pd, x_sb=x_sb, on=out_t:
                               e.scalar_tensor_tensor(
                on[:, half * 4:(half + 1) * 4, :].rearrange("p a b -> p (a b)"),
                pd[:, 0:1024], 0.5,
                x_sb[:, half * 4:(half + 1) * 4, :].rearrange("p a b -> p (a b)"),
                ALU.mult, ALU.add), waits=tw))
        rdt["PA"], rdt["PB"] = outs[0], outs[1]
        return outs

    def ln_to_ubf(x_sb, mcol, sqcol, rstdcol, bfb, sqb, ubf, waits=()):
        em.cop("v", lambda e, x=x_sb, bfb=bfb: e.tensor_copy(
            bfb[:].rearrange("p a b -> p (a b)"), x[:].rearrange("p a b -> p (a b)")),
            waits=list(waits) + [rdt.get("ST_MM")])
        em.cop("s", lambda e, bfb=bfb, sqb=sqb: e.activation(
            sqb[:].rearrange("p a b -> p (a b)"), bfb[:].rearrange("p a b -> p (a b)"),
            AF.Square), waits=[em.tok("v"), rdt.get("XQB")])
        em.mb()
        if lncut < 2:
            return
        st_t = stats_cols(bfb, sqb, mcol, sqcol, [em.tok("s")])
        if lncut < 3:
            return
        rs_t = rstd_col(mcol, sqcol, rstdcol, CEPS5, st_t)
        if lncut < 4:
            return
        fl = flush_sc(mcol, mcol + 1, st_t) + flush_sc(rstdcol, rstdcol + 1, [rs_t])
        em.mb()
        if lncut < 5:
            return
        mb_, tkm = bcast(mcol, fl)
        if lncut < 6:
            return
        em.cop("v", lambda e, x=x_sb, mb_=mb_, ubf=ubf: e.tensor_tensor(
            ubf[:].rearrange("p a b -> p (a b)"), x[:].rearrange("p a b -> p (a b)"),
            AP(mb_.tensor, mb_.offset, [mb_.ap[0], [0, 8], mb_.ap[1]]), ALU.subtract),
            waits=[tkm])
        rb, tkr = bcast(rstdcol, fl)
        em.cop("v", lambda e, rb=rb: e.tensor_copy(MBC[:, 0, :], rb), waits=[tkr])
        em.mb()

    NW = {0: NQ, 1: NK, 2: NV, 3: NG, 4: NGK1}
    ICTOT = [0]
    t_xa_free = None
    em.pipe = True
    for c in range(nchunks):
        txa = em.dma("d", lambda e, c=c: e.dma_start(
            out=XA[:],
            in_=xT.rearrange("(k p) t -> p k t", p=128)[:, :, c * T:(c + 1) * T]),
            waits=[t_xa_free] if t_xa_free else [])
        em.mb()
        if p1cut < -2:
            em.barrier()
            continue
        ln_to_ubf(XA, 0, 1, 2, QF, XQB, KF, waits=[txa])
        if p1cut < -1:
            em.barrier()
            continue
        emit_ffn(MBC[0:64, 0, :], XA, XP, KF, [em.tok("v")])
        em.mb()
        if dbg and c == 0:
            em.dma("d", lambda e: e.dma_start(out=dbg_out["d_ub"], in_=KF[:].rearrange("p a b -> p (a b)")))
            em.dma("d", lambda e: e.dma_start(out=dbg_out["d_fs"], in_=FSF[:]))
            em.dma("d", lambda e: e.dma_start(out=dbg_out["d_fh"], in_=FH[:].rearrange("p a b -> p (a b)")))
            em.dma("d", lambda e: e.dma_start(out=dbg_out["d_fg"], in_=FGF[:]))
            em.dma("d", lambda e: e.dma_start(out=dbg_out["d_sc"], in_=SC[:].rearrange("p a b -> p (a b)")))
            em.dma("d", lambda e: e.dma_start(out=dbg_out["d_mbc"], in_=MBC[:].rearrange("p a b -> p (a b)")))
            em.barrier()
        if p1cut < 0:
            em.barrier()
            continue
        em.dma("d", lambda e, c=c: e.dma_start(
            out=xp_st[c], in_=XP[:].rearrange("p a b -> p (a b)")), waits=[em.tok("v")])
        if p1cut < 1:
            em.barrier()
            continue
        em.cop("v", lambda e: e.tensor_copy(QF[:].rearrange("p a b -> p (a b)"),
                                            XP[:].rearrange("p a b -> p (a b)")),
               waits=[rdt.get("ST_MM")])
        em.cop("s", lambda e: e.activation(XQB[:].rearrange("p a b -> p (a b)"),
                                           QF[:].rearrange("p a b -> p (a b)"), AF.Square),
               waits=[em.tok("v"), rdt.get("XQB"), rdt.get("ST_MM")])
        em.mb()
        st_t = stats_cols(QF, XQB, 4, 5, [em.tok("s")])
        rs_t = rstd_col(4, 5, 6, CEPS5, st_t)
        for tt in range(2):
            em.cop("v", lambda e, tt=tt: e.tensor_mul(COLS[:, 44:45], SC[:, tt, 4:5],
                                                      SC[:, tt, 4:5]), waits=[rs_t])
            em.cop("v", lambda e, tt=tt: e.tensor_sub(COLS[:, 45:46], SC[:, tt, 5:6],
                                                      COLS[:, 44:45]))
            em.cop("v", lambda e, tt=tt: e.tensor_scalar(COLS[:, 46:47], COLS[:, 45:46],
                                                         1e-5, None, ALU.add))
            em.cop("v", lambda e, tt=tt: e.reciprocal(COLS[:, 47:48], COLS[:, 46:47]))
            em.cop("v", lambda e, tt=tt: e.tensor_mul(SC[:, tt, 28:29], COLS[:, 47:48],
                                                      COLS[:, 45:46]))
        rsqrt_cols(28, 40, 1e-8)
        for tt in range(2):
            em.cop("v", lambda e, tt=tt: e.tensor_mul(SC[:, tt, 7:8], SC[:, tt, 6:7],
                                                      SC[:, tt, 28:29]))
        fl2 = flush_sc(4, 5, st_t)
        em.mb()
        m2b, tk2 = bcast(4, fl2)
        em.cop("v", lambda e, m2b=m2b: e.tensor_tensor(
            U1[:].rearrange("p a b -> p (a b)"), XP[:].rearrange("p a b -> p (a b)"),
            AP(m2b.tensor, m2b.offset, [m2b.ap[0], [0, 8], m2b.ap[1]]), ALU.subtract),
            waits=[tk2])
        em.mb()

        if p1cut < 2:
            em.barrier()
            continue
        for si in range(5):
            nten = NW[si]
            tw = em.cop("v", lambda e, nten=nten: e.tensor_tensor(
                WSCR[:].rearrange("p a b -> p (a b)"),
                U1[:].rearrange("p a b -> p (a b)"),
                AP(nten, 0, [[8, 128], [1, 8], [0, T]]), ALU.mult))
            rowmax_col(WSCR[:], 8, 8 + si, [tw])
            em.mb()
        for tt in range(2):
            em.cop("v", lambda e, tt=tt: e.scalar_tensor_tensor(
                SC[:, tt, 13:18], SC[:, tt, 8:13], 1.0,
                AP(SC, tt * 32 + 7, [[64, 128], [0, 5]]), ALU.mult, ALU.mult))
            em.cop("v", lambda e, tt=tt: e.tensor_scalar(SC[:, tt, 13:18],
                                                         SC[:, tt, 13:18],
                                                         1e-5, None, ALU.max))
            em.cop("v", lambda e, tt=tt: e.reciprocal(SC[:, tt, 18:23], SC[:, tt, 13:18]))
            em.cop("v", lambda e, tt=tt: e.scalar_tensor_tensor(
                SC[:, tt, 8:13], SC[:, tt, 18:23], 127.0,
                AP(SC, tt * 32 + 7, [[64, 128], [0, 5]]), ALU.mult, ALU.mult))
            em.cop("v", lambda e, tt=tt: e.scalar_tensor_tensor(
                SC[:, tt, 23:26], SC[:, tt, 14:17], 1.0 / 127.0, MW[:, 1:4],
                ALU.mult, ALU.mult))
        fl3 = flush_sc(8, 13, [em.tok("v")]) + flush_sc(23, 26, [em.tok("v")])
        em.mb()

        def quantize(si):
            nten = NW[si]
            phib, tkb = bcast(8 + si, fl3)
            em.cop("v", lambda e, nten=nten: e.tensor_tensor(
                QSCR[:].rearrange("p a b -> p (a b)"),
                U1[:].rearrange("p a b -> p (a b)"),
                AP(nten, 0, [[8, 128], [1, 8], [0, T]]), ALU.mult))
            em.cop("v", lambda e, phib=phib: e.tensor_tensor(
                QSCR[:].rearrange("p a b -> p (a b)"),
                QSCR[:].rearrange("p a b -> p (a b)"),
                AP(phib.tensor, phib.offset, [phib.ap[0], [0, 8], phib.ap[1]]),
                ALU.mult), waits=[tkb])
            for hf in range(2):
                em.cop("v", lambda e, hf=hf: e.tensor_scalar(
                    XI[:].rearrange("p a b -> p (a b)"),
                    QSCR[:, hf * 4:(hf + 1) * 4, :].rearrange("p a b -> p (a b)"),
                    RC, RC, ALU.add, ALU.subtract), waits=[em.tok("s")])
                em.cop("s", lambda e, hf=hf: e.activation(
                    XQB[:, hf * 4:(hf + 1) * 4, :].rearrange("p a b -> p (a b)"),
                    XI[:].rearrange("p a b -> p (a b)"), AF.Copy),
                    waits=[em.tok("v"), rdt.get("XQB"), rdt.get("ST_MM")])
            return em.tok("s")

        if p1cut < 3:
            em.barrier()
            continue
        tq = quantize(0)
        tqm = []
        for m in range(8):
            pd = PA if m < 4 else PB
            off = (m % 4) * 256
            for k in range(8):
                tqm.append(em.cop("t", lambda e, m=m, k=k, pd=pd, off=off: e.matmul(
                    pd[:, off:off + 256], WQ[:, k, m * 128:(m + 1) * 128], XQB[:, k, :],
                    start=(k == 0), stop=(k == 7)),
                    waits=[tq, rdt.get("PA" if m < 4 else "PB")]))
        for half, pd in [(0, PA), (1, PB)]:
            rdt["PA" if half == 0 else "PB"] = em.cop(
                "s", lambda e, half=half, pd=pd: e.activation(
                    QF[:, half * 4:(half + 1) * 4, :].rearrange("p a b -> p (a b)"),
                    pd[:, 0:1024], AF.Copy), waits=tqm)
        rdt["XQB"] = tqm[-1]
        em.mb()
        if p1cut < 4:
            em.barrier()
            continue
        tq = quantize(1)
        tkm = []
        for m in range(8):
            pd = PA if m < 4 else PB
            off = (m % 4) * 256
            for k in range(8):
                tkm.append(em.cop("t", lambda e, m=m, k=k, pd=pd, off=off: e.matmul(
                    pd[:, off:off + 256], WK[:, k, m * 128:(m + 1) * 128], XQB[:, k, :],
                    start=(k == 0), stop=(k == 7)),
                    waits=[tq, rdt.get("PA" if m < 4 else "PB")]))
        for half, pd in [(0, PA), (1, PB)]:
            rdt["PA" if half == 0 else "PB"] = em.cop(
                "s", lambda e, half=half, pd=pd: e.activation(
                    KF[:, half * 4:(half + 1) * 4, :].rearrange("p a b -> p (a b)"),
                    pd[:, 0:1024], AF.Copy), waits=tkm)
        rdt["XQB"] = tkm[-1]
        em.mb()
        if p1cut < 5:
            em.barrier()
            continue
        tq = quantize(2)
        wst_rd = {0: None, 1: None}
        for u in range(8):
            wslot = u % 2
            twl = em.dma("pd", lambda e, u=u, wslot=wslot: e.dma_start(
                out=WST[:, wslot, :, :],
                in_=wvq_st[:, :, u * 256:(u + 1) * 256]),
                waits=[wst_rd[wslot] or em.tok("t")])
            for tt in range(2):
                pd = [PA, PB][tt]
                pdn = "PA" if tt == 0 else "PB"
                tvm = []
                for k in range(8):
                    tvm.append(em.cop("t", lambda e, tt=tt, k=k, pd=pd, wslot=wslot:
                                      e.matmul(pd[:, 0:256],
                                               XQB[:, k, tt * 128:(tt + 1) * 128],
                                               WST[:, wslot, k, :],
                                               start=(k == 0), stop=(k == 7)),
                                      waits=[tq, twl, rdt.get(pdn)]))
                rdt[pdn] = em.cop("s", lambda e, tt=tt, u=u, pd=pd: e.activation(
                    VT[:, tt, u * 256:(u + 1) * 256], pd[:, 0:256], AF.Identity,
                    scale=SC[:, tt, 24:25]), waits=tvm)
                rdt["XQB"] = tvm[-1]
                wst_rd[wslot] = tvm[-1]
            em.mb()
        if p1cut < 6:
            em.barrier()
            continue
        tq = quantize(4)
        ti1 = []
        for k in range(8):
            ti1.append(em.cop("t", lambda e, k=k: e.matmul(
                PO1[0:16, 0:256], WGK1[:, k, :], XQB[:, k, :],
                start=(k == 0), stop=(k == 7)), waits=[tq, rdt.get("PO1")]))
        rdt["PO1"] = em.cop("v", lambda e: e.tensor_copy(I1[:], PO1[0:16, 0:256]),
                            waits=ti1)
        em.cop("v", lambda e: e.tensor_copy(I1B[:], I1[:]))
        rdt["XQB"] = ti1[-1]
        em.mb()
        if p1cut < 7:
            em.barrier()
            continue
        tq = quantize(3)
        psigb, tkg = bcast(25, fl3)
        em.cop("v", lambda e, psigb=psigb: e.tensor_copy(MBC[:, 1, :], psigb),
               waits=[tkg])
        wst_rd = {0: None, 1: None}
        for u in range(8):
            wslot = u % 2
            twl = em.dma("pd", lambda e, u=u, wslot=wslot: e.dma_start(
                out=WST[:, wslot, :, :],
                in_=wgq_st[:, :, u * 256:(u + 1) * 256]),
                waits=[wst_rd[wslot] or em.tok("t")])
            for mm in range(2):
                m = u * 2 + mm
                pd = [PA, PB][mm]
                pdn = "PA" if mm == 0 else "PB"
                tgm = []
                for k in range(8):
                    tgm.append(em.cop("t", lambda e, mm=mm, k=k, pd=pd, wslot=wslot:
                                      e.matmul(pd[:, 0:256],
                                               WST[:, wslot, k, mm * 128:(mm + 1) * 128],
                                               XQB[:, k, :],
                                               start=(k == 0), stop=(k == 7)),
                                      waits=[tq, twl, rdt.get(pdn)]))
                rdt["XQB"] = tgm[-1]
                wst_rd[wslot] = tgm[-1]
                tgm.append(em.cop("v", lambda e, m=m, pd=pd: e.tensor_tensor(
                    GFB[:, m, :], pd[:, 0:256], MBC[:, 1, :], ALU.mult), waits=tgm))
                rdt[pdn] = tgm[-1]
            em.mb()
        if p1cut < 8:
            em.barrier()
            continue
        em.cop("s", lambda e: e.activation(SQ16[:], I1B[:], AF.Square),
               waits=[em.tok("v")])
        t0 = [em.cop("t", lambda e: e.matmul(PA[0:1, 0:256], ONESC[0:16, :], SQ16[:],
                                             start=True, stop=True),
                     waits=[em.tok("s"), rdt.get("PA")])]
        tr = em.cop("v", lambda e: e.tensor_scalar(SROW[0:1, 0:256], PA[0:1, 0:256],
                                                   1.0 / 16.0, None, ALU.mult), waits=t0)
        rdt["PA"] = tr
        if g2cut < 2:
            em.barrier()
            continue
        for tt in range(2):
            tk = em.cop("t", lambda e, tt=tt: e.transpose(
                PO2[0:128, 448 + tt:449 + tt], SROW[0:1, tt * 128:(tt + 1) * 128],
                IDENTF[0:1, 0:1]), waits=[tr, rdt.get("PO2")])
            rdt["SROW"] = tk
            rdt["PO2"] = em.cop("v", lambda e, tt=tt: e.tensor_copy(
                SC[:, tt, 26:27], PO2[0:128, 448 + tt:449 + tt]), waits=[tk])
        em.cop("s", lambda e: e.activation(SC[:, :, 26], SC[:, :, 26], AF.Ln,
                                           bias=CEPS8[:]), waits=[em.tok("v")])
        em.cop("s", lambda e: e.activation(SC[:, :, 26], SC[:, :, 26], AF.Exp, scale=-0.5))
        if g2cut < 3:
            em.barrier()
            continue
        em.cop("v", lambda e: e.tensor_scalar(I1[:], I1[:], NGK2[:], None, ALU.mult),
               waits=[em.tok("s")])
        em.cop("v", lambda e: e.tensor_copy(AP(TMRB, 0, [[256, 16], [1, 256]]), I1[:]),
               waits=[rdt.get("TMRB")])
        for tt in range(2):
            tk = em.cop("t", lambda e, tt=tt: e.transpose(
                PTR[:, tt * 128:tt * 128 + 16],
                AP(TMRB, tt * 128, [[256, 16], [1, 128]]), IDENT[0:16, 0:16]),
                waits=[em.tok("v"), rdt.get("PTR")])
            rdt["TMRB"] = tk
            rdt["PTR"] = em.cop("v", lambda e, tt=tt: e.reduce_max(
                SC[:, tt, 27:28], PTR[:, tt * 128:tt * 128 + 16], axis=AX.X,
                apply_absolute_value=True), waits=[tk])
        if g2cut < 4:
            em.barrier()
            continue
        for tt in range(2):
            em.cop("v", lambda e, tt=tt: e.tensor_mul(SC[:, tt, 27:28], SC[:, tt, 27:28],
                                                      SC[:, tt, 26:27]))
            em.cop("v", lambda e, tt=tt: e.tensor_scalar(SC[:, tt, 27:28],
                                                         SC[:, tt, 27:28],
                                                         1e-5, None, ALU.max))
            em.cop("v", lambda e, tt=tt: e.reciprocal(COLS[:, 44:45], SC[:, tt, 27:28]))
            em.cop("v", lambda e, tt=tt: e.scalar_tensor_tensor(
                SC[:, tt, 19:20], COLS[:, 44:45], 127.0, SC[:, tt, 26:27],
                ALU.mult, ALU.mult))
            em.cop("v", lambda e, tt=tt: e.scalar_tensor_tensor(
                SC[:, tt, 20:21], SC[:, tt, 27:28], 1.0 / 127.0, MW[:, 7:8],
                ALU.mult, ALU.mult))
        fl4 = flush_sc(19, 21, [em.tok("v")])
        em.mb()
        if g2cut < 5:
            em.barrier()
            continue
        phi2b, tka = bcast(19, fl4)
        em.cop("v", lambda e, phi2b=phi2b: e.tensor_tensor(I1[:], I1[:], phi2b[0:16, :],
                                                           ALU.mult), waits=[tka])
        em.cop("v", lambda e: e.tensor_scalar(XI[0:16, 0, 0:256], I1[:], RC, RC,
                                              ALU.add, ALU.subtract),
               waits=[em.tok("s")])
        em.cop("s", lambda e: e.activation(I1B[:], XI[0:16, 0, 0:256], AF.Copy),
               waits=[em.tok("v")])
        if g2cut < 6:
            em.barrier()
            continue
        psi2b, tkb2 = bcast(20, fl4)
        em.cop("v", lambda e, psi2b=psi2b: e.tensor_copy(MBC[:, 1, :], psi2b),
               waits=[tkb2])
        tz = []
        for m in range(8):
            pd = PA if m < 4 else PB
            off = (m % 4) * 256
            if g2cut < 7:
                continue
            tz.append(em.cop("t", lambda e, m=m, pd=pd, off=off: e.matmul(
                pd[:, off:off + 256], WGK2[:, m * 128:(m + 1) * 128], I1B[:],
                start=True, stop=True),
                waits=[em.tok("s"), rdt.get("PA" if m < 4 else "PB")]))
        for m in range(8):
            pd = PA if m < 4 else PB
            off = (m % 4) * 256
            if g2cut < 8:
                continue
            tz.append(em.cop("v", lambda e, m=m, pd=pd, off=off: e.tensor_tensor(
                U1[:, m, :], pd[:, off:off + 256], MBC[:, 1, :], ALU.mult), waits=tz))
            rdt["PA" if m < 4 else "PB"] = tz[-1]
            if g2cut < 9:
                continue
            tz.append(em.cop("p", lambda e, m=m: e.tensor_scalar(
                U1[:, m, :], U1[:, m, :], BGK2[:, m:m + 1], None, ALU.add), waits=tz))
        em.mb()
        if p1cut < 9:
            em.barrier()
            continue
        em.cop("s", lambda e: e.activation(WSCR[:].rearrange("p a b -> p (a b)"),
                                           U1[:].rearrange("p a b -> p (a b)"),
                                           AF.Abs), waits=[em.tok("p"), em.tok("v")])
        em.cop("s", lambda e: e.activation(WSCR[:].rearrange("p a b -> p (a b)"),
                                           WSCR[:].rearrange("p a b -> p (a b)"),
                                           AF.Exp, scale=-1.0))
        em.cop("s", lambda e: e.activation(WSCR[:].rearrange("p a b -> p (a b)"),
                                           WSCR[:].rearrange("p a b -> p (a b)"),
                                           AF.Ln, bias=1.0))
        em.cop("v", lambda e: e.tensor_scalar(QSCR[:].rearrange("p a b -> p (a b)"),
                                              U1[:].rearrange("p a b -> p (a b)"),
                                              0.0, 1.0 / 16.0, ALU.min, ALU.mult),
               waits=[em.tok("p")])
        em.cop("v", lambda e: e.scalar_tensor_tensor(
            GL[:].rearrange("p a b -> p (a b)"),
            WSCR[:].rearrange("p a b -> p (a b)"), -1.0 / 16.0,
            QSCR[:].rearrange("p a b -> p (a b)"), ALU.mult, ALU.add),
            waits=[em.tok("s")])
        t_xa_free = em.tok("v")
        em.mb()
        if p1cut < 10:
            em.barrier()
            continue
        em.cop("s", lambda e: e.activation(THB[:].rearrange("p a b -> p (a b)"),
                                           GFB[:].rearrange("p a b -> p (a b)"),
                                           AF.Tanh, scale=0.5), waits=[em.tok("v")])
        em.cop("p", lambda e: e.tensor_scalar(THB[:].rearrange("p a b -> p (a b)"),
                                              THB[:].rearrange("p a b -> p (a b)"),
                                              0.5, 0.5, ALU.mult, ALU.add),
               waits=[em.tok("s")])
        em.cop("v", lambda e: e.tensor_tensor(SGB[:].rearrange("p a b -> p (a b)"),
                                              THB[:].rearrange("p a b -> p (a b)"),
                                              GFB[:].rearrange("p a b -> p (a b)"),
                                              ALU.mult), waits=[em.tok("p")])
        em.mb()

        if p1cut < 11:
            em.barrier()
            continue
        for ic in range(icpc):
            gic = ICTOT[0]
            ICTOT[0] += 1
            tsl = slice(ic * C, (ic + 1) * C)
            tsc = []
            for k in range(8):
                tsc.append(em.cop("v", lambda e, k=k, tsl=tsl: e.tensor_tensor_scan(
                    BBs(k), GL[:, k, tsl], ZEROT[:], 0.0, ALU.add, ALU.bypass)))
            em.cop("v", lambda e: e.tensor_copy(
                COLS[:, 0:8], AP(QSCR, C - 1, [[8 * T, 128], [T, 8]])),
                waits=tsc + [rdt.get("COLS")])
            em.cop("v", lambda e: e.tensor_copy(
                COLS[:, 8:16], AP(QSCR, C // 2 - 1, [[8 * T, 128], [T, 8]])))
            em.cop("v", lambda e: e.tensor_scalar(COLS[:, 16:24], COLS[:, 8:16],
                                                  -1.0, None, ALU.mult))
            em.cop("v", lambda e, gic=gic: e.tensor_copy(
                AP(LTAB, gic % 32, [[8 * 32, 128], [32, 8]]), COLS[:, 24:32]))
            em.cop("v", lambda e: e.tensor_add(COLS[:, 24:32], COLS[:, 24:32],
                                               COLS[:, 0:8]))
            em.cop("v", lambda e: e.tensor_sub(COLS[:, 32:40], COLS[:, 0:8],
                                               COLS[:, 8:16]))
            em.cop("s", lambda e: e.activation(ELT[:], COLS[:, 0:8], AF.Exp),
                   waits=[em.tok("v"), rdt.get("ELT")])
            em.cop("s", lambda e: e.activation(COLS[:, 8:16], COLS[:, 8:16], AF.Exp))
            em.cop("s", lambda e: e.activation(COLS[:, 32:40], COLS[:, 32:40], AF.Exp))
            rdt["COLS"] = em.tok("s")
            em.cop("v", lambda e: e.tensor_tensor(
                BS3, BB3, AP(COLS, 16, [[48, 128], [1, 8], [0, C]]), ALU.add))
            em.cop("s", lambda e: e.activation(E1[:], BS3, AF.Exp),
                   waits=[em.tok("v")])
            em.cop("s", lambda e: e.activation(E2[:], BS3, AF.Exp, scale=-1.0))
            em.mb()
            em.cop("v", lambda e, tsl=tsl: e.tensor_tensor(
                QS[:], QF[:, :, tsl], E1[:], ALU.mult),
                waits=[em.tok("s"), rdt.get("GLA_T"), rdt.get("QI")])
            em.cop("v", lambda e, tsl=tsl: e.tensor_tensor(
                KS[:], KF[:, :, tsl], E2[:], ALU.mult))
            em.cop("v", lambda e: e.tensor_tensor(
                QI[:].rearrange("p a b -> p (a b)"),
                QS[:].rearrange("p a b -> p (a b)"),
                AP(COLS, 8, [[48, 128], [1, 8], [0, C]]), ALU.mult))
            em.cop("v", lambda e: e.tensor_tensor(
                KX[:].rearrange("p a b -> p (a b)"),
                KS[:].rearrange("p a b -> p (a b)"),
                AP(COLS, 32, [[48, 128], [1, 8], [0, C]]), ALU.mult))
            tqs = em.tok("v")
            em.mb()
            rdt["QI"] = em.dma("d", lambda e, gic=gic: e.dma_start(
                out=qt_st[gic], in_=QI[:].rearrange("p a b -> p (a b)")),
                waits=[tqs])
            for h in range(H):
                po = [PO1, PO2][h % 2]
                pon = "PO1" if h % 2 == 0 else "PO2"
                ts16 = em.cop("p", lambda e, h=h: e.tensor_copy(
                    S16[:].rearrange("p a b -> p (a b)"),
                    S32[:, 2 * h:2 * h + 2, :].rearrange("p a b -> p (a b)")),
                    waits=[rdt.get("S16"), rdt.get(("S32", h))])
                ta = []
                for kk in range(2):
                    ta.append(em.cop("t", lambda e, h=h, kk=kk: e.matmul(
                        PA[:, 0:128], KS[:, 2 * h + kk, :], QS[:, 2 * h + kk, :],
                        start=(kk == 0), stop=(kk == 1)),
                        waits=[rdt.get("PA"), tqs]))
                rdt["PA"] = em.cop("v", lambda e: e.tensor_tensor(
                    ABF[:], PA[:, 0:128], TRIU[:], ALU.mult), waits=ta)
                em.cop("v", lambda e, ic=ic: e.tensor_scalar(
                    AB[:], ABF[:], SC[:, ic, 23:24], None, ALU.mult))
                tab = em.tok("v")
                to = []
                for j in range(4):
                    to.append(em.cop("t", lambda e, h=h, j=j, ic=ic, po=po: e.matmul(
                        po[:, j * 128:(j + 1) * 128],
                        VT[:, ic, h * 512 + j * 128:h * 512 + (j + 1) * 128],
                        AB[:], start=True, stop=False),
                        waits=[tab, ts16, rdt.get(pon)]))
                    for kk in range(2):
                        to.append(em.cop("t", lambda e, h=h, j=j, kk=kk, po=po: e.matmul(
                            po[:, j * 128:(j + 1) * 128],
                            S16[:, kk, j * 128:(j + 1) * 128],
                            QI[:, 2 * h + kk, :], start=False, stop=(kk == 1))))
                rdt["S16"] = to[-1]
                rdt[pon] = em.cop("v", lambda e, h=h, po=po, tsl=tsl: e.tensor_tensor(
                    OGB[:, 4 * h:4 * h + 4, tsl],
                    po[:, 0:512].rearrange("p (a b) -> p a b", a=4),
                    SGB[:, 4 * h:4 * h + 4, tsl], ALU.mult), waits=to)
                tkt = []
                for kk in range(2):
                    tkt.append(em.cop("t", lambda e, h=h, kk=kk: e.transpose(
                        PTR[:, kk * 128:(kk + 1) * 128], KX[:, 2 * h + kk, :],
                        IDENT[:]), waits=[rdt.get("PTR")]))
                for kk in range(2):
                    tkt.append(em.cop("v", lambda e, kk=kk, ic=ic: e.tensor_scalar(
                        KXT[:, kk, :], PTR[:, kk * 128:(kk + 1) * 128],
                        SC[:, ic, 23:24], None, ALU.mult), waits=tkt))
                rdt["PTR"] = tkt[-1]
                tu = []
                for kk in range(2):
                    tu.append(em.cop("t", lambda e, h=h, kk=kk, ic=ic: e.matmul(
                        PB[:, kk * 512:(kk + 1) * 512], KXT[:, kk, :],
                        VT[:, ic, h * 512:(h + 1) * 512], start=True, stop=True),
                        waits=tkt + [rdt.get("PB")]))
                rdt["GLA_T"] = tu[-1]
                for kk in range(2):
                    rdt["PB"] = rdt[("S32", h)] = rdt["ELT"] = em.cop(
                        "v", lambda e, h=h, kk=kk: e.scalar_tensor_tensor(
                            S32[:, 2 * h + kk, :], S32[:, 2 * h + kk, :],
                            ELT[:, 2 * h + kk:2 * h + kk + 1],
                            PB[:, kk * 512:(kk + 1) * 512], ALU.mult, ALU.add),
                        waits=tu + [ts16])
                em.mb()
        em.dma("d", lambda e, c=c: e.dma_start(
            out=og_st[c], in_=OGB[:].rearrange("p a b -> p (a b)")),
            waits=[em.tok("v")])
        em.dma("d", lambda e, c=c: e.dma_start(
            out=sg_st[c], in_=SGB[:].rearrange("p a b -> p (a b)")))
        if dbg and c == 0:
            for nm, src in [("d_xp", XP), ("d_q", QF), ("d_k", KF), ("d_gl", GL),
                            ("d_og", OGB), ("d_v", VT)]:
                em.dma("d", lambda e, nm=nm, src=src: e.dma_start(
                    out=dbg_out[nm], in_=src[:].rearrange("p a b -> p (a b)")))
        em.barrier()
    em.pipe = False

    if stage == 1:
        em.flush()
        ph1.close()
        return nc

    em.dma("d", lambda e: e.dma_start(
        out=AP(cc_in, 0, [[512, 128], [512 * 128, 8], [1, 512]]), in_=S32[:]))
    em.barrier()
    em.op("p", lambda e: e.collective_compute(
        "AllGather", ALU.bypass, replica_groups=[[0, 1], [2, 3], [4, 5], [6, 7]],
        ins=[cc_in.ap().opt()], outs=[cc_out.ap().opt()]), inc=1)
    em.barrier()
    em.dma("d", lambda e: e.dma_start(
        out=S32[:], in_=AP(cc_out, 0, [[512, 128], [512 * 128, 8], [1, 512]])))
    em.barrier()
    em.cop("v", lambda e: e.tensor_scalar(S32[:].rearrange("p a b -> p (a b)"),
                                          S32[:].rearrange("p a b -> p (a b)"),
                                          SMASK[:], None, ALU.mult))
    if dbg:
        em.dma("d", lambda e: e.dma_start(out=dbg_out["d_s32"],
                                          in_=S32[:].rearrange("p a b -> p (a b)")),
               waits=[em.tok("v")])
    em.barrier()
    em.flush()
    ph1.close()
    if stage == 2:
        return nc

    ph3 = contextlib.ExitStack()
    sb3 = lambda n, s, d=F32: ph3.enter_context(nc.sbuf_tensor(n, s, d))
    WO = sb3("WO", [128, 16, D], BF16)
    OF = sb3("OF", [128, 16, T])
    TMQ = sb3("TMQ", [128, 16, T])
    OQ = sb3("OQ", [128, 16, T], BF16)
    XI3 = sb3("XI3", [128, 8, T], I32)
    T16 = sb3("T16", [128, 2, 512], BF16)
    QTL = sb3("QTL", [128, 8, C], BF16)
    OGL = sb3("OGL", [128, 16, T], BF16)
    SGL = sb3("SGL", [128, 16, T], BF16)
    XPL = sb3("XPL", [128, 8, T])
    XP2 = sb3("XP2", [128, 8, T])
    OUTB = sb3("OUTB", [128, 8, T])
    TP3 = sb3("TP3", [128, 8, T])
    UB3 = sb3("UB3", [128, 8, T], BF16)

    em.dma("d", lambda e: e.dma_start(out=WO[:].rearrange("p a b -> p (a b)"),
                                      in_=woq_st.rearrange("p a b -> p (a b)")))
    em.barrier()

    em.pipe = True
    for c in range(nchunks):
        tl = []
        tl.append(em.dma("d", lambda e, c=c: e.dma_start(
            out=OGL[:].rearrange("p a b -> p (a b)"), in_=og_st[c])))
        tl.append(em.dma("d", lambda e, c=c: e.dma_start(
            out=SGL[:].rearrange("p a b -> p (a b)"), in_=sg_st[c])))
        tl.append(em.dma("d", lambda e, c=c: e.dma_start(
            out=XPL[:].rearrange("p a b -> p (a b)"), in_=xp_st[c])))
        em.mb()
        for ic in range(icpc):
            gic = c * icpc + ic
            tsl = slice(ic * C, (ic + 1) * C)
            tq3 = em.dma("pd", lambda e, gic=gic: e.dma_start(
                out=QTL[:].rearrange("p a b -> p (a b)"), in_=qt_st[gic]),
                waits=[rdt.get("QTL")])
            tse = em.cop("s", lambda e, gic=gic: e.activation(
                ELT[:], AP(LTAB, gic % 32, [[8 * 32, 128], [32, 8]]), AF.Exp),
                waits=[rdt.get("ELT")])
            for h in range(H):
                po = [PO1, PO2][h % 2]
                pon = "PO1" if h % 2 == 0 else "PO2"
                for kk in range(2):
                    em.cop("v", lambda e, h=h, kk=kk: e.tensor_scalar(
                        T16[:, kk, :], S32[:, 2 * h + kk, :],
                        ELT[:, 2 * h + kk:2 * h + kk + 1], None, ALU.mult),
                        waits=[tse, rdt.get("T16")])
                tt16 = em.tok("v")
                rdt["ELT"] = tt16
                tc = []
                for j in range(4):
                    for kk in range(2):
                        tc.append(em.cop("t", lambda e, h=h, j=j, kk=kk, po=po:
                                         e.matmul(
                            po[:, j * 128:(j + 1) * 128],
                            T16[:, kk, j * 128:(j + 1) * 128],
                            QTL[:, 2 * h + kk, :], start=(kk == 0), stop=(kk == 1)),
                            waits=[tt16, tq3, rdt.get(pon)]))
                rdt["T16"] = rdt["QTL"] = tc[-1]
                rdt[pon] = em.cop("v", lambda e, h=h, po=po, tsl=tsl: e.tensor_tensor(
                    OF[:, 4 * h:4 * h + 4, tsl],
                    po[:, 0:512].rearrange("p (a b) -> p a b", a=4),
                    SGL[:, 4 * h:4 * h + 4, tsl], ALU.mult), waits=tc + tl)
                em.mb()
        em.cop("v", lambda e: e.tensor_tensor(OF[:].rearrange("p a b -> p (a b)"),
                                              OF[:].rearrange("p a b -> p (a b)"),
                                              OGL[:].rearrange("p a b -> p (a b)"),
                                              ALU.add))
        em.mb()
        em.cop("s", lambda e: e.activation(OGL[:].rearrange("p a b -> p (a b)"),
                                           OF[:].rearrange("p a b -> p (a b)"),
                                           AF.Square), waits=[em.tok("v")])
        t0 = []
        for k in range(16):
            t0.append(em.cop("t", lambda e, k=k: e.matmul(
                PA[0:1, 0:256], ONESC[:], OGL[:, k, :], start=(k == 0),
                stop=(k == 15)), waits=[em.tok("s"), rdt.get("PA"), rdt.get("SROW")]))
        tr = em.cop("v", lambda e: e.tensor_scalar(SROW[0:1, 0:256], PA[0:1, 0:256],
                                                   1.0 / DV, None, ALU.mult), waits=t0)
        rdt["PA"] = tr
        for tt in range(2):
            tk = em.cop("t", lambda e, tt=tt: e.transpose(
                PO2[0:128, 448 + tt:449 + tt], SROW[0:1, tt * 128:(tt + 1) * 128],
                IDENTF[0:1, 0:1]), waits=[tr, rdt.get("PO2")])
            rdt["SROW"] = tk
            rdt["PO2"] = em.cop("v", lambda e, tt=tt: e.tensor_copy(
                SC[:, tt, 0:1], PO2[0:128, 448 + tt:449 + tt]), waits=[tk])
        rsqrt_cols(0, 40, 1e-8)
        tw3 = em.cop("v", lambda e: e.tensor_tensor(
            TMQ[:].rearrange("p a b -> p (a b)"), OF[:].rearrange("p a b -> p (a b)"),
            AP(NO, 0, [[16, 128], [1, 16], [0, T]]), ALU.mult), waits=[em.tok("s")])
        rowmax_col(TMQ[:], 16, 1, [tw3])
        for tt in range(2):
            em.cop("v", lambda e, tt=tt: e.tensor_mul(SC[:, tt, 1:2], SC[:, tt, 1:2],
                                                      SC[:, tt, 0:1]))
            em.cop("v", lambda e, tt=tt: e.tensor_scalar(SC[:, tt, 1:2], SC[:, tt, 1:2],
                                                         1e-5, None, ALU.max))
            em.cop("v", lambda e, tt=tt: e.reciprocal(COLS[:, 44:45], SC[:, tt, 1:2]))
            em.cop("v", lambda e, tt=tt: e.scalar_tensor_tensor(
                SC[:, tt, 3:4], COLS[:, 44:45], 127.0, SC[:, tt, 0:1],
                ALU.mult, ALU.mult))
            em.cop("v", lambda e, tt=tt: e.scalar_tensor_tensor(
                SC[:, tt, 4:5], SC[:, tt, 1:2], 1.0 / 127.0, MW[:, 6:7],
                ALU.mult, ALU.mult))
        em.cop("v", lambda e: e.tensor_copy(AP(TMRB, 0, [[256, 128], [1, 2]]),
                                            SC[:, :, 4]), waits=[rdt.get("TMRB")])
        em.cop("v", lambda e: e.tensor_copy(SC[:, :, 5], AP(TMRB, 0, [[256, 128], [1, 2]])))
        em.cop("v", lambda e: e.tensor_sub(SC[:, :, 6], SC[:, :, 4], SC[:, :, 5]))
        fl5 = flush_sc(3, 7, [em.tok("v")])
        em.mb()
        phiob, tko = bcast(3, fl5)
        em.cop("v", lambda e, phiob=phiob: e.tensor_tensor(
            TMQ[:].rearrange("p a b -> p (a b)"), TMQ[:].rearrange("p a b -> p (a b)"),
            AP(phiob.tensor, phiob.offset, [phiob.ap[0], [0, 16], phiob.ap[1]]),
            ALU.mult), waits=[tko])
        for half in range(2):
            em.cop("v", lambda e, half=half: e.tensor_scalar(
                XI3[:].rearrange("p a b -> p (a b)"),
                TMQ[:, half * 8:(half + 1) * 8, :].rearrange("p a b -> p (a b)"),
                RC, RC, ALU.add, ALU.subtract), waits=[em.tok("s")])
            em.cop("s", lambda e, half=half: e.activation(
                OQ[:, half * 8:(half + 1) * 8, :].rearrange("p a b -> p (a b)"),
                XI3[:].rearrange("p a b -> p (a b)"), AF.Copy), waits=[em.tok("v")])
            em.mb()
        s_ = bcslot[0]
        bcslot[0] = 1 - s_
        tp1 = em.cop("t", lambda e, s_=s_: e.matmul(
            PBC[:, s_ * 256:(s_ + 1) * 256], SELT[:, 5, :],
            SCR[:].rearrange("a b c -> a (b c)"), start=True, stop=False),
            waits=fl5 + [em.tok("v"), em.tok("s")])
        tp2 = em.cop("t", lambda e, s_=s_: e.matmul(
            PBC[:, s_ * 256:(s_ + 1) * 256], SELT[:, 6, :],
            SCR[:].rearrange("a b c -> a (b c)"), start=False, stop=True))
        em.cop("v", lambda e, s_=s_: e.tensor_copy(
            MBC[:, 0, :], PBC[:, s_ * 256:(s_ + 1) * 256]), waits=[tp1, tp2])
        two = []
        for m in range(8):
            pd = PA if m < 4 else PB
            off = (m % 4) * 256
            for k in range(16):
                two.append(em.cop("t", lambda e, m=m, k=k, pd=pd, off=off: e.matmul(
                    pd[:, off:off + 256], WO[:, k, m * 128:(m + 1) * 128], OQ[:, k, :],
                    start=(k == 0), stop=(k == 15)),
                    waits=[em.tok("s"), rdt.get("PA" if m < 4 else "PB")]))
        for half, pd in [(0, PA), (1, PB)]:
            rdt["PA" if half == 0 else "PB"] = em.cop(
                "v", lambda e, half=half, pd=pd: e.tensor_tensor(
                    TP3[:, half * 4:(half + 1) * 4, :].rearrange("p a b -> p (a b)"),
                    pd[:, 0:1024],
                    AP(MBC, 0, [[2 * T, 128], [0, 4], [1, T]]), ALU.mult), waits=two)
        em.cop("v", lambda e: e.tensor_add(XP2[:].rearrange("p a b -> p (a b)"),
                                           TP3[:].rearrange("p a b -> p (a b)"),
                                           XPL[:].rearrange("p a b -> p (a b)")))
        em.mb()
        rdt["ST_MM"] = rdt["XQB"] = two[-1]
        ln_to_ubf(XP2, 8, 9, 10, OQ[:, 0:8, :], OQ[:, 8:16, :], UB3)
        emit_ffn(MBC[0:64, 0, :], XP2, OUTB, UB3, [em.tok("v")])
        if dbg and c == 0:
            em.dma("d", lambda e: e.dma_start(out=dbg_out["d_of"],
                                              in_=OF[:].rearrange("p a b -> p (a b)")),
                   waits=[em.tok("v")])
        em.mb()
        em.dma("d", lambda e, c=c: e.dma_start(
            out=yT.rearrange("(k p) t -> p k t", p=128)[:, :, c * T:(c + 1) * T],
            in_=OUTB[:]), waits=[em.tok("v")])
        em.barrier()

    em.flush()
    ph3.close()
    return nc


_CACHE = {}


def _host_inputs(inputs):
    f32 = np.float32
    wq = inputs["L0_attn_wq"]; wk = inputs["L0_attn_wk"]
    wv = inputs["L0_attn_wv"]; wg = inputs["L0_attn_wg"]
    wgk1 = inputs["L0_attn_wgk1"]; wgk2 = inputs["L0_attn_wgk2"]
    wo = inputs["L0_attn_wo"]
    W1 = np.asarray(inputs["L0_ffn_W1"], f32); b1 = np.asarray(inputs["L0_ffn_b1"], f32)
    W2 = np.asarray(inputs["L0_ffn_W2"], f32)
    bin_ = np.asarray(inputs["L0_ffn_bin_"], f32)
    com = {}
    com["wqT"] = np.ascontiguousarray(np.asarray(wq, f32).T)
    com["wkT"] = np.ascontiguousarray(np.asarray(wk, f32).T)
    com["wvT"] = np.ascontiguousarray(np.asarray(wv, f32).T)
    com["wgT"] = np.ascontiguousarray(np.asarray(wg, f32).T)
    com["wgk1T"] = np.ascontiguousarray(np.asarray(wgk1, f32).T)
    com["wgk2T"] = np.ascontiguousarray(np.asarray(wgk2, f32).T)
    com["woT"] = np.ascontiguousarray(np.asarray(wo, f32).T)
    arr = lambda v, nt: np.ascontiguousarray(np.asarray(v, f32).reshape(nt, 128).T)
    com["nq_a"] = arr(inputs["L0_attn_nq"], 8)
    com["nk_a"] = arr(inputs["L0_attn_nk"], 8)
    com["nv_a"] = arr(inputs["L0_attn_nv"], 8)
    com["ng_a"] = arr(inputs["L0_attn_ng"], 8)
    com["ngk1_a"] = arr(inputs["L0_attn_ngk1"], 8)
    com["ngk2_a"] = np.ascontiguousarray(np.asarray(inputs["L0_attn_ngk2"], f32).reshape(16, 1))
    com["bgk2_a"] = arr(inputs["L0_attn_bgk2"], 8)
    com["no_a"] = arr(inputs["L0_attn_no"], 16)
    com["Win"] = np.ascontiguousarray(np.asarray(inputs["L0_ffn_Win"], f32))
    com["WoutT"] = np.ascontiguousarray(np.asarray(inputs["L0_ffn_Wout"], f32))
    repw1 = np.zeros((64, 8, 128), f32)
    sumw2 = np.zeros((128, 8, 64), f32)
    w1a = np.zeros((128, 8), f32); b1a = np.zeros((128, 8), f32)
    bina = np.zeros((128, 8), f32)
    for k in range(8):
        for mloc in range(8):
            m = 8 * k + mloc
            for i in range(16):
                p = mloc * 16 + i
                repw1[m, k, p] = W1[m, i]
                sumw2[p, k, m] = W2[m, i]
                w1a[p, k] = W1[m, i]
                b1a[p, k] = b1[m, i]
                bina[p, k] = bin_[m]
    com["repw1"] = repw1; com["sumw2"] = sumw2
    com["w1_a"] = w1a; com["b1_a"] = b1a; com["binmi_a"] = bina
    com["b2_a"] = np.ascontiguousarray(np.asarray(inputs["L0_ffn_b2"], f32).reshape(64, 1))
    com["bout_r"] = np.ascontiguousarray(
        np.asarray(inputs["L0_ffn_bout"], f32).reshape(1, 8, 128).astype(NB))
    com["triu"] = np.ascontiguousarray(np.triu(np.ones((128, 128), np.float32)))
    com["ident"] = np.ascontiguousarray(np.eye(128, dtype=f32).astype(NB))
    com["identf"] = np.ascontiguousarray(np.eye(128, dtype=f32))
    com["ones_c"] = np.ones((128, 1), f32).astype(NB)
    com["ones_r2"] = np.ones((1, 256), f32).astype(NB)
    com["ones_cf"] = np.ones((128, 1), f32)
    com["ones_rf"] = np.ones((1, 128), f32)
    sel = np.zeros((32, 32, 128), f32)
    for j in range(32):
        sel[j, j, :] = 1.0
    com["sel"] = np.ascontiguousarray(sel.astype(NB))
    return com


def kernel(**inputs):
    x = np.asarray(inputs["L0_attn_hidden_states"], np.float32)
    B, L, Dm = x.shape
    half = L // 2
    com = _host_inputs(inputs)
    if "nc" not in _CACHE:
        _CACHE["nc"] = build_nc(16)
    nc = _CACHE["nc"]
    in_maps = []
    for core in range(8):
        b, s = core // 2, core % 2
        m = dict(com)
        m["xT"] = np.ascontiguousarray(x[b, s * half:(s + 1) * half, :].T)
        m["smask"] = np.full((128, 1), float(s), np.float32)
        in_maps.append(m)
    res = run_bass_kernel_spmd(nc, in_maps, core_ids=list(range(8)))
    out = np.empty_like(x)
    for core in range(8):
        b, s = core // 2, core % 2
        out[b, s * half:(s + 1) * half, :] = np.asarray(res.results[core]["yT"]).T
    return out


# version 05
# speedup vs baseline: 1.1370x; 1.0120x over previous
import numpy as np
import ml_dtypes
import contextlib
import concourse.bass as bass
import concourse.mybir as mybir
from concourse.bass_utils import run_bass_kernel_spmd

F32 = mybir.dt.float32
BF16 = mybir.dt.bfloat16
I32 = mybir.dt.int32
U8 = mybir.dt.uint8
AF = mybir.ActivationFunctionType
ALU = mybir.AluOpType
AX = mybir.AxisListType
AP = bass.AP
NB = ml_dtypes.bfloat16

RC = 12582912.0
D = 1024
DV = 2048
H = 4
T = 256
C = 128
SERIAL = False


class EM:
    ENGS = ("t", "v", "s", "p", "d")
    SEMK = ("t", "v", "s", "p", "d", "pd")

    def __init__(self, nc, stack):
        self.nc = nc
        self.q = {e: [] for e in self.ENGS}
        self.cnt = {e: 0 for e in self.SEMK}
        self.sems = {e: stack.enter_context(nc.semaphore("sem_" + e)) for e in self.SEMK}
        self.waited = {e: {e2: 0 for e2 in self.SEMK} for e in self.ENGS}

    def op(self, eng, fn, waits=(), inc=0):
        qk = "p" if eng == "pd" else eng
        wl = []
        if eng in ("v", "s", "p", "d", "pd") and self.cnt[eng] > 0:
            wl.append((self.sems[eng], self.cnt[eng]))
        for w in waits:
            if w is None:
                continue
            se, sv = w
            if se == eng:
                continue
            if self.waited[qk][se] < sv:
                self.waited[qk][se] = sv
                wl.append((self.sems[se], sv))
        sem = self.sems[eng]

        def run(e, wl=wl, fn=fn, inc=inc, sem=sem):
            for (s, v) in wl:
                e.wait_ge(s, v)
            ins = fn(e)
            if inc:
                ins.then_inc(sem, inc)
            return ins
        self.q[qk].append(run)
        if inc:
            self.cnt[eng] += inc
        return (eng, self.cnt[eng])

    def cop(self, eng, fn, waits=()):
        return self.op(eng, fn, waits=waits, inc=1)

    def dma(self, eng, fn, waits=()):
        return self.op(eng, fn, waits=waits, inc=16)

    def tok(self, eng):
        return (eng, self.cnt[eng])

    def barrier(self, engs=None):
        engs = engs or self.ENGS
        toks = [(e, self.cnt[e]) for e in (self.SEMK if engs is self.ENGS or engs == self.ENGS else engs)]
        for e in engs:
            for (se, sv) in toks:
                if se != e and self.waited[e][se] < sv:
                    self.waited[e][se] = sv
                    sem = self.sems[se]
                    self.q[e].append(lambda en, s=sem, v=sv: en.wait_ge(s, v))

    def mb(self):
        if SERIAL or not getattr(self, "pipe", False):
            self.barrier()

    def flush(self):
        with self.nc.Block() as blk:
            for ename, attr in [("t", "tensor"), ("v", "vector"), ("s", "scalar"),
                                ("p", "gpsimd"), ("d", "sync")]:
                ops = self.q[ename]
                if not ops:
                    continue
                getattr(blk, attr)(lambda e, ops=tuple(ops): [f(e) for f in ops])
        self.q = {e: [] for e in self.ENGS}


def build_nc(nchunks=16, dbg=False, stage=3, p1cut=99, lncut=99, fcut=99, g2cut=99):
    ntok = nchunks * T
    nic = ntok // C
    icpc = T // C
    import os as _os
    nc = bass.Bass("TRN2", target_bir_lowering=False,
                   detect_race_conditions=_os.environ.get("KRACE", "1") == "1")
    din = {}

    def inp(name, shape, dt=F32):
        din[name] = nc.dram_tensor(name, shape, dt, kind="ExternalInput").ap()
        return din[name]

    xT = inp("xT", [D, ntok])
    for nm, shp in [("wqT", [D, D]), ("wkT", [D, D]), ("wvT", [D, DV]),
                    ("wgT", [D, DV]), ("wgk1T", [D, 16]), ("wgk2T", [16, D]),
                    ("woT", [DV, D]), ("nq_a", [128, 8]), ("nk_a", [128, 8]),
                    ("nv_a", [128, 8]), ("ng_a", [128, 8]), ("ngk1_a", [128, 8]),
                    ("ngk2_a", [16, 1]), ("bgk2_a", [128, 8]), ("no_a", [128, 16]),
                    ("Win", [D, 64]), ("WoutT", [64, D]), ("repw1", [64, 8, 128]),
                    ("w1_a", [128, 8]), ("b1_a", [128, 8]), ("binmi_a", [128, 8]),
                    ("sumw2", [128, 8, 64]), ("b2_a", [64, 1]),
                    ("identf", [128, 128]), ("ones_cf", [128, 1]), ("ones_rf", [1, 128]),
                    ("smask", [128, 1])]:
        inp(nm, shp)
    inp("triu", [128, 128])
    inp("ident", [128, 128], BF16)
    inp("ones_c", [128, 1], BF16)
    inp("ones_r2", [1, 256], BF16)
    inp("sel", [32, 32, 128], BF16)
    inp("bout_r", [1, 8, 128], BF16)

    yT = nc.dram_tensor("yT", [D, ntok], F32, kind="ExternalOutput").ap()
    dbg_out = {}
    if dbg:
        for nm, shp, dt in [("d_rep", [128, 8 * 128], BF16), ("d_pa", [128, 1024], F32),
                            ("d_ub", [128, 8 * T], BF16), ("d_fs", [128, T], BF16),
                            ("d_fh", [128, 8 * T], BF16), ("d_fg", [128, T], BF16),
                            ("d_sc", [128, 2 * 32], F32), ("d_mbc", [128, 2 * T], F32),
                            ("d_xp", [128, 8 * T], F32), ("d_q", [128, 8 * T], BF16),
                            ("d_k", [128, 8 * T], BF16), ("d_gl", [128, 8 * T], BF16),
                            ("d_og", [128, 16 * T], BF16), ("d_of", [128, 16 * T], F32),
                            ("d_s32", [128, 8 * 512], F32), ("d_v", [128, 2 * 2048], BF16)]:
            dbg_out[nm] = nc.dram_tensor(nm, shp, dt, kind="ExternalOutput").ap()

    og_st = nc.dram_tensor("og_st", [nchunks, 128, 16 * T], BF16).ap()
    sg_st = nc.dram_tensor("sg_st", [nchunks, 128, 16 * T], BF16).ap()
    qt_st = nc.dram_tensor("qt_st", [nic, 128, 8 * C], BF16).ap()
    xp_st = nc.dram_tensor("xp_st", [nchunks, 128, 8 * T], F32).ap()
    wvq_st = nc.dram_tensor("wvq_st", [128, 8, DV], BF16).ap()
    wgq_st = nc.dram_tensor("wgq_st", [128, 8, DV], BF16).ap()
    woq_st = nc.dram_tensor("woq_st", [128, 16, D], BF16).ap()
    cc_in = nc.dram_tensor("cc_in", [1024, 512], F32)
    cc_out = nc.dram_tensor("cc_out", [2048, 512], F32)

    perm = contextlib.ExitStack()
    em = EM(nc, perm)
    sbp = lambda n, s, d=F32: perm.enter_context(nc.sbuf_tensor(n, s, d))

    WGK1 = sbp("WGK1", [128, 8, 16], BF16)
    WGK2F = sbp("WGK2F", [128, D], BF16)
    WGK2 = WGK2F[0:16, :]
    WIN = sbp("WIN", [128, 8, 64], BF16)
    WOUTF = sbp("WOUTF", [128, D], BF16)
    WOUT = WOUTF[0:64, :]
    REPF = sbp("REPF", [128, 8, 128], BF16)
    REP = REPF[0:64, :, :]
    SUM2 = sbp("SUM2", [128, 8, 64], BF16)
    B1E = sbp("B1E", [128, 8])
    NQ = sbp("NQ", [128, 8]); NK = sbp("NK", [128, 8]); NV = sbp("NV", [128, 8])
    NG = sbp("NG", [128, 8]); NGK1 = sbp("NGK1", [128, 8]); NGK2F = sbp("NGK2F", [128, 1]); NGK2 = NGK2F[0:16, :]
    BGK2 = sbp("BGK2", [128, 8]); NO = sbp("NO", [128, 16])
    B2F = sbp("B2F", [128, 1]); B2 = B2F[0:64, :]
    BOUTRF = sbp("BOUTRF", [128, 8, 128], BF16); BOUTR = BOUTRF[0:1, :, :]
    TRIU = sbp("TRIU", [128, 128])
    IDENT = sbp("IDENT", [128, 128], BF16)
    IDENTF = sbp("IDENTF", [128, 128], F32)
    ONESC = sbp("ONESC", [128, 1], BF16)
    ONESR2F = sbp("ONESR2F", [128, 256], BF16); ONESR2 = ONESR2F[0:1, :]
    ONESCF = sbp("ONESCF", [128, 1], F32)
    ONESRFF = sbp("ONESRFF", [128, 128], F32); ONESRF = ONESRFF[0:1, :]
    SELTF = sbp("SELTF", [128, 32, 128], BF16); SELT = SELTF[0:32, :, :]
    SMASK = sbp("SMASK", [128, 1])
    ZEROT = sbp("ZEROT", [128, 128])
    CEPS5 = sbp("CEPS5", [128, 1]); CEPS8 = sbp("CEPS8", [128, 1])
    MW = sbp("MW", [128, 8]); WSC = sbp("WSC", [128, 8])
    S32 = sbp("S32", [128, 8, 512])
    LTAB = sbp("LTAB", [128, 8, 32])
    COLS = sbp("COLS", [128, 48])
    SC = sbp("SC", [128, 2, 32])
    SCRF = sbp("SCRF", [128, 2, 128], BF16); SCR = SCRF[0:32, :, :]
    SROWF = sbp("SROWF", [128, 512]); SROW = SROWF[0:1, :]
    MBC = sbp("MBC", [128, 2, T])
    FSF = sbp("FSF", [128, T], BF16); FS = FSF[0:64, :]
    FH = sbp("FH", [128, 8, T], BF16)
    FGF = sbp("FGF", [128, T], BF16); FG = FGF[0:64, :]
    I1F = sbp("I1F", [128, T]); I1 = I1F[0:16, :]
    I1BF = sbp("I1BF", [128, T], BF16); I1B = I1BF[0:16, :]
    SQ16F = sbp("SQ16F", [128, T], BF16); SQ16 = SQ16F[0:16, :]
    TMRB = sbp("TMRB", [128, 2, 128], BF16)
    ELT = sbp("ELT", [128, 8])

    PA = perm.enter_context(nc.psum_tensor("PA", [128, 1024], F32))
    PB = perm.enter_context(nc.psum_tensor("PB", [128, 1024], F32))
    PO1 = perm.enter_context(nc.psum_tensor("PO1", [128, 512], F32))
    PO2 = perm.enter_context(nc.psum_tensor("PO2", [128, 512], F32))
    PBC = perm.enter_context(nc.psum_tensor("PBC", [128, 512], F32))
    PTR = perm.enter_context(nc.psum_tensor("PTR", [128, 256], BF16))

    ph1 = contextlib.ExitStack()
    sb1 = lambda n, s, d=F32: ph1.enter_context(nc.sbuf_tensor(n, s, d))
    WQ = sb1("WQ", [128, 8, D], BF16)
    WK = sb1("WK", [128, 8, D], BF16)
    WST = sb1("WST", [128, 2, 8, 256], BF16)

    stup = contextlib.ExitStack()
    sbs = lambda n, s, d=F32: stup.enter_context(nc.sbuf_tensor(n, s, d))
    STG = sbs("STG", [128, 2, 4096])
    QSC = sbs("QSC", [128, 4096])
    QSB = sbs("QSB", [128, 4096], BF16)
    FSTG = sbs("FSTG", [128, 8, 64])
    WOSF = sbs("WOSF", [128, D]); WOS = WOSF[0:64, :]
    RPSF = sbs("RPSF", [128, 8, 128]); RPS = RPSF[0:64, :, :]
    SMS = sbs("SMS", [128, 8, 64])

    for name, dst in [("nq_a", NQ), ("nk_a", NK), ("nv_a", NV), ("ng_a", NG),
                      ("ngk1_a", NGK1), ("ngk2_a", NGK2), ("bgk2_a", BGK2),
                      ("no_a", NO), ("b2_a", B2), ("bout_r", BOUTR),
                      ("triu", TRIU), ("ident", IDENT), ("identf", IDENTF),
                      ("ones_c", ONESC), ("ones_r2", ONESR2), ("ones_cf", ONESCF),
                      ("ones_rf", ONESRF), ("sel", SELT), ("smask", SMASK)]:
        em.dma("d", lambda e, s=din[name], d=dst: e.dma_start(out=d[:], in_=s))
    em.dma("d", lambda e: e.dma_start(out=FSTG[:], in_=din["Win"].rearrange("(k p) m -> p k m", p=128)))
    em.dma("d", lambda e: e.dma_start(out=WOS[:], in_=din["WoutT"]))
    em.dma("d", lambda e: e.dma_start(out=RPS[:], in_=din["repw1"]))
    em.dma("d", lambda e: e.dma_start(out=SMS[:], in_=din["sumw2"]))
    em.dma("d", lambda e: e.dma_start(out=COLS[:, 0:8], in_=din["w1_a"]))
    em.dma("d", lambda e: e.dma_start(out=COLS[:, 8:16], in_=din["b1_a"]))
    em.dma("d", lambda e: e.dma_start(out=COLS[:, 16:24], in_=din["binmi_a"]))
    tld = em.tok("d")
    em.cop("v", lambda e: e.memset(ZEROT[:], 0.0))
    em.cop("v", lambda e: e.memset(CEPS5[:], 1e-5))
    em.cop("v", lambda e: e.memset(CEPS8[:], 1e-8))
    em.cop("v", lambda e: e.memset(S32[:].rearrange("p a b -> p (a b)"), 0.0))
    em.cop("v", lambda e: e.memset(LTAB[:].rearrange("p a b -> p (a b)"), 0.0))
    em.cop("v", lambda e: e.memset(SC[:].rearrange("p a b -> p (a b)"), 0.0))
    em.cop("v", lambda e: e.memset(COLS[:, 24:48], 0.0), waits=[tld])
    em.cop("v", lambda e: e.memset(SROW[:], 0.0))
    em.cop("v", lambda e: e.tensor_copy(WIN[:].rearrange("p a b -> p (a b)"),
                                        FSTG[:].rearrange("p a b -> p (a b)")), waits=[tld])
    em.cop("v", lambda e: e.tensor_copy(WOUT[:], WOS[:]))
    em.cop("v", lambda e: e.tensor_copy(REP[:].rearrange("p a b -> p (a b)"),
                                        RPS[:].rearrange("p a b -> p (a b)")))
    em.cop("v", lambda e: e.tensor_copy(SUM2[:].rearrange("p a b -> p (a b)"),
                                        SMS[:].rearrange("p a b -> p (a b)")))
    em.cop("v", lambda e: e.tensor_mul(B1E[:], COLS[:, 0:8], COLS[:, 16:24]), waits=[tld])
    em.cop("v", lambda e: e.tensor_add(B1E[:], B1E[:], COLS[:, 8:16]))
    em.barrier()

    skipwq = _os.environ.get("SKIPWQ", "0") == "1"
    if skipwq:
        wjobs = []
    else:
        pass
    wjobs = [] if skipwq else [
        (din["wqT"], ("sbuf", WQ), 8, D, 0),
        (din["wkT"], ("sbuf", WK), 8, D, 1),
        (din["wvT"], ("hbm", wvq_st), 8, DV, 2),
        (din["wgT"], ("hbm", wgq_st), 8, DV, 3),
        (din["wgk1T"], ("sbuf", WGK1), 8, 16, 4),
        (din["woT"], ("hbm", woq_st), 16, D, 6),
    ]
    for (hbm, dest, nkt, dout, widx) in wjobs:
        per_unit = max(1, 4096 // dout)
        units = [(k0, min(k0 + per_unit, nkt)) for k0 in range(0, nkt, per_unit)]
        src3 = hbm.rearrange("(k p) o -> p k o", p=128)
        prevred = None
        for ui, (k0, k1) in enumerate(units):
            fl = (k1 - k0) * dout
            slot = ui % 2
            tin = em.dma("d", lambda e, k0=k0, k1=k1, slot=slot, src3=src3, dout=dout:
                         e.dma_start(out=STG[:, slot, 0:(k1 - k0) * dout]
                                     .rearrange("p (a b) -> p a b", a=(k1 - k0)),
                                     in_=src3[:, k0:k1, :]),
                         waits=[prevred] if prevred else ())
            em.cop("v", lambda e, slot=slot, fl=fl: e.reduce_sum(
                COLS[:, 40:41], STG[:, slot, 0:fl], axis=AX.X,
                apply_absolute_value=True), waits=[tin])
            if ui == 0:
                em.cop("v", lambda e: e.tensor_copy(COLS[:, 41:42], COLS[:, 40:41]))
            else:
                em.cop("v", lambda e: e.tensor_add(COLS[:, 41:42], COLS[:, 41:42],
                                                   COLS[:, 40:41]))
            prevred = em.tok("v")
        em.cop("t", lambda e: e.matmul(PA[0:1, 0:1], ONESCF[:], COLS[:, 41:42],
                                       start=True, stop=True), waits=[prevred])
        tmm = em.tok("t")
        numel = nkt * 128 * dout
        em.cop("v", lambda e, numel=numel: e.tensor_scalar(
            SROW[0:1, 0:1], PA[0:1, 0:1], 1.0 / numel, 1e-5, ALU.mult, ALU.max),
            waits=[tmm])
        tms = em.tok("v")
        em.cop("t", lambda e: e.matmul(PB[:, 0:1], ONESRF[:], SROW[0:1, 0:1],
                                       start=True, stop=True), waits=[tms])
        tmb = em.tok("t")
        em.cop("v", lambda e, widx=widx: e.tensor_copy(MW[:, widx:widx + 1], PB[:, 0:1]),
               waits=[tmb])
        em.cop("v", lambda e, widx=widx: e.reciprocal(WSC[:, widx:widx + 1],
                                                      MW[:, widx:widx + 1]))
        em.barrier()
        for ui, (k0, k1) in enumerate(units):
            fl = (k1 - k0) * dout
            slot = ui % 2
            tin = em.dma("d", lambda e, k0=k0, k1=k1, slot=slot, src3=src3, dout=dout:
                         e.dma_start(out=STG[:, slot, 0:(k1 - k0) * dout]
                                     .rearrange("p (a b) -> p a b", a=(k1 - k0)),
                                     in_=src3[:, k0:k1, :]),
                         waits=[em.tok("v"), em.tok("s"), em.tok("d")])
            em.cop("s", lambda e, slot=slot, fl=fl, widx=widx: e.activation(
                QSC[:, 0:fl], STG[:, slot, 0:fl], AF.Identity, scale=WSC[:, widx:widx + 1]),
                waits=[tin])
            tsc = em.tok("s")
            em.cop("v", lambda e, fl=fl: e.tensor_scalar(
                QSC[:, 0:fl], QSC[:, 0:fl], RC, RC, ALU.add, ALU.subtract), waits=[tsc])
            if dest[0] == "sbuf":
                em.cop("v", lambda e, fl=fl, dest=dest, k0=k0, k1=k1: e.tensor_scalar(
                    dest[1][:, k0:k1, :].rearrange("p a b -> p (a b)"),
                    QSC[:, 0:fl], 1.0, -1.0, ALU.min, ALU.max))
            else:
                em.cop("v", lambda e, fl=fl: e.tensor_scalar(
                    QSB[:, 0:fl], QSC[:, 0:fl], 1.0, -1.0, ALU.min, ALU.max))
                em.dma("d", lambda e, fl=fl, dest=dest, k0=k0, k1=k1: e.dma_start(
                    out=dest[1][:, k0:k1, :].rearrange("p a b -> p (a b)"),
                    in_=QSB[:, 0:fl]), waits=[em.tok("v")])
            em.barrier(["v", "s", "d"])
    if skipwq:
        tin = None
    tin = em.dma("d", lambda e: e.dma_start(out=STG[0:16, 0, 0:1024], in_=din["wgk2T"])) if not skipwq else None
    if not skipwq:
      em.cop("v", lambda e: e.reduce_sum(COLS[0:16, 40:41], STG[0:16, 0, 0:1024],
                                       axis=AX.X, apply_absolute_value=True), waits=[tin])
    if not skipwq:
      em.cop("t", lambda e: e.matmul(PA[0:1, 0:1], ONESCF[0:16, :], COLS[0:16, 40:41],
                                   start=True, stop=True), waits=[em.tok("v")])
      em.cop("v", lambda e: e.tensor_scalar(SROW[0:1, 0:1], PA[0:1, 0:1],
                                          1.0 / (16 * 1024), 1e-5, ALU.mult, ALU.max),
           waits=[em.tok("t")])
      em.cop("t", lambda e: e.matmul(PB[:, 0:1], ONESRF[:], SROW[0:1, 0:1],
                                   start=True, stop=True), waits=[em.tok("v")])
      em.cop("v", lambda e: e.tensor_copy(MW[:, 7:8], PB[:, 0:1]), waits=[em.tok("t")])
      em.cop("v", lambda e: e.reciprocal(WSC[:, 7:8], MW[:, 7:8]))
      em.cop("s", lambda e: e.activation(QSC[0:16, 0:1024], STG[0:16, 0, 0:1024],
                                       AF.Identity, scale=WSC[0:16, 7:8]), waits=[em.tok("v")])
      em.cop("v", lambda e: e.tensor_scalar(QSC[0:16, 0:1024], QSC[0:16, 0:1024],
                                          RC, RC, ALU.add, ALU.subtract),
           waits=[em.tok("s")])
      em.cop("v", lambda e: e.tensor_scalar(WGK2[:], QSC[0:16, 0:1024], 1.0, -1.0,
                                          ALU.min, ALU.max))
    em.barrier()
    em.flush()
    stup.close()
    if stage == 0:
        ph1.close()
        return nc

    XA = sb1("XA", [128, 8, T])
    XP = sb1("XP", [128, 8, T])
    U1 = sb1("U1", [128, 8, T])
    QSCR = sb1("QSCR", [128, 8, T])
    XI = sb1("XI", [128, 4, T], I32)
    XQB = sb1("XQB", [128, 8, T], BF16)
    QF = sb1("QF", [128, 8, T], BF16)
    KF = sb1("KF", [128, 8, T], BF16)
    VT = sb1("VT", [128, 2, DV], BF16)
    GFB = sb1("GFB", [128, 16, T], BF16)
    THB = sb1("THB", [128, 16, T], BF16)
    SGB = sb1("SGB", [128, 16, T], BF16)
    OGB = sb1("OGB", [128, 16, T], BF16)
    GL = sb1("GL", [128, 8, T], BF16)
    E1 = sb1("E1", [128, 8, C], BF16)
    E2 = sb1("E2", [128, 8, C], BF16)
    QS = sb1("QS", [128, 8, C], BF16)
    KS = sb1("KS", [128, 8, C], BF16)
    QI = sb1("QI", [128, 8, C], BF16)
    KX = sb1("KX", [128, 8, C], BF16)
    KXT = sb1("KXT", [128, 2, C], BF16)
    ABF = sb1("ABF", [128, 128])
    AB = sb1("AB", [128, 128], BF16)
    S16 = sb1("S16", [128, 2, 512], BF16)
    WSCR = XA
    BB3 = AP(QSCR, 0, [[8 * T, 128], [T, 8], [1, C]])
    BS3 = AP(QSCR, C, [[8 * T, 128], [T, 8], [1, C]])

    def BBs(k):
        return AP(QSCR, k * T, [[8 * T, 128], [1, C]])

    bcslot = [0]
    rdt = {}

    def bcast(j, waits=()):
        s = bcslot[0]
        bcslot[0] = 1 - s
        tk = em.cop("t", lambda e, s=s, j=j: e.matmul(
            PBC[:, s * 256:(s + 1) * 256], SELT[:, j, :],
            SCR[:].rearrange("a b c -> a (b c)"), start=True, stop=True),
            waits=list(waits) + [em.tok("v")])
        rdt["SCR_rd"] = tk
        return PBC[:, s * 256:(s + 1) * 256], tk

    def flush_sc(c0, c1, waits):
        tb = em.cop("v", lambda e: e.tensor_copy(
            TMRB[:, :, 0:32].rearrange("p a b -> p (a b)") if False else
            AP(TMRB, 0, [[256, 128], [128, 2], [1, 32]]), SC[:, :, 0:32]),
            waits=list(waits) + [rdt.get("TMRB")])
        tks = []
        for tt in range(2):
            tks.append(em.cop("t", lambda e, tt=tt: e.transpose(
                PTR[0:32, tt * 128:tt * 128 + 128],
                AP(TMRB, tt * 128, [[256, 128], [1, 32]]), IDENT[:]),
                waits=[tb, rdt.get("PTR")]))
        rdt["TMRB"] = tks[-1]
        out_tks = [em.cop("v", lambda e: e.tensor_copy(
            SCR[:].rearrange("a b c -> a (b c)"), PTR[0:32, 0:256]),
            waits=tks + [rdt.get("SCR_rd")])]
        rdt["PTR"] = out_tks[0]
        return out_tks

    def stats_cols(src_bf, src_sq, mcol, sqcol, waits):
        t0 = []
        for (src, r0) in [(src_bf, 0), (src_sq, 1)]:
            for k in range(8):
                t0.append(em.cop("t", lambda e, src=src, k=k, r0=r0: e.matmul(
                    PA[0:1, r0 * 256:(r0 + 1) * 256], ONESC[:], src[:, k, :],
                    start=(k == 0), stop=(k == 7)),
                    waits=list(waits) + [rdt.get("PA"), rdt.get("SROW")]))
        rdt["ST_MM"] = t0[-1]
        tr = em.cop("v", lambda e: e.tensor_scalar(SROW[0:1, 0:512], PA[0:1, 0:512],
                                                   1.0 / 1024.0, None, ALU.mult), waits=t0)
        rdt["PA"] = tr
        tks = []
        for (r0, col) in [(0, mcol), (1, sqcol)]:
            for tt in range(2):
                tk = em.cop("t", lambda e, r0=r0, tt=tt: e.transpose(
                    PO2[0:128, 448 + r0 * 2 + tt:449 + r0 * 2 + tt],
                    SROW[0:1, r0 * 256 + tt * 128:r0 * 256 + (tt + 1) * 128],
                    IDENTF[0:1, 0:1]), waits=[tr, rdt.get("PO2")])
                rdt["SROW"] = tk
                tks.append(em.cop("v", lambda e, col=col, tt=tt, r0=r0: e.tensor_copy(
                    SC[:, tt, col:col + 1],
                    PO2[0:128, 448 + r0 * 2 + tt:449 + r0 * 2 + tt]), waits=[tk]))
        rdt["PO2"] = tks[-1]
        return tks

    def rsqrt_cols(rstdcol, vcols, epsval):
        em.cop("v", lambda e: e.tensor_scalar(COLS[:, vcols:vcols + 2],
                                              SC[:, :, rstdcol], epsval, None, ALU.add))
        em.cop("s", lambda e: e.activation(SC[:, :, rstdcol], COLS[:, vcols:vcols + 2],
                                           AF.Ln), waits=[em.tok("v")])
        em.cop("s", lambda e: e.activation(SC[:, :, rstdcol], SC[:, :, rstdcol],
                                           AF.Exp, scale=-0.5))
        em.cop("v", lambda e: e.tensor_tensor(COLS[:, vcols + 2:vcols + 4],
                                              SC[:, :, rstdcol], SC[:, :, rstdcol],
                                              ALU.mult), waits=[em.tok("s")])
        em.cop("v", lambda e: e.tensor_tensor(COLS[:, vcols + 2:vcols + 4],
                                              COLS[:, vcols + 2:vcols + 4],
                                              COLS[:, vcols:vcols + 2], ALU.mult))
        em.cop("v", lambda e: e.tensor_scalar(COLS[:, vcols + 2:vcols + 4],
                                              COLS[:, vcols + 2:vcols + 4],
                                              -0.5, 1.5, ALU.mult, ALU.add))
        em.cop("v", lambda e: e.tensor_tensor(SC[:, :, rstdcol], SC[:, :, rstdcol],
                                              COLS[:, vcols + 2:vcols + 4], ALU.mult))
        return em.tok("v")

    def rstd_col(mcol, sqcol, rstdcol, epscol, waits):
        for tt in range(2):
            em.cop("v", lambda e, tt=tt: e.tensor_mul(COLS[:, 44:45],
                                                      SC[:, tt, mcol:mcol + 1],
                                                      SC[:, tt, mcol:mcol + 1]), waits=waits)
            em.cop("v", lambda e, tt=tt: e.tensor_sub(SC[:, tt, rstdcol:rstdcol + 1],
                                                      SC[:, tt, sqcol:sqcol + 1],
                                                      COLS[:, 44:45]))
        return rsqrt_cols(rstdcol, 40, 1e-5)

    def rowmax_col(w_ap, ndt, col, waits):
        t1 = em.cop("v", lambda e, w=w_ap: e.reduce_max(
            TMRB[:].rearrange("p a b -> p (a b)"), w.rearrange("p a b -> p b a"),
            axis=AX.X, apply_absolute_value=True),
            waits=list(waits) + [rdt.get("TMRB")])
        tks = []
        for tt in range(2):
            tk = em.cop("t", lambda e, tt=tt: e.transpose(
                PTR[:, tt * 128:(tt + 1) * 128], TMRB[:, tt, :], IDENT[:]),
                waits=[t1, rdt.get("PTR")])
            rdt["TMRB"] = tk
            tks.append(em.cop("v", lambda e, tt=tt, col=col: e.reduce_max(
                SC[:, tt, col:col + 1], PTR[:, tt * 128:(tt + 1) * 128], axis=AX.X),
                waits=[tk]))
        rdt["PTR"] = tks[-1]
        return tks

    def emit_ffn(rstd_sb, x_sb, out_t, ubf, waits):
        t0 = []
        for k in range(8):
            t0.append(em.cop("t", lambda e, k=k, ubf=ubf: e.matmul(
                PO1[0:64, 0:256], WIN[:, k, :], ubf[:, k, :], start=(k == 0),
                stop=(k == 7)), waits=list(waits) + [rdt.get("PO1")]))
        ts_ = em.cop("v", lambda e, r=rstd_sb: e.tensor_tensor(
            FS[:], PO1[0:64, 0:256], r, ALU.mult), waits=t0)
        rdt["PO1"] = ts_
        em.mb()
        trep = []
        for kk in range(8):
            pd = PA if kk < 4 else PB
            off = (kk % 4) * 256
            trep.append(em.cop("t", lambda e, kk=kk, pd=pd, off=off: e.matmul(
                pd[:, off:off + 256], REP[:, kk, :], FS[:], start=True, stop=True),
                waits=[ts_, rdt.get("PA" if kk < 4 else "PB")]))
        em.mb()
        if dbg:
            if not hasattr(em, "_dumped_pa"):
                em._dumped_pa = True
                em.cop("v", lambda e: e.tensor_copy(QSCR[:, 0:4, :].rearrange("p a b -> p (a b)"), PA[:, 0:1024]))
                em.barrier()
                em.dma("d", lambda e: e.dma_start(out=dbg_out["d_pa"], in_=QSCR[:, 0:4, :].rearrange("p a b -> p (a b)")))
                em.dma("d", lambda e: e.dma_start(out=dbg_out["d_rep"],
                                                  in_=REPF[:].rearrange("p a b -> p (a b)")))
                em.barrier()
        trl = []
        for kk in range(8):
            pd = PA if kk < 4 else PB
            off = (kk % 4) * 256
            trl.append(em.cop("s", lambda e, kk=kk, pd=pd, off=off: e.activation(
                FH[:, kk, :], pd[:, off:off + 256], AF.Relu, bias=B1E[:, kk:kk + 1]),
                waits=trep))
        rdt["PA"] = rdt["PB"] = trl[-1]
        em.mb()
        tsum = []
        for kk in range(8):
            tsum.append(em.cop("t", lambda e, kk=kk: e.matmul(
                PO1[0:64, 256:512], SUM2[:, kk, :], FH[:, kk, :],
                start=(kk == 0), stop=(kk == 7)), waits=trl + [rdt.get("PO1")]))
        em.mb()
        tg = em.cop("s", lambda e: e.activation(FG[:], PO1[0:64, 256:512], AF.Identity,
                                                bias=B2[:]), waits=tsum)
        rdt["PO1"] = tg
        em.mb()
        tw = []
        for m in range(8):
            pd = PA if m < 4 else PB
            off = (m % 4) * 256
            tw.append(em.cop("t", lambda e, m=m, pd=pd, off=off: e.matmul(
                pd[:, off:off + 256], WOUT[:, m * 128:(m + 1) * 128], FG[:],
                start=True, stop=False), waits=[tg]))
            tw.append(em.cop("t", lambda e, m=m, pd=pd, off=off: e.matmul(
                pd[:, off:off + 256], BOUTR[:, m, :], ONESR2[:], start=False, stop=True)))
        em.mb()
        outs = []
        for half, pd in [(0, PA), (1, PB)]:
            outs.append(em.cop("v", lambda e, half=half, pd=pd, x_sb=x_sb, on=out_t:
                               e.scalar_tensor_tensor(
                on[:, half * 4:(half + 1) * 4, :].rearrange("p a b -> p (a b)"),
                pd[:, 0:1024], 0.5,
                x_sb[:, half * 4:(half + 1) * 4, :].rearrange("p a b -> p (a b)"),
                ALU.mult, ALU.add), waits=tw))
        rdt["PA"], rdt["PB"] = outs[0], outs[1]
        return outs

    def ln_to_ubf(x_sb, mcol, sqcol, rstdcol, bfb, sqb, ubf, waits=()):
        em.cop("v", lambda e, x=x_sb, bfb=bfb: e.tensor_copy(
            bfb[:].rearrange("p a b -> p (a b)"), x[:].rearrange("p a b -> p (a b)")),
            waits=list(waits) + [rdt.get("ST_MM")])
        em.cop("s", lambda e, bfb=bfb, sqb=sqb: e.activation(
            sqb[:].rearrange("p a b -> p (a b)"), bfb[:].rearrange("p a b -> p (a b)"),
            AF.Square), waits=[em.tok("v"), rdt.get("XQB")])
        em.mb()
        if lncut < 2:
            return
        st_t = stats_cols(bfb, sqb, mcol, sqcol, [em.tok("s")])
        if lncut < 3:
            return
        rs_t = rstd_col(mcol, sqcol, rstdcol, CEPS5, st_t)
        if lncut < 4:
            return
        fl = flush_sc(mcol, mcol + 1, st_t) + flush_sc(rstdcol, rstdcol + 1, [rs_t])
        em.mb()
        if lncut < 5:
            return
        mb_, tkm = bcast(mcol, fl)
        if lncut < 6:
            return
        em.cop("v", lambda e, x=x_sb, mb_=mb_, ubf=ubf: e.tensor_tensor(
            ubf[:].rearrange("p a b -> p (a b)"), x[:].rearrange("p a b -> p (a b)"),
            AP(mb_.tensor, mb_.offset, [mb_.ap[0], [0, 8], mb_.ap[1]]), ALU.subtract),
            waits=[tkm])
        rb, tkr = bcast(rstdcol, fl)
        em.cop("v", lambda e, rb=rb: e.tensor_copy(MBC[:, 0, :], rb), waits=[tkr])
        em.mb()

    NW = {0: NQ, 1: NK, 2: NV, 3: NG, 4: NGK1}
    ICTOT = [0]
    t_xa_free = None
    em.pipe = True
    for c in range(nchunks):
        txa = em.dma("d", lambda e, c=c: e.dma_start(
            out=XA[:],
            in_=xT.rearrange("(k p) t -> p k t", p=128)[:, :, c * T:(c + 1) * T]),
            waits=[t_xa_free] if t_xa_free else [])
        em.mb()
        if p1cut < -2:
            em.barrier()
            continue
        ln_to_ubf(XA, 0, 1, 2, QF, XQB, KF, waits=[txa])
        if p1cut < -1:
            em.barrier()
            continue
        emit_ffn(MBC[0:64, 0, :], XA, XP, KF, [em.tok("v")])
        em.mb()
        if dbg and c == 0:
            em.dma("d", lambda e: e.dma_start(out=dbg_out["d_ub"], in_=KF[:].rearrange("p a b -> p (a b)")))
            em.dma("d", lambda e: e.dma_start(out=dbg_out["d_fs"], in_=FSF[:]))
            em.dma("d", lambda e: e.dma_start(out=dbg_out["d_fh"], in_=FH[:].rearrange("p a b -> p (a b)")))
            em.dma("d", lambda e: e.dma_start(out=dbg_out["d_fg"], in_=FGF[:]))
            em.dma("d", lambda e: e.dma_start(out=dbg_out["d_sc"], in_=SC[:].rearrange("p a b -> p (a b)")))
            em.dma("d", lambda e: e.dma_start(out=dbg_out["d_mbc"], in_=MBC[:].rearrange("p a b -> p (a b)")))
            em.barrier()
        if p1cut < 0:
            em.barrier()
            continue
        em.dma("d", lambda e, c=c: e.dma_start(
            out=xp_st[c], in_=XP[:].rearrange("p a b -> p (a b)")), waits=[em.tok("v")])
        if p1cut < 1:
            em.barrier()
            continue
        em.cop("v", lambda e: e.tensor_copy(QF[:].rearrange("p a b -> p (a b)"),
                                            XP[:].rearrange("p a b -> p (a b)")),
               waits=[rdt.get("ST_MM")])
        em.cop("s", lambda e: e.activation(XQB[:].rearrange("p a b -> p (a b)"),
                                           QF[:].rearrange("p a b -> p (a b)"), AF.Square),
               waits=[em.tok("v"), rdt.get("XQB"), rdt.get("ST_MM")])
        em.mb()
        st_t = stats_cols(QF, XQB, 4, 5, [em.tok("s")])
        rs_t = rstd_col(4, 5, 6, CEPS5, st_t)
        for tt in range(2):
            em.cop("v", lambda e, tt=tt: e.tensor_mul(COLS[:, 44:45], SC[:, tt, 4:5],
                                                      SC[:, tt, 4:5]), waits=[rs_t])
            em.cop("v", lambda e, tt=tt: e.tensor_sub(COLS[:, 45:46], SC[:, tt, 5:6],
                                                      COLS[:, 44:45]))
            em.cop("v", lambda e, tt=tt: e.tensor_scalar(COLS[:, 46:47], COLS[:, 45:46],
                                                         1e-5, None, ALU.add))
            em.cop("v", lambda e, tt=tt: e.reciprocal(COLS[:, 47:48], COLS[:, 46:47]))
            em.cop("v", lambda e, tt=tt: e.tensor_mul(SC[:, tt, 28:29], COLS[:, 47:48],
                                                      COLS[:, 45:46]))
        rsqrt_cols(28, 40, 1e-8)
        for tt in range(2):
            em.cop("v", lambda e, tt=tt: e.tensor_mul(SC[:, tt, 7:8], SC[:, tt, 6:7],
                                                      SC[:, tt, 28:29]))
        fl2 = flush_sc(4, 5, st_t)
        em.mb()
        m2b, tk2 = bcast(4, fl2)
        em.cop("v", lambda e, m2b=m2b: e.tensor_tensor(
            U1[:].rearrange("p a b -> p (a b)"), XP[:].rearrange("p a b -> p (a b)"),
            AP(m2b.tensor, m2b.offset, [m2b.ap[0], [0, 8], m2b.ap[1]]), ALU.subtract),
            waits=[tk2])
        em.mb()

        if p1cut < 2:
            em.barrier()
            continue
        for si in range(5):
            nten = NW[si]
            tw = em.cop("v", lambda e, nten=nten: e.tensor_tensor(
                WSCR[:].rearrange("p a b -> p (a b)"),
                U1[:].rearrange("p a b -> p (a b)"),
                AP(nten, 0, [[8, 128], [1, 8], [0, T]]), ALU.mult))
            rowmax_col(WSCR[:], 8, 8 + si, [tw])
            em.mb()
        for tt in range(2):
            em.cop("v", lambda e, tt=tt: e.scalar_tensor_tensor(
                SC[:, tt, 13:18], SC[:, tt, 8:13], 1.0,
                AP(SC, tt * 32 + 7, [[64, 128], [0, 5]]), ALU.mult, ALU.mult))
            em.cop("v", lambda e, tt=tt: e.tensor_scalar(SC[:, tt, 13:18],
                                                         SC[:, tt, 13:18],
                                                         1e-5, None, ALU.max))
            em.cop("v", lambda e, tt=tt: e.reciprocal(SC[:, tt, 18:23], SC[:, tt, 13:18]))
            em.cop("v", lambda e, tt=tt: e.scalar_tensor_tensor(
                SC[:, tt, 8:13], SC[:, tt, 18:23], 127.0,
                AP(SC, tt * 32 + 7, [[64, 128], [0, 5]]), ALU.mult, ALU.mult))
            em.cop("v", lambda e, tt=tt: e.scalar_tensor_tensor(
                SC[:, tt, 23:26], SC[:, tt, 14:17], 1.0 / 127.0, MW[:, 1:4],
                ALU.mult, ALU.mult))
        fl3 = flush_sc(8, 13, [em.tok("v")]) + flush_sc(23, 26, [em.tok("v")])
        em.mb()

        def quantize(si):
            nten = NW[si]
            phib, tkb = bcast(8 + si, fl3)
            em.cop("v", lambda e, nten=nten: e.tensor_tensor(
                QSCR[:].rearrange("p a b -> p (a b)"),
                U1[:].rearrange("p a b -> p (a b)"),
                AP(nten, 0, [[8, 128], [1, 8], [0, T]]), ALU.mult))
            em.cop("v", lambda e, phib=phib: e.tensor_tensor(
                QSCR[:].rearrange("p a b -> p (a b)"),
                QSCR[:].rearrange("p a b -> p (a b)"),
                AP(phib.tensor, phib.offset, [phib.ap[0], [0, 8], phib.ap[1]]),
                ALU.mult), waits=[tkb])
            for hf in range(2):
                em.cop("v", lambda e, hf=hf: e.tensor_scalar(
                    XI[:].rearrange("p a b -> p (a b)"),
                    QSCR[:, hf * 4:(hf + 1) * 4, :].rearrange("p a b -> p (a b)"),
                    RC, RC, ALU.add, ALU.subtract), waits=[em.tok("s")])
                em.cop("s", lambda e, hf=hf: e.activation(
                    XQB[:, hf * 4:(hf + 1) * 4, :].rearrange("p a b -> p (a b)"),
                    XI[:].rearrange("p a b -> p (a b)"), AF.Copy),
                    waits=[em.tok("v"), rdt.get("XQB"), rdt.get("ST_MM")])
            return em.tok("s")

        if p1cut < 3:
            em.barrier()
            continue
        tq = quantize(0)
        tqm = []
        for m in range(8):
            pd = PA if m < 4 else PB
            off = (m % 4) * 256
            for k in range(8):
                tqm.append(em.cop("t", lambda e, m=m, k=k, pd=pd, off=off: e.matmul(
                    pd[:, off:off + 256], WQ[:, k, m * 128:(m + 1) * 128], XQB[:, k, :],
                    start=(k == 0), stop=(k == 7)),
                    waits=[tq, rdt.get("PA" if m < 4 else "PB")]))
        for half, pd in [(0, PA), (1, PB)]:
            rdt["PA" if half == 0 else "PB"] = em.cop(
                "s", lambda e, half=half, pd=pd: e.activation(
                    QF[:, half * 4:(half + 1) * 4, :].rearrange("p a b -> p (a b)"),
                    pd[:, 0:1024], AF.Copy), waits=tqm)
        rdt["XQB"] = tqm[-1]
        em.mb()
        if p1cut < 4:
            em.barrier()
            continue
        tq = quantize(1)
        tkm = []
        for m in range(8):
            pd = PA if m < 4 else PB
            off = (m % 4) * 256
            for k in range(8):
                tkm.append(em.cop("t", lambda e, m=m, k=k, pd=pd, off=off: e.matmul(
                    pd[:, off:off + 256], WK[:, k, m * 128:(m + 1) * 128], XQB[:, k, :],
                    start=(k == 0), stop=(k == 7)),
                    waits=[tq, rdt.get("PA" if m < 4 else "PB")]))
        for half, pd in [(0, PA), (1, PB)]:
            rdt["PA" if half == 0 else "PB"] = em.cop(
                "s", lambda e, half=half, pd=pd: e.activation(
                    KF[:, half * 4:(half + 1) * 4, :].rearrange("p a b -> p (a b)"),
                    pd[:, 0:1024], AF.Copy), waits=tkm)
        rdt["XQB"] = tkm[-1]
        em.mb()
        if p1cut < 5:
            em.barrier()
            continue
        tq = quantize(2)
        wst_rd = {0: None, 1: None}
        for u in range(8):
            wslot = u % 2
            twl = em.dma("pd", lambda e, u=u, wslot=wslot: e.dma_start(
                out=WST[:, wslot, :, :],
                in_=wvq_st[:, :, u * 256:(u + 1) * 256]),
                waits=[wst_rd[wslot] or em.tok("t")])
            for tt in range(2):
                pd = [PA, PB][tt]
                pdn = "PA" if tt == 0 else "PB"
                tvm = []
                for k in range(8):
                    tvm.append(em.cop("t", lambda e, tt=tt, k=k, pd=pd, wslot=wslot:
                                      e.matmul(pd[:, 0:256],
                                               XQB[:, k, tt * 128:(tt + 1) * 128],
                                               WST[:, wslot, k, :],
                                               start=(k == 0), stop=(k == 7)),
                                      waits=[tq, twl, rdt.get(pdn)]))
                rdt[pdn] = em.cop("s", lambda e, tt=tt, u=u, pd=pd: e.activation(
                    VT[:, tt, u * 256:(u + 1) * 256], pd[:, 0:256], AF.Identity,
                    scale=SC[:, tt, 24:25]), waits=tvm)
                rdt["XQB"] = tvm[-1]
                wst_rd[wslot] = tvm[-1]
            em.mb()
        if p1cut < 6:
            em.barrier()
            continue
        tq = quantize(4)
        ti1 = []
        for k in range(8):
            ti1.append(em.cop("t", lambda e, k=k: e.matmul(
                PO1[0:16, 0:256], WGK1[:, k, :], XQB[:, k, :],
                start=(k == 0), stop=(k == 7)), waits=[tq, rdt.get("PO1")]))
        rdt["PO1"] = em.cop("v", lambda e: e.tensor_copy(I1[:], PO1[0:16, 0:256]),
                            waits=ti1)
        em.cop("v", lambda e: e.tensor_copy(I1B[:], I1[:]))
        rdt["XQB"] = ti1[-1]
        em.mb()
        if p1cut < 7:
            em.barrier()
            continue
        tq = quantize(3)
        psigb, tkg = bcast(25, fl3)
        em.cop("v", lambda e, psigb=psigb: e.tensor_copy(MBC[:, 1, :], psigb),
               waits=[tkg])
        wst_rd = {0: None, 1: None}
        for u in range(8):
            wslot = u % 2
            twl = em.dma("pd", lambda e, u=u, wslot=wslot: e.dma_start(
                out=WST[:, wslot, :, :],
                in_=wgq_st[:, :, u * 256:(u + 1) * 256]),
                waits=[wst_rd[wslot] or em.tok("t")])
            for mm in range(2):
                m = u * 2 + mm
                pd = [PA, PB][mm]
                pdn = "PA" if mm == 0 else "PB"
                tgm = []
                for k in range(8):
                    tgm.append(em.cop("t", lambda e, mm=mm, k=k, pd=pd, wslot=wslot:
                                      e.matmul(pd[:, 0:256],
                                               WST[:, wslot, k, mm * 128:(mm + 1) * 128],
                                               XQB[:, k, :],
                                               start=(k == 0), stop=(k == 7)),
                                      waits=[tq, twl, rdt.get(pdn)]))
                rdt["XQB"] = tgm[-1]
                wst_rd[wslot] = tgm[-1]
                tgm.append(em.cop("v", lambda e, m=m, pd=pd: e.tensor_tensor(
                    GFB[:, m, :], pd[:, 0:256], MBC[:, 1, :], ALU.mult), waits=tgm))
                rdt[pdn] = tgm[-1]
            em.mb()
        if p1cut < 8:
            em.barrier()
            continue
        em.cop("s", lambda e: e.activation(SQ16[:], I1B[:], AF.Square),
               waits=[em.tok("v")])
        t0 = [em.cop("t", lambda e: e.matmul(PA[0:1, 0:256], ONESC[0:16, :], SQ16[:],
                                             start=True, stop=True),
                     waits=[em.tok("s"), rdt.get("PA")])]
        tr = em.cop("v", lambda e: e.tensor_scalar(SROW[0:1, 0:256], PA[0:1, 0:256],
                                                   1.0 / 16.0, None, ALU.mult), waits=t0)
        rdt["PA"] = tr
        if g2cut < 2:
            em.barrier()
            continue
        for tt in range(2):
            tk = em.cop("t", lambda e, tt=tt: e.transpose(
                PO2[0:128, 448 + tt:449 + tt], SROW[0:1, tt * 128:(tt + 1) * 128],
                IDENTF[0:1, 0:1]), waits=[tr, rdt.get("PO2")])
            rdt["SROW"] = tk
            rdt["PO2"] = em.cop("v", lambda e, tt=tt: e.tensor_copy(
                SC[:, tt, 26:27], PO2[0:128, 448 + tt:449 + tt]), waits=[tk])
        em.cop("s", lambda e: e.activation(SC[:, :, 26], SC[:, :, 26], AF.Ln,
                                           bias=CEPS8[:]), waits=[em.tok("v")])
        em.cop("s", lambda e: e.activation(SC[:, :, 26], SC[:, :, 26], AF.Exp, scale=-0.5))
        if g2cut < 3:
            em.barrier()
            continue
        em.cop("v", lambda e: e.tensor_scalar(I1[:], I1[:], NGK2[:], None, ALU.mult),
               waits=[em.tok("s")])
        em.cop("v", lambda e: e.tensor_copy(AP(TMRB, 0, [[256, 16], [1, 256]]), I1[:]),
               waits=[rdt.get("TMRB")])
        for tt in range(2):
            tk = em.cop("t", lambda e, tt=tt: e.transpose(
                PTR[:, tt * 128:tt * 128 + 16],
                AP(TMRB, tt * 128, [[256, 16], [1, 128]]), IDENT[0:16, 0:16]),
                waits=[em.tok("v"), rdt.get("PTR")])
            rdt["TMRB"] = tk
            rdt["PTR"] = em.cop("v", lambda e, tt=tt: e.reduce_max(
                SC[:, tt, 27:28], PTR[:, tt * 128:tt * 128 + 16], axis=AX.X,
                apply_absolute_value=True), waits=[tk])
        if g2cut < 4:
            em.barrier()
            continue
        for tt in range(2):
            em.cop("v", lambda e, tt=tt: e.tensor_mul(SC[:, tt, 27:28], SC[:, tt, 27:28],
                                                      SC[:, tt, 26:27]))
            em.cop("v", lambda e, tt=tt: e.tensor_scalar(SC[:, tt, 27:28],
                                                         SC[:, tt, 27:28],
                                                         1e-5, None, ALU.max))
            em.cop("v", lambda e, tt=tt: e.reciprocal(COLS[:, 44:45], SC[:, tt, 27:28]))
            em.cop("v", lambda e, tt=tt: e.scalar_tensor_tensor(
                SC[:, tt, 19:20], COLS[:, 44:45], 127.0, SC[:, tt, 26:27],
                ALU.mult, ALU.mult))
            em.cop("v", lambda e, tt=tt: e.scalar_tensor_tensor(
                SC[:, tt, 20:21], SC[:, tt, 27:28], 1.0 / 127.0, MW[:, 7:8],
                ALU.mult, ALU.mult))
        fl4 = flush_sc(19, 21, [em.tok("v")])
        em.mb()
        if g2cut < 5:
            em.barrier()
            continue
        phi2b, tka = bcast(19, fl4)
        em.cop("v", lambda e, phi2b=phi2b: e.tensor_tensor(I1[:], I1[:], phi2b[0:16, :],
                                                           ALU.mult), waits=[tka])
        em.cop("v", lambda e: e.tensor_scalar(XI[0:16, 0, 0:256], I1[:], RC, RC,
                                              ALU.add, ALU.subtract),
               waits=[em.tok("s")])
        em.cop("s", lambda e: e.activation(I1B[:], XI[0:16, 0, 0:256], AF.Copy),
               waits=[em.tok("v")])
        if g2cut < 6:
            em.barrier()
            continue
        psi2b, tkb2 = bcast(20, fl4)
        em.cop("v", lambda e, psi2b=psi2b: e.tensor_copy(MBC[:, 1, :], psi2b),
               waits=[tkb2])
        tz = []
        for m in range(8):
            pd = PA if m < 4 else PB
            off = (m % 4) * 256
            if g2cut < 7:
                continue
            tz.append(em.cop("t", lambda e, m=m, pd=pd, off=off: e.matmul(
                pd[:, off:off + 256], WGK2[:, m * 128:(m + 1) * 128], I1B[:],
                start=True, stop=True),
                waits=[em.tok("s"), rdt.get("PA" if m < 4 else "PB")]))
        for m in range(8):
            pd = PA if m < 4 else PB
            off = (m % 4) * 256
            if g2cut < 8:
                continue
            tz.append(em.cop("v", lambda e, m=m, pd=pd, off=off: e.tensor_tensor(
                U1[:, m, :], pd[:, off:off + 256], MBC[:, 1, :], ALU.mult), waits=tz))
            rdt["PA" if m < 4 else "PB"] = tz[-1]
            if g2cut < 9:
                continue
            tz.append(em.cop("p", lambda e, m=m: e.tensor_scalar(
                U1[:, m, :], U1[:, m, :], BGK2[:, m:m + 1], None, ALU.add), waits=tz))
        em.mb()
        if p1cut < 9:
            em.barrier()
            continue
        em.cop("s", lambda e: e.activation(WSCR[:].rearrange("p a b -> p (a b)"),
                                           U1[:].rearrange("p a b -> p (a b)"),
                                           AF.Abs), waits=[em.tok("p"), em.tok("v")])
        em.cop("s", lambda e: e.activation(WSCR[:].rearrange("p a b -> p (a b)"),
                                           WSCR[:].rearrange("p a b -> p (a b)"),
                                           AF.Exp, scale=-1.0))
        em.cop("s", lambda e: e.activation(WSCR[:].rearrange("p a b -> p (a b)"),
                                           WSCR[:].rearrange("p a b -> p (a b)"),
                                           AF.Ln, bias=1.0))
        em.cop("v", lambda e: e.tensor_scalar(QSCR[:].rearrange("p a b -> p (a b)"),
                                              U1[:].rearrange("p a b -> p (a b)"),
                                              0.0, 1.0 / 16.0, ALU.min, ALU.mult),
               waits=[em.tok("p")])
        em.cop("v", lambda e: e.scalar_tensor_tensor(
            GL[:].rearrange("p a b -> p (a b)"),
            WSCR[:].rearrange("p a b -> p (a b)"), -1.0 / 16.0,
            QSCR[:].rearrange("p a b -> p (a b)"), ALU.mult, ALU.add),
            waits=[em.tok("s")])
        t_xa_free = em.tok("v")
        em.mb()
        if p1cut < 10:
            em.barrier()
            continue
        em.cop("s", lambda e: e.activation(THB[:].rearrange("p a b -> p (a b)"),
                                           GFB[:].rearrange("p a b -> p (a b)"),
                                           AF.Tanh, scale=0.5), waits=[em.tok("v")])
        em.cop("p", lambda e: e.tensor_scalar(THB[:].rearrange("p a b -> p (a b)"),
                                              THB[:].rearrange("p a b -> p (a b)"),
                                              0.5, 0.5, ALU.mult, ALU.add),
               waits=[em.tok("s")])
        em.cop("v", lambda e: e.tensor_tensor(SGB[:].rearrange("p a b -> p (a b)"),
                                              THB[:].rearrange("p a b -> p (a b)"),
                                              GFB[:].rearrange("p a b -> p (a b)"),
                                              ALU.mult), waits=[em.tok("p")])
        em.mb()

        if p1cut < 11:
            em.barrier()
            continue
        for ic in range(icpc):
            gic = ICTOT[0]
            ICTOT[0] += 1
            tsl = slice(ic * C, (ic + 1) * C)
            tsc = []
            for k in range(8):
                tsc.append(em.cop("v", lambda e, k=k, tsl=tsl: e.tensor_tensor_scan(
                    BBs(k), GL[:, k, tsl], ZEROT[:], 0.0, ALU.add, ALU.bypass)))
            em.cop("v", lambda e: e.tensor_copy(
                COLS[:, 0:8], AP(QSCR, C - 1, [[8 * T, 128], [T, 8]])),
                waits=tsc + [rdt.get("COLS")])
            em.cop("v", lambda e: e.tensor_copy(
                COLS[:, 8:16], AP(QSCR, C // 2 - 1, [[8 * T, 128], [T, 8]])))
            em.cop("v", lambda e: e.tensor_scalar(COLS[:, 16:24], COLS[:, 8:16],
                                                  -1.0, None, ALU.mult))
            em.cop("v", lambda e, gic=gic: e.tensor_copy(
                AP(LTAB, gic % 32, [[8 * 32, 128], [32, 8]]), COLS[:, 24:32]))
            em.cop("v", lambda e: e.tensor_add(COLS[:, 24:32], COLS[:, 24:32],
                                               COLS[:, 0:8]))
            em.cop("v", lambda e: e.tensor_sub(COLS[:, 32:40], COLS[:, 0:8],
                                               COLS[:, 8:16]))
            em.cop("s", lambda e: e.activation(ELT[:], COLS[:, 0:8], AF.Exp),
                   waits=[em.tok("v"), rdt.get("ELT")])
            em.cop("s", lambda e: e.activation(COLS[:, 8:16], COLS[:, 8:16], AF.Exp))
            em.cop("s", lambda e: e.activation(COLS[:, 32:40], COLS[:, 32:40], AF.Exp))
            rdt["COLS"] = em.tok("s")
            em.cop("v", lambda e: e.tensor_tensor(
                BS3, BB3, AP(COLS, 16, [[48, 128], [1, 8], [0, C]]), ALU.add))
            em.cop("s", lambda e: e.activation(E1[:], BS3, AF.Exp),
                   waits=[em.tok("v")])
            em.cop("s", lambda e: e.activation(E2[:], BS3, AF.Exp, scale=-1.0))
            em.mb()
            em.cop("v", lambda e, tsl=tsl: e.tensor_tensor(
                QS[:], QF[:, :, tsl], E1[:], ALU.mult),
                waits=[em.tok("s"), rdt.get("GLA_T"), rdt.get("QI")])
            em.cop("v", lambda e, tsl=tsl: e.tensor_tensor(
                KS[:], KF[:, :, tsl], E2[:], ALU.mult))
            em.cop("v", lambda e: e.tensor_tensor(
                QI[:].rearrange("p a b -> p (a b)"),
                QS[:].rearrange("p a b -> p (a b)"),
                AP(COLS, 8, [[48, 128], [1, 8], [0, C]]), ALU.mult))
            em.cop("v", lambda e: e.tensor_tensor(
                KX[:].rearrange("p a b -> p (a b)"),
                KS[:].rearrange("p a b -> p (a b)"),
                AP(COLS, 32, [[48, 128], [1, 8], [0, C]]), ALU.mult))
            tqs = em.tok("v")
            em.mb()
            rdt["QI"] = em.dma("d", lambda e, gic=gic: e.dma_start(
                out=qt_st[gic], in_=QI[:].rearrange("p a b -> p (a b)")),
                waits=[tqs])
            for h in range(H):
                po = [PO1, PO2][h % 2]
                pon = "PO1" if h % 2 == 0 else "PO2"
                ts16 = em.cop("p", lambda e, h=h: e.tensor_copy(
                    S16[:].rearrange("p a b -> p (a b)"),
                    S32[:, 2 * h:2 * h + 2, :].rearrange("p a b -> p (a b)")),
                    waits=[rdt.get("S16"), rdt.get(("S32", h))])
                ta = []
                for kk in range(2):
                    ta.append(em.cop("t", lambda e, h=h, kk=kk: e.matmul(
                        PA[:, 0:128], KS[:, 2 * h + kk, :], QS[:, 2 * h + kk, :],
                        start=(kk == 0), stop=(kk == 1)),
                        waits=[rdt.get("PA"), tqs]))
                rdt["PA"] = em.cop("v", lambda e: e.tensor_tensor(
                    ABF[:], PA[:, 0:128], TRIU[:], ALU.mult), waits=ta)
                em.cop("v", lambda e, ic=ic: e.tensor_scalar(
                    AB[:], ABF[:], SC[:, ic, 23:24], None, ALU.mult))
                tab = em.tok("v")
                to = []
                for j in range(4):
                    to.append(em.cop("t", lambda e, h=h, j=j, ic=ic, po=po: e.matmul(
                        po[:, j * 128:(j + 1) * 128],
                        VT[:, ic, h * 512 + j * 128:h * 512 + (j + 1) * 128],
                        AB[:], start=True, stop=False),
                        waits=[tab, ts16, rdt.get(pon)]))
                    for kk in range(2):
                        to.append(em.cop("t", lambda e, h=h, j=j, kk=kk, po=po: e.matmul(
                            po[:, j * 128:(j + 1) * 128],
                            S16[:, kk, j * 128:(j + 1) * 128],
                            QI[:, 2 * h + kk, :], start=False, stop=(kk == 1))))
                rdt["S16"] = to[-1]
                rdt[pon] = em.cop("v", lambda e, h=h, po=po, tsl=tsl: e.tensor_tensor(
                    OGB[:, 4 * h:4 * h + 4, tsl],
                    po[:, 0:512].rearrange("p (a b) -> p a b", a=4),
                    SGB[:, 4 * h:4 * h + 4, tsl], ALU.mult), waits=to)
                tkt = []
                for kk in range(2):
                    tkt.append(em.cop("t", lambda e, h=h, kk=kk: e.transpose(
                        PTR[:, kk * 128:(kk + 1) * 128], KX[:, 2 * h + kk, :],
                        IDENT[:]), waits=[rdt.get("PTR")]))
                for kk in range(2):
                    tkt.append(em.cop("v", lambda e, kk=kk, ic=ic: e.tensor_scalar(
                        KXT[:, kk, :], PTR[:, kk * 128:(kk + 1) * 128],
                        SC[:, ic, 23:24], None, ALU.mult), waits=tkt))
                rdt["PTR"] = tkt[-1]
                tu = []
                for kk in range(2):
                    tu.append(em.cop("t", lambda e, h=h, kk=kk, ic=ic: e.matmul(
                        PB[:, kk * 512:(kk + 1) * 512], KXT[:, kk, :],
                        VT[:, ic, h * 512:(h + 1) * 512], start=True, stop=True),
                        waits=tkt + [rdt.get("PB")]))
                rdt["GLA_T"] = tu[-1]
                for kk in range(2):
                    rdt["PB"] = rdt[("S32", h)] = rdt["ELT"] = em.cop(
                        "v", lambda e, h=h, kk=kk: e.scalar_tensor_tensor(
                            S32[:, 2 * h + kk, :], S32[:, 2 * h + kk, :],
                            ELT[:, 2 * h + kk:2 * h + kk + 1],
                            PB[:, kk * 512:(kk + 1) * 512], ALU.mult, ALU.add),
                        waits=tu + [ts16])
                em.mb()
        em.dma("d", lambda e, c=c: e.dma_start(
            out=og_st[c], in_=OGB[:].rearrange("p a b -> p (a b)")),
            waits=[em.tok("v")])
        em.dma("d", lambda e, c=c: e.dma_start(
            out=sg_st[c], in_=SGB[:].rearrange("p a b -> p (a b)")))
        if dbg and c == 0:
            for nm, src in [("d_xp", XP), ("d_q", QF), ("d_k", KF), ("d_gl", GL),
                            ("d_og", OGB), ("d_v", VT)]:
                em.dma("d", lambda e, nm=nm, src=src: e.dma_start(
                    out=dbg_out[nm], in_=src[:].rearrange("p a b -> p (a b)")))
        em.barrier()
    em.pipe = False

    if stage == 1:
        em.flush()
        ph1.close()
        return nc

    em.dma("d", lambda e: e.dma_start(
        out=AP(cc_in, 0, [[512, 128], [512 * 128, 8], [1, 512]]), in_=S32[:]))
    em.barrier()
    em.op("p", lambda e: e.collective_compute(
        "AllGather", ALU.bypass, replica_groups=[[0, 1], [2, 3], [4, 5], [6, 7]],
        ins=[cc_in.ap().opt()], outs=[cc_out.ap().opt()]), inc=1)
    em.barrier()
    em.dma("d", lambda e: e.dma_start(
        out=S32[:], in_=AP(cc_out, 0, [[512, 128], [512 * 128, 8], [1, 512]])))
    em.barrier()
    em.cop("v", lambda e: e.tensor_scalar(S32[:].rearrange("p a b -> p (a b)"),
                                          S32[:].rearrange("p a b -> p (a b)"),
                                          SMASK[:], None, ALU.mult))
    if dbg:
        em.dma("d", lambda e: e.dma_start(out=dbg_out["d_s32"],
                                          in_=S32[:].rearrange("p a b -> p (a b)")),
               waits=[em.tok("v")])
    em.barrier()
    em.flush()
    ph1.close()
    if stage == 2:
        return nc

    ph3 = contextlib.ExitStack()
    sb3 = lambda n, s, d=F32: ph3.enter_context(nc.sbuf_tensor(n, s, d))
    WO = sb3("WO", [128, 16, D], BF16)
    OF = sb3("OF", [128, 16, T])
    TMQ = sb3("TMQ", [128, 16, T])
    OQ = sb3("OQ", [128, 16, T], BF16)
    XI3 = sb3("XI3", [128, 8, T], I32)
    T16 = sb3("T16", [128, 2, 512], BF16)
    QTL = sb3("QTL", [128, 8, C], BF16)
    OGL = sb3("OGL", [128, 16, T], BF16)
    SGL = sb3("SGL", [128, 16, T], BF16)
    XPL = sb3("XPL", [128, 8, T])
    XP2 = sb3("XP2", [128, 8, T])
    OUTB = sb3("OUTB", [128, 8, T])
    TP3 = sb3("TP3", [128, 8, T])
    UB3 = sb3("UB3", [128, 8, T], BF16)

    em.dma("d", lambda e: e.dma_start(out=WO[:].rearrange("p a b -> p (a b)"),
                                      in_=woq_st.rearrange("p a b -> p (a b)")))
    em.barrier()

    em.pipe = True
    for c in range(nchunks):
        tl = []
        tl.append(em.dma("d", lambda e, c=c: e.dma_start(
            out=OGL[:].rearrange("p a b -> p (a b)"), in_=og_st[c])))
        tl.append(em.dma("d", lambda e, c=c: e.dma_start(
            out=SGL[:].rearrange("p a b -> p (a b)"), in_=sg_st[c])))
        tl.append(em.dma("d", lambda e, c=c: e.dma_start(
            out=XPL[:].rearrange("p a b -> p (a b)"), in_=xp_st[c])))
        em.mb()
        for ic in range(icpc):
            gic = c * icpc + ic
            tsl = slice(ic * C, (ic + 1) * C)
            tq3 = em.dma("pd", lambda e, gic=gic: e.dma_start(
                out=QTL[:].rearrange("p a b -> p (a b)"), in_=qt_st[gic]),
                waits=[rdt.get("QTL")])
            tse = em.cop("s", lambda e, gic=gic: e.activation(
                ELT[:], AP(LTAB, gic % 32, [[8 * 32, 128], [32, 8]]), AF.Exp),
                waits=[rdt.get("ELT")])
            for h in range(H):
                po = [PO1, PO2][h % 2]
                pon = "PO1" if h % 2 == 0 else "PO2"
                for kk in range(2):
                    em.cop("v", lambda e, h=h, kk=kk: e.tensor_scalar(
                        T16[:, kk, :], S32[:, 2 * h + kk, :],
                        ELT[:, 2 * h + kk:2 * h + kk + 1], None, ALU.mult),
                        waits=[tse, rdt.get("T16")])
                tt16 = em.tok("v")
                rdt["ELT"] = tt16
                tc = []
                for j in range(4):
                    for kk in range(2):
                        tc.append(em.cop("t", lambda e, h=h, j=j, kk=kk, po=po:
                                         e.matmul(
                            po[:, j * 128:(j + 1) * 128],
                            T16[:, kk, j * 128:(j + 1) * 128],
                            QTL[:, 2 * h + kk, :], start=(kk == 0), stop=(kk == 1)),
                            waits=[tt16, tq3, rdt.get(pon)]))
                rdt["T16"] = rdt["QTL"] = tc[-1]
                rdt[pon] = em.cop("v", lambda e, h=h, po=po, tsl=tsl: e.tensor_tensor(
                    OF[:, 4 * h:4 * h + 4, tsl],
                    po[:, 0:512].rearrange("p (a b) -> p a b", a=4),
                    SGL[:, 4 * h:4 * h + 4, tsl], ALU.mult), waits=tc + tl)
                em.mb()
        em.cop("v", lambda e: e.tensor_tensor(OF[:].rearrange("p a b -> p (a b)"),
                                              OF[:].rearrange("p a b -> p (a b)"),
                                              OGL[:].rearrange("p a b -> p (a b)"),
                                              ALU.add))
        em.mb()
        em.cop("s", lambda e: e.activation(OGL[:].rearrange("p a b -> p (a b)"),
                                           OF[:].rearrange("p a b -> p (a b)"),
                                           AF.Square), waits=[em.tok("v")])
        t0 = []
        for k in range(16):
            t0.append(em.cop("t", lambda e, k=k: e.matmul(
                PA[0:1, 0:256], ONESC[:], OGL[:, k, :], start=(k == 0),
                stop=(k == 15)), waits=[em.tok("s"), rdt.get("PA"), rdt.get("SROW")]))
        tr = em.cop("v", lambda e: e.tensor_scalar(SROW[0:1, 0:256], PA[0:1, 0:256],
                                                   1.0 / DV, None, ALU.mult), waits=t0)
        rdt["PA"] = tr
        for tt in range(2):
            tk = em.cop("t", lambda e, tt=tt: e.transpose(
                PO2[0:128, 448 + tt:449 + tt], SROW[0:1, tt * 128:(tt + 1) * 128],
                IDENTF[0:1, 0:1]), waits=[tr, rdt.get("PO2")])
            rdt["SROW"] = tk
            rdt["PO2"] = em.cop("v", lambda e, tt=tt: e.tensor_copy(
                SC[:, tt, 0:1], PO2[0:128, 448 + tt:449 + tt]), waits=[tk])
        rsqrt_cols(0, 40, 1e-8)
        tw3 = em.cop("v", lambda e: e.tensor_tensor(
            TMQ[:].rearrange("p a b -> p (a b)"), OF[:].rearrange("p a b -> p (a b)"),
            AP(NO, 0, [[16, 128], [1, 16], [0, T]]), ALU.mult), waits=[em.tok("s")])
        rowmax_col(TMQ[:], 16, 1, [tw3])
        for tt in range(2):
            em.cop("v", lambda e, tt=tt: e.tensor_mul(SC[:, tt, 1:2], SC[:, tt, 1:2],
                                                      SC[:, tt, 0:1]))
            em.cop("v", lambda e, tt=tt: e.tensor_scalar(SC[:, tt, 1:2], SC[:, tt, 1:2],
                                                         1e-5, None, ALU.max))
            em.cop("v", lambda e, tt=tt: e.reciprocal(COLS[:, 44:45], SC[:, tt, 1:2]))
            em.cop("v", lambda e, tt=tt: e.scalar_tensor_tensor(
                SC[:, tt, 3:4], COLS[:, 44:45], 127.0, SC[:, tt, 0:1],
                ALU.mult, ALU.mult))
            em.cop("v", lambda e, tt=tt: e.scalar_tensor_tensor(
                SC[:, tt, 4:5], SC[:, tt, 1:2], 1.0 / 127.0, MW[:, 6:7],
                ALU.mult, ALU.mult))
        em.cop("v", lambda e: e.tensor_copy(AP(TMRB, 0, [[256, 128], [1, 2]]),
                                            SC[:, :, 4]), waits=[rdt.get("TMRB")])
        em.cop("v", lambda e: e.tensor_copy(SC[:, :, 5], AP(TMRB, 0, [[256, 128], [1, 2]])))
        em.cop("v", lambda e: e.tensor_sub(SC[:, :, 6], SC[:, :, 4], SC[:, :, 5]))
        fl5 = flush_sc(3, 7, [em.tok("v")])
        em.mb()
        phiob, tko = bcast(3, fl5)
        em.cop("v", lambda e, phiob=phiob: e.tensor_tensor(
            TMQ[:].rearrange("p a b -> p (a b)"), TMQ[:].rearrange("p a b -> p (a b)"),
            AP(phiob.tensor, phiob.offset, [phiob.ap[0], [0, 16], phiob.ap[1]]),
            ALU.mult), waits=[tko])
        for half in range(2):
            em.cop("v", lambda e, half=half: e.tensor_scalar(
                XI3[:].rearrange("p a b -> p (a b)"),
                TMQ[:, half * 8:(half + 1) * 8, :].rearrange("p a b -> p (a b)"),
                RC, RC, ALU.add, ALU.subtract), waits=[em.tok("s")])
            em.cop("s", lambda e, half=half: e.activation(
                OQ[:, half * 8:(half + 1) * 8, :].rearrange("p a b -> p (a b)"),
                XI3[:].rearrange("p a b -> p (a b)"), AF.Copy), waits=[em.tok("v")])
            em.mb()
        s_ = bcslot[0]
        bcslot[0] = 1 - s_
        tp1 = em.cop("t", lambda e, s_=s_: e.matmul(
            PBC[:, s_ * 256:(s_ + 1) * 256], SELT[:, 5, :],
            SCR[:].rearrange("a b c -> a (b c)"), start=True, stop=False),
            waits=fl5 + [em.tok("v")])
        tp2 = em.cop("t", lambda e, s_=s_: e.matmul(
            PBC[:, s_ * 256:(s_ + 1) * 256], SELT[:, 6, :],
            SCR[:].rearrange("a b c -> a (b c)"), start=False, stop=True))
        em.cop("v", lambda e, s_=s_: e.tensor_copy(
            MBC[:, 0, :], PBC[:, s_ * 256:(s_ + 1) * 256]), waits=[tp1, tp2])
        two = []
        for m in range(8):
            pd = PA if m < 4 else PB
            off = (m % 4) * 256
            for k in range(16):
                two.append(em.cop("t", lambda e, m=m, k=k, pd=pd, off=off: e.matmul(
                    pd[:, off:off + 256], WO[:, k, m * 128:(m + 1) * 128], OQ[:, k, :],
                    start=(k == 0), stop=(k == 15)),
                    waits=[em.tok("s"), rdt.get("PA" if m < 4 else "PB")]))
        for half, pd in [(0, PA), (1, PB)]:
            rdt["PA" if half == 0 else "PB"] = em.cop(
                "v", lambda e, half=half, pd=pd: e.tensor_tensor(
                    TP3[:, half * 4:(half + 1) * 4, :].rearrange("p a b -> p (a b)"),
                    pd[:, 0:1024],
                    AP(MBC, 0, [[2 * T, 128], [0, 4], [1, T]]), ALU.mult), waits=two)
        em.cop("v", lambda e: e.tensor_add(XP2[:].rearrange("p a b -> p (a b)"),
                                           TP3[:].rearrange("p a b -> p (a b)"),
                                           XPL[:].rearrange("p a b -> p (a b)")))
        em.mb()
        rdt["ST_MM"] = rdt["XQB"] = two[-1]
        ln_to_ubf(XP2, 8, 9, 10, OQ[:, 0:8, :], OQ[:, 8:16, :], UB3)
        emit_ffn(MBC[0:64, 0, :], XP2, OUTB, UB3, [em.tok("v")])
        if dbg and c == 0:
            em.dma("d", lambda e: e.dma_start(out=dbg_out["d_of"],
                                              in_=OF[:].rearrange("p a b -> p (a b)")),
                   waits=[em.tok("v")])
        em.mb()
        em.dma("d", lambda e, c=c: e.dma_start(
            out=yT.rearrange("(k p) t -> p k t", p=128)[:, :, c * T:(c + 1) * T],
            in_=OUTB[:]), waits=[em.tok("v")])
        em.barrier()

    em.flush()
    ph3.close()
    return nc


_CACHE = {}


def _host_inputs(inputs):
    f32 = np.float32
    wq = inputs["L0_attn_wq"]; wk = inputs["L0_attn_wk"]
    wv = inputs["L0_attn_wv"]; wg = inputs["L0_attn_wg"]
    wgk1 = inputs["L0_attn_wgk1"]; wgk2 = inputs["L0_attn_wgk2"]
    wo = inputs["L0_attn_wo"]
    W1 = np.asarray(inputs["L0_ffn_W1"], f32); b1 = np.asarray(inputs["L0_ffn_b1"], f32)
    W2 = np.asarray(inputs["L0_ffn_W2"], f32)
    bin_ = np.asarray(inputs["L0_ffn_bin_"], f32)
    com = {}
    com["wqT"] = np.ascontiguousarray(np.asarray(wq, f32).T)
    com["wkT"] = np.ascontiguousarray(np.asarray(wk, f32).T)
    com["wvT"] = np.ascontiguousarray(np.asarray(wv, f32).T)
    com["wgT"] = np.ascontiguousarray(np.asarray(wg, f32).T)
    com["wgk1T"] = np.ascontiguousarray(np.asarray(wgk1, f32).T)
    com["wgk2T"] = np.ascontiguousarray(np.asarray(wgk2, f32).T)
    com["woT"] = np.ascontiguousarray(np.asarray(wo, f32).T)
    arr = lambda v, nt: np.ascontiguousarray(np.asarray(v, f32).reshape(nt, 128).T)
    com["nq_a"] = arr(inputs["L0_attn_nq"], 8)
    com["nk_a"] = arr(inputs["L0_attn_nk"], 8)
    com["nv_a"] = arr(inputs["L0_attn_nv"], 8)
    com["ng_a"] = arr(inputs["L0_attn_ng"], 8)
    com["ngk1_a"] = arr(inputs["L0_attn_ngk1"], 8)
    com["ngk2_a"] = np.ascontiguousarray(np.asarray(inputs["L0_attn_ngk2"], f32).reshape(16, 1))
    com["bgk2_a"] = arr(inputs["L0_attn_bgk2"], 8)
    com["no_a"] = arr(inputs["L0_attn_no"], 16)
    com["Win"] = np.ascontiguousarray(np.asarray(inputs["L0_ffn_Win"], f32))
    com["WoutT"] = np.ascontiguousarray(np.asarray(inputs["L0_ffn_Wout"], f32))
    repw1 = np.zeros((64, 8, 128), f32)
    sumw2 = np.zeros((128, 8, 64), f32)
    w1a = np.zeros((128, 8), f32); b1a = np.zeros((128, 8), f32)
    bina = np.zeros((128, 8), f32)
    for k in range(8):
        for mloc in range(8):
            m = 8 * k + mloc
            for i in range(16):
                p = mloc * 16 + i
                repw1[m, k, p] = W1[m, i]
                sumw2[p, k, m] = W2[m, i]
                w1a[p, k] = W1[m, i]
                b1a[p, k] = b1[m, i]
                bina[p, k] = bin_[m]
    com["repw1"] = repw1; com["sumw2"] = sumw2
    com["w1_a"] = w1a; com["b1_a"] = b1a; com["binmi_a"] = bina
    com["b2_a"] = np.ascontiguousarray(np.asarray(inputs["L0_ffn_b2"], f32).reshape(64, 1))
    com["bout_r"] = np.ascontiguousarray(
        np.asarray(inputs["L0_ffn_bout"], f32).reshape(1, 8, 128).astype(NB))
    com["triu"] = np.ascontiguousarray(np.triu(np.ones((128, 128), np.float32)))
    com["ident"] = np.ascontiguousarray(np.eye(128, dtype=f32).astype(NB))
    com["identf"] = np.ascontiguousarray(np.eye(128, dtype=f32))
    com["ones_c"] = np.ones((128, 1), f32).astype(NB)
    com["ones_r2"] = np.ones((1, 256), f32).astype(NB)
    com["ones_cf"] = np.ones((128, 1), f32)
    com["ones_rf"] = np.ones((1, 128), f32)
    sel = np.zeros((32, 32, 128), f32)
    for j in range(32):
        sel[j, j, :] = 1.0
    com["sel"] = np.ascontiguousarray(sel.astype(NB))
    return com


def kernel(**inputs):
    x = np.asarray(inputs["L0_attn_hidden_states"], np.float32)
    B, L, Dm = x.shape
    half = L // 2
    com = _host_inputs(inputs)
    if "nc" not in _CACHE:
        _CACHE["nc"] = build_nc(16)
    nc = _CACHE["nc"]
    in_maps = []
    for core in range(8):
        b, s = core // 2, core % 2
        m = dict(com)
        m["xT"] = np.ascontiguousarray(x[b, s * half:(s + 1) * half, :].T)
        m["smask"] = np.full((128, 1), float(s), np.float32)
        in_maps.append(m)
    res = run_bass_kernel_spmd(nc, in_maps, core_ids=list(range(8)))
    out = np.empty_like(x)
    for core in range(8):
        b, s = core // 2, core % 2
        out[b, s * half:(s + 1) * half, :] = np.asarray(res.results[core]["yT"]).T
    return out
